# version 14 (final)
# baseline (speedup 1.0000x reference)
import numpy as np
import ml_dtypes
import concourse.bass as bass
import concourse.mybir as mybir
from concourse.bass_utils import run_bass_kernel_spmd
from concourse.alu_op_type import AluOpType as ALU

F32 = mybir.dt.float32
BF16 = mybir.dt.bfloat16
AF = mybir.ActivationFunctionType
AX = mybir.AxisListType
BF = ml_dtypes.bfloat16

ENGS = ("sync", "tensor", "vector", "scalar", "gpsimd")
DEBUG_DUMP_M = False
NCORES = 8
S = 2048
D = 1024
NCH = 8
TG = 512
NTG = S // TG
EPS_LN = 1e-5
EPS_RMS = 1e-6


class _Sem:
    __slots__ = ("h", "n")

    def __init__(self, h):
        self.h = h
        self.n = 0


class Rv:
    __slots__ = ("buf", "p0", "np_", "base", "ncols")

    def __init__(self, buf, base, ncols, p0=0, np_=128):
        self.buf, self.base, self.ncols, self.p0, self.np_ = buf, base, ncols, p0, np_

    def __getitem__(self, idx):
        if not isinstance(idx, tuple):
            idx = (idx, slice(None))
        p, q = idx
        ps = slice(self.p0 + (p.start or 0),
                   self.p0 + (p.stop if p.stop is not None else self.np_))
        qs = slice(self.base + (q.start or 0),
                   self.base + (q.stop if q.stop is not None else self.ncols))
        return self.buf[ps, qs]

    @property
    def name(self):
        return f"{self.buf.name}@{self.p0}+{self.base}"


def _keys(region):
    if isinstance(region, str):
        return [region]
    buf, lo, hi = region
    if isinstance(buf, Rv):
        name = f"{buf.buf.name}@{buf.p0}"
        lo, hi = buf.base + lo, buf.base + hi
    else:
        name = (buf.name if hasattr(buf, "name") else str(buf)) + "@0"
    return [f"{name}:{b}" for b in range(lo // 128, (hi + 127) // 128)]


DMA_POOL = 8


class Prog:
    def __init__(self, nc):
        self.nc = nc
        self.q = {e: [] for e in ENGS}
        self.esem = {}
        self.dpool = {}
        self.duse = {}
        self.didx = {}
        for e in ("tensor", "vector", "scalar", "gpsimd"):
            self.esem[e] = _Sem(nc._stk.enter_context(nc.semaphore(f"se_{e}")))
        for e in ("sync", "gpsimd", "scalar"):
            self.dpool[e] = [_Sem(nc._stk.enter_context(nc.semaphore(f"sd_{e}{i}")))
                             for i in range(DMA_POOL)]
            self.didx[e] = 0
        self.last_w = {}
        self.readers = {}
        self.waited = {e: {} for e in ENGS}
        self.nops = {e: 0 for e in ENGS}

    def _wait(self, engine, sem, val):
        w = self.waited[engine]
        k = id(sem)
        if w.get(k, 0) >= val:
            return
        w[k] = val
        self.q[engine].append(lambda eng, s=sem.h, v=val: eng.wait_ge(s, v))

    def _edges(self, engine, reads, writes, group):
        tsem = self.esem["tensor"]
        skip = (engine == "tensor")
        for k in reads:
            lw = self.last_w.get(k)
            if lw is not None and not (skip and lw[0] is tsem):
                self._wait(engine, lw[0], lw[1])
        for k in writes:
            lw = self.last_w.get(k)
            if lw is not None and not (group is not None and lw[2] == group) \
                    and not (skip and lw[0] is tsem):
                self._wait(engine, lw[0], lw[1])
            for (s, v) in self.readers.get(k, ()):
                if not (skip and s is tsem):
                    self._wait(engine, s, v)

    def _update(self, reads, writes, sem, val, group):
        for k in writes:
            self.last_w[k] = (sem, val, group)
            self.readers[k] = []
        for k in reads:
            self.readers.setdefault(k, []).append((sem, val))

    def op(self, engine, fn, reads=(), writes=(), group=None, dma=False):
        rk = [k for r in reads for k in _keys(r)]
        wk = [k for w in writes for k in _keys(w)]
        self._edges(engine, rk, wk, group)
        if dma:
            sem = self.dpool[engine][self.didx[engine] % DMA_POOL]
            self.didx[engine] += 1
            if sem.n > 0:
                self._wait(engine, sem, sem.n)
            step = 16
        else:
            sem, step = self.esem[engine], 1
        sem.n += step
        self.q[engine].append(
            lambda eng, fn=fn, s=sem.h, st=step: fn(eng).then_inc(s, st))
        self._update(rk, wk, sem, sem.n, group)
        self.nops[engine] += 1
        return (sem, sem.n)

    def dma(self, engine, out, in_, reads=(), writes=(), **kw):
        return self.op(engine,
                       lambda e, o=out, i=in_, kw=kw: e.dma_start(out=o, in_=i, **kw),
                       reads=reads, writes=writes, dma=True)

    def global_barrier(self):
        sems = list(self.esem.values()) + [s for p in self.dpool.values() for s in p]
        for e in ENGS:
            for s in sems:
                if s.n > 0:
                    self._wait(e, s, s.n)

    def finish_dmas(self, engine):
        for q, pool in self.dpool.items():
            for s in pool:
                if s.n > 0:
                    self._wait(engine, s, s.n)

    def barrier_wait(self, engine, sem_val):
        self._wait(engine, sem_val[0], sem_val[1])

    def replay(self):
        with self.nc.Block() as block:
            @block.sync
            def _(eng):
                for f in self.q["sync"]:
                    f(eng)

            @block.tensor
            def _(eng):
                for f in self.q["tensor"]:
                    f(eng)

            @block.vector
            def _(eng):
                for f in self.q["vector"]:
                    f(eng)

            @block.scalar
            def _(eng):
                for f in self.q["scalar"]:
                    f(eng)

            @block.gpsimd
            def _(eng):
                for f in self.q["gpsimd"]:
                    f(eng)


def _t(a):
    return np.ascontiguousarray(np.asarray(a).T)


def _col(b):
    b = np.asarray(b)
    return np.ascontiguousarray(b.reshape(-1, 128).T)


def prep_weights(I):
    w = {}
    for nm in ("Wq", "Wk", "Wv", "Wd"):
        w["aa_" + nm.lower()] = np.ascontiguousarray(I[f"L0_attn_a_{nm}"])
    for nm in ("bq", "bk", "bv", "bd"):
        w["aa_" + nm] = _col(I[f"L0_attn_a_{nm}"])
    w["aa_bv_row"] = np.ascontiguousarray(I["L0_attn_a_bv"].reshape(1, D))
    w["f_wqT"] = _t(I["L0_ffn_Wq"])
    w["f_wkcT"] = _t(I["L0_ffn_Wkc"])
    w["f_wvcT"] = _t(I["L0_ffn_Wvc"])
    w["f_woT"] = _t(I["L0_ffn_Wo"])
    w["f_wrT"] = _t(I["L0_ffn_Wr"])
    w["f_ebias"] = np.ascontiguousarray(I["L0_ffn_expert_bias"].reshape(1, 8))
    w["f_sw1T"] = _t(I["L0_ffn_sW1"])
    w["f_sw2T"] = _t(I["L0_ffn_sW2"])
    w["f_sw3T"] = _t(I["L0_ffn_sW3"])
    w["f_ew1T"] = np.ascontiguousarray(I["L0_ffn_eW1"].transpose(0, 2, 1))
    w["f_ew2T"] = np.ascontiguousarray(I["L0_ffn_eW2"].transpose(0, 2, 1))
    w["f_ew3T"] = np.ascontiguousarray(I["L0_ffn_eW3"].transpose(0, 2, 1))
    w["f_g1"] = _col(I["L0_ffn_g1"])
    w["f_g2"] = _col(I["L0_ffn_g2"])
    w["ab_wq"] = np.ascontiguousarray(I["L0_attn_b_Wq"])
    w["ab_wk"] = np.ascontiguousarray(I["L0_attn_b_Wk"])
    w["ab_wv"] = np.ascontiguousarray(I["L0_attn_b_Wv"])
    w["ab_wpost"] = np.ascontiguousarray(I["L0_attn_b_Wpost"])
    w["sa_waT"] = _t(I["L1_attn_a_Wa_w"])
    w["sa_bT"] = _t(I["L1_attn_a_B_w"])
    w["sa_cT"] = _t(I["L1_attn_a_C_w"])
    w["sa_dT"] = _t(I["L1_attn_a_D_w"])
    w["sa_wab"] = _col(I["L1_attn_a_Wa_b"])
    w["lb_qkvT"] = _t(I["L1_attn_b_qkv_w"])
    qb = np.asarray(I["L1_attn_b_qkv_b"])
    w["lb_qb"] = _col(qb[:D])
    w["lb_kb"] = _col(qb[D:2 * D])
    w["lb_kb_row"] = np.ascontiguousarray(qb[D:2 * D].reshape(1, D))
    w["lb_vb_row"] = np.ascontiguousarray(qb[2 * D:].reshape(1, D))
    w["lb_woT"] = _t(I["L1_attn_b_out_w"])
    w["lb_ob"] = _col(I["L1_attn_b_out_b"])
    w["lf_wrT"] = _t(I["L1_ffn_w_freq_real"])
    w["lf_wiT"] = _t(I["L1_ffn_w_freq_imag"])
    w["lf_bias"] = _col(I["L1_ffn_bias"])
    k = np.arange(D)
    ang = 2.0 * np.pi * np.outer(k, k) / D
    w["c_fr"] = np.cos(ang).astype(np.float32)
    w["c_fineg"] = np.sin(ang).astype(np.float32)
    w["c_gr"] = (np.cos(ang) / D).astype(np.float32)
    w["c_gi"] = (np.sin(ang) / D).astype(np.float32)
    w["c_gineg"] = (-np.sin(ang) / D).astype(np.float32)
    w["c_identb"] = np.eye(128, dtype=BF)
    w["c_identf"] = np.eye(128, dtype=np.float32)
    j = np.arange(128)[:, None]
    t = np.arange(1792)[None, :]
    w["c_band"] = (np.abs(j - (t - 768)) <= 256).astype(BF)
    cm = np.zeros((128, 2 * 256), dtype=BF)
    for jt in range(2):
        j = jt * 128 + np.arange(128)[:, None]
        t = np.arange(256)[None, :]
        cm[:, jt * 256:(jt + 1) * 256] = (j <= t).astype(BF)
    w["c_causal"] = cm
    e8 = np.zeros((8, 8 * 128), dtype=np.float32)
    for e in range(8):
        e8[e, e * 128:(e + 1) * 128] = 1.0
    w["c_e8"] = e8
    return w


class Ctx:
    pass


def XK(buf, lo, hi):
    return (buf, lo, hi)


def _recip_bf16(e, o, i):
    with e.bass.allow_low_precision(reason="attention denominator reciprocal "
                                    "rounded to bf16 for the cheap broadcast"):
        return e.reciprocal(out=o, in_=i)


def _copy_bf16(e, o, i):
    with e.bass.allow_low_precision(reason="staging a bf16 matmul operand; "
                                    "same rounding as the ACT Identity evac"):
        return e.tensor_copy(out=o, in_=i)


def emit_ln_rows(P, c, X, x0, mrow, rrow, mrrow, key):
    nc = P.nc
    for tg in range(NTG):
        t0 = tg * TG
        ps_sx = c.ps[4]
        ps_sx2 = c.ps[5]
        for ch in range(NCH):
            xr = XK(X, ch * S + t0, ch * S + t0 + TG)
            xcol = X[:, ch * S + t0: ch * S + t0 + TG]
            P.op("scalar", lambda e, o=c.sc_u[:, :TG], i=xcol: e.activation(
                out=o, in_=i, func=AF.Square),
                reads=[xr], writes=[XK(c.sc_u, 0, TG)])
            P.op("tensor", lambda e, o=ps_sx[0:1, :TG], l=c.ONESF[:, 0:1], r=xcol,
                 st=(ch == 0), sp=(ch == NCH - 1): e.matmul(
                     out=o, lhsT=l, rhs=r, start=st, stop=sp),
                 reads=[xr, "ones1f"], writes=["ps4"], group=key + f"sx{tg}")
            P.op("tensor", lambda e, o=ps_sx2[0:1, :TG], l=c.ONESF[:, 0:1], r=c.sc_u[:, :TG],
                 st=(ch == 0), sp=(ch == NCH - 1): e.matmul(
                     out=o, lhsT=l, rhs=r, start=st, stop=sp),
                 reads=[XK(c.sc_u, 0, TG), "onesc"], writes=["ps5"], group=key + f"sx2{tg}")
        mr = c.row_b[:, :TG]
        rr = rrow[:, t0:t0 + TG]
        mrr = mrrow[:, t0:t0 + TG]
        vr = c.row_a[:, :TG]
        P.op("vector", lambda e, o=mr, i=ps_sx[0:1, :TG]: e.tensor_scalar_mul(
            out=o, in0=i, scalar1=1.0 / D),
            reads=["ps4"], writes=[key + "m"])
        P.op("vector", lambda e, o=vr, i=mr: e.tensor_tensor(
            out=o, in0=i, in1=i, op=ALU.mult),
            reads=[key + "m"], writes=["row_a"])
        P.op("vector", lambda e, o=vr, i=ps_sx2[0:1, :TG], t=vr: e.scalar_tensor_tensor(
            out=o, in0=i, scalar=1.0 / D, in1=t, op0=ALU.mult, op1=ALU.subtract),
            reads=["ps5", "row_a"], writes=["row_a"])
        P.op("scalar", lambda e, o=vr, i=vr, b=c.eps_ln[0:1, 0:1]: e.activation(
            out=o, in_=i, func=AF.Ln, bias=b),
            reads=["row_a", "consts"], writes=["row_a"])
        P.op("scalar", lambda e, o=rr, i=vr: e.activation(
            out=o, in_=i, func=AF.Exp, scale=-0.5),
            reads=["row_a"], writes=[key + "r"])
        P.op("vector", lambda e, o=mrr, a=mr, b=rr: e.tensor_tensor(
            out=o, in0=a, in1=b, op=ALU.mult),
            reads=[key + "m", key + "r"], writes=[key + "mr"])


def emit_ln_bcast_w(P, c, rrow, mrrow, t0, w, key, bank_r, bank_mr):
    p1, p2 = rrow.p0, mrrow.p0
    P.op("tensor", lambda e, o=c.ps[bank_r][:, :w], l=c.ONESF[p1:p1 + 1, :],
         r=rrow[:, t0:t0 + w]: e.matmul(out=o, lhsT=l, rhs=r, start=True, stop=True),
         reads=[key + "r", "ones1f"], writes=[f"ps{bank_r}"])
    P.op("tensor", lambda e, o=c.ps[bank_mr][:, :w], l=c.ONESF[p2:p2 + 1, :],
         r=mrrow[:, t0:t0 + w]: e.matmul(out=o, lhsT=l, rhs=r, start=True, stop=True),
         reads=[key + "mr", "ones1f"], writes=[f"ps{bank_mr}"])
    P.op("scalar", lambda e, o=c.rb[:, 0:w], i=c.ps[bank_r][:, :w]:
         e.activation(out=o, in_=i, func=AF.Identity),
         reads=[f"ps{bank_r}"], writes=["rbc_r"])
    P.op("scalar", lambda e, o=c.mrb[:, 0:w], i=c.ps[bank_mr][:, :w]:
         e.activation(out=o, in_=i, func=AF.Identity),
         reads=[f"ps{bank_mr}"], writes=["rbc_mr"])


def emit_ln_bcast(P, c, rrow, mrrow, tg, key, bank_r, bank_mr):
    emit_ln_bcast_w(P, c, rrow, mrrow, tg * TG, TG, key, bank_r, bank_mr)


def emit_xn_tile(P, c, X, ch, t0, w, dst, dcol, bank_r, bank_mr, extra_write=None):
    xr = XK(X, ch * S + t0, ch * S + t0 + w)
    if ch % 2 == 0:
        u = c.sc_u[:, :w]
        P.op("vector", lambda e, o=u, a=X[:, ch * S + t0:ch * S + t0 + w],
             b=c.rb[:, 0:w]: e.tensor_tensor(out=o, in0=a, in1=b, op=ALU.mult),
             reads=[xr, "rbc_r"], writes=[XK(c.sc_u, 0, w)])
        P.op("vector", lambda e, o=dst[:, dcol:dcol + w], a=u, b=c.mrb[:, 0:w]:
             e.tensor_tensor(out=o, in0=a, in1=b, op=ALU.subtract),
             reads=[XK(c.sc_u, 0, w), "rbc_mr"], writes=[XK(dst, dcol, dcol + w)])
    else:
        P.op("gpsimd", lambda e, o=dst[:, dcol:dcol + w],
             a=X[:, ch * S + t0:ch * S + t0 + w], b=c.rb[:, 0:w]:
             e.tensor_tensor(out=o, in0=a, in1=b, op=ALU.mult),
             reads=[xr, "rbc_r"], writes=[XK(dst, dcol, dcol + w)])
        P.op("gpsimd", lambda e, o=dst[:, dcol:dcol + w], a=dst[:, dcol:dcol + w],
             b=c.mrb[:, 0:w]: e.tensor_tensor(out=o, in0=a, in1=b, op=ALU.subtract),
             reads=[XK(dst, dcol, dcol + w), "rbc_mr"],
             writes=[XK(dst, dcol, dcol + w)])


def emit_wload(P, c, sbuf, dram, rows, cols, key, row0=0, col0=0, wid=None,
               spill=True):
    nch = rows // 128
    n = nch * cols
    wid = wid or key
    scr = c.spills.get(wid)
    if scr is None:
        src = dram[row0:row0 + rows, col0:col0 + cols].rearrange(
            "(c p) f -> p c f", p=128)
        P.dma("gpsimd", sbuf[:, :n].rearrange("p (c f) -> p c f", c=nch),
              src, writes=[XK(sbuf, 0, n)])
        if spill:
            scr = P.nc.dram_tensor(f"sp_{wid}", [128, n], BF16,
                                   kind="Internal").ap()
            P.dma("sync", scr, sbuf[:, :n], reads=[XK(sbuf, 0, n)],
                  writes=[f"sp_{wid}"])
            c.spills[wid] = scr
    else:
        h = max(n // 2 // 128 * 128, 128) if n >= 256 else n
        P.dma("sync", sbuf[:, 0:h], scr[:, 0:h], reads=[f"sp_{wid}"],
              writes=[XK(sbuf, 0, h)])
        if h < n:
            P.dma("scalar", sbuf[:, h:n], scr[:, h:n], reads=[f"sp_{wid}"],
                  writes=[XK(sbuf, h, n)])
    return nch


def emit_proj(P, c, wbuf, wkey, nch_in, fout, rhs_buf, rhs_key, rhs_col, wtile,
              epilogue, banks=(0, 1), wr=128):
    nf = (fout + wr - 1) // wr
    for fch in range(nf):
        w0 = fch * wr
        wlen = min(wr, fout - w0)
        bank = banks[fch % len(banks)]
        for ch in range(nch_in):
            rc = rhs_col(ch)
            P.op("tensor", lambda e, o=c.ps[bank][:wlen, :TG],
                 l=wbuf[:, ch * fout + w0: ch * fout + w0 + wlen],
                 r=rhs_buf[:, rc:rc + TG],
                 st=(ch == 0), sp=(ch == nch_in - 1): e.matmul(
                     out=o, lhsT=l, rhs=r, start=st, stop=sp),
                 reads=[XK(wbuf, ch * fout + w0, ch * fout + w0 + wlen),
                        XK(rhs_buf, rc, rc + TG)],
                 writes=[f"ps{bank}"], group=f"{wkey}g{fch}")
        epilogue(fch, bank)


def emit_l1_attn_a(P, c, X):
    nc = P.nc
    emit_ln_rows(P, c, X, 0, c.row_m, c.row_r, c.row_mr, "sa_ln")
    emit_wload(P, c, c.wA, c.d_sa_waT, 1024, 256, "wA")
    emit_wload(P, c, c.wB, c.d_sa_bT, 1024, 256, "wB")
    emit_wload(P, c, c.wC, c.d_sa_cT, 256, 1024, "wC")
    emit_wload(P, c, c.wD, c.d_sa_dT, 1024, 1024, "wD")
    P.dma("sync", c.colA[:, 0:2], c.d_sa_wab[:, 0:2], writes=["colA"])

    for tg in range(NTG):
        t0 = tg * TG
        emit_ln_bcast(P, c, c.row_r, c.row_mr, tg, "sa_ln", 5, 6)
        for ch in range(NCH):
            emit_xn_tile(P, c, X, ch, t0, TG, c.xn, ch * TG, 5, 6)
        def ep_a(fch, bank, t0=t0):
            P.op("scalar", lambda e, o=c.bufA[:, fch * S + t0: fch * S + t0 + TG],
                 i=c.ps[bank][:, :TG], b=c.colA[:, fch:fch + 1]: e.activation(
                     out=o, in_=i, func=AF.Sigmoid, bias=b),
                 reads=[f"ps{bank}", "colA"],
                 writes=[XK(c.bufA, fch * S + t0, fch * S + t0 + TG)])
        emit_proj(P, c, c.wA, f"wA{tg}", NCH, 256, c.xn, "xn",
                  lambda ch: ch * TG, 128, ep_a, banks=(0, 1))

        def ep_b(fch, bank, t0=t0):
            P.op("scalar", lambda e, o=c.bufB[:, fch * S + t0: fch * S + t0 + TG],
                 i=c.ps[bank][:, :TG]: e.activation(out=o, in_=i, func=AF.Identity),
                 reads=[f"ps{bank}"],
                 writes=[XK(c.bufB, fch * S + t0, fch * S + t0 + TG)])
        emit_proj(P, c, c.wB, f"wB{tg}", NCH, 256, c.xn, "xn",
                  lambda ch: ch * TG, 128, ep_b, banks=(2, 3))
        def ep_d(fch, bank, t0=t0):
            P.op("vector", lambda e, o=X[:, fch * S + t0: fch * S + t0 + TG],
                 a=X[:, fch * S + t0: fch * S + t0 + TG], b=c.ps[bank][:, :TG]:
                 e.tensor_tensor(out=o, in0=a, in1=b, op=ALU.add),
                 reads=[f"ps{bank}", XK(X, fch * S + t0, fch * S + t0 + TG)],
                 writes=[XK(X, fch * S + t0, fch * S + t0 + TG)])
        emit_proj(P, c, c.wD, f"wD{tg}", NCH, 1024, c.xn, "xn",
                  lambda ch: ch * TG, 128, ep_d, banks=(0, 1, 2, 3))

    for ch in range(2):
        P.op("vector", lambda e, o=c.bufH[:, ch * S:(ch + 1) * S],
             a=c.bufA[:, ch * S:(ch + 1) * S], b=c.bufB[:, ch * S:(ch + 1) * S]:
             e.tensor_tensor_scan(out=o, data0=a, data1=b, initial=0.0,
                                  op0=ALU.mult, op1=ALU.add),
             reads=[XK(c.bufA, ch * S, (ch + 1) * S), XK(c.bufB, ch * S, (ch + 1) * S)],
             writes=[XK(c.bufH, ch * S, (ch + 1) * S)])

    for tg in range(NTG):
        t0 = tg * TG

        def ep_c(fch, bank, t0=t0):
            P.op("vector", lambda e, o=X[:, fch * S + t0: fch * S + t0 + TG],
                 a=X[:, fch * S + t0: fch * S + t0 + TG], b=c.ps[bank][:, :TG]:
                 e.tensor_tensor(out=o, in0=a, in1=b, op=ALU.add),
                 reads=[f"ps{bank}", XK(X, fch * S + t0, fch * S + t0 + TG)],
                 writes=[XK(X, fch * S + t0, fch * S + t0 + TG)])
        emit_proj(P, c, c.wC, f"wC{tg}", 2, 1024, c.bufH, "bufH",
                  lambda ch, t0=t0: ch * S + t0, 128, ep_c, banks=(0, 1, 2, 3))


def alloc_ctx(nc, stk, need):
    c = Ctx()
    c.spills = {}
    sb = lambda name, shape, dt: stk.enter_context(nc.sbuf_tensor(name, shape, dt))
    psa = lambda name, shape, dt: stk.enter_context(nc.psum_tensor(name, shape, dt))
    c.X = sb("Xres", [128, NCH * S], F32)
    c.Wp1 = sb("Wp1", [128, 8192], BF16)
    c.Wp2 = sb("Wp2", [128, 8192], BF16)
    c.Wp3 = sb("Wp3", [128, 8192], BF16)
    c.B1 = sb("B1", [128, 16384], BF16)
    c.B2 = sb("B2", [128, 16384], BF16)
    c.ps = [psa(f"pb{i}", [128, 512], F32) for i in range(7)]
    c.psb = psa("pb7b", [128, 1024], BF16)
    c.xn = sb("xn", [128, NCH * TG], BF16)
    c.sc_u = sb("sc_u", [128, TG], F32)
    c.RBC = sb("RBC", [128, 1024], F32)
    c.rb = Rv(c.RBC, 0, 512)
    c.mrb = Rv(c.RBC, 512, 512)
    c.ROWS = sb("ROWS", [128, S], F32)
    c.row_m = None
    c.row_r = Rv(c.ROWS, 0, S, p0=32, np_=1)
    c.row_mr = Rv(c.ROWS, 0, S, p0=64, np_=1)
    c.ROWB = sb("ROWB", [128, TG], F32)
    c.row_a = Rv(c.ROWB, 0, TG, p0=0, np_=1)
    c.row_b = Rv(c.ROWB, 0, TG, p0=32, np_=1)
    c.row_c = Rv(c.ROWB, 0, TG, p0=64, np_=1)
    c.onesc = sb("onesc", [128, 1], BF16)
    c.eps_ln = sb("eps_ln", [1, 1], F32)
    c.eps_rms = sb("eps_rms", [1, 1], F32)
    c.ONESF = sb("ONESF", [128, 128], F32)
    c.ONESB = sb("ONESB", [128, 128], BF16)
    c.colA = sb("colA", [128, 32], F32)
    c.colB = sb("colB", [128, 32], F32)
    c.CB = sb("CB", [128, 640], BF16)
    c.causal = Rv(c.CB, 0, 512)
    c.identb = Rv(c.CB, 512, 128)
    c.row_r2 = Rv(c.ROWS, 0, S, p0=0, np_=1)
    c.CB2 = sb("CB2", [128, 1600], BF16)
    c.kb_row = Rv(c.CB2, 0, 1024, p0=0, np_=1)
    c.vb_row = Rv(c.CB2, 0, 1024, p0=32, np_=1)
    c.bv_row = Rv(c.CB2, 0, 1024, p0=64, np_=1)
    c.eb_row = Rv(c.CB2, 1024, 8, p0=0, np_=1)
    c.den_bf = Rv(c.CB2, 1056, 512, p0=0, np_=1)
    c.drep = Rv(c.CB2, 1056, 512)
    c.colC = sb("colC", [128, 16], F32)
    c.KVF = sb("KVF", [128, 520], F32)
    c.CE8 = Rv(c.B2, 12288, 1024, p0=0, np_=8)
    c.COMB = Rv(c.B2, 14336, S, p0=0, np_=8)
    c.wA = Rv(c.Wp1, 0, 2048)
    c.wB = Rv(c.Wp1, 2048, 2048)
    c.wC = Rv(c.Wp1, 4096, 2048)
    c.wD = Rv(c.Wp2, 0, 8192)
    c.bufA = Rv(c.B1, 0, 4096)
    c.bufB = Rv(c.B1, 4096, 4096)
    c.bufH = Rv(c.B1, 8192, 4096)
    c.Mv = Rv(c.B2, 0, 8192)
    c.m_built = False
    c.m_resident = False
    return c


def emit_const_init(P, c, stages):
    P.op("gpsimd", lambda e: e.memset(c.onesc[:], 1.0), writes=["onesc"])
    P.op("gpsimd", lambda e: e.memset(c.ONESF[:], 1.0), writes=["ones1f"])
    P.op("gpsimd", lambda e: e.memset(c.ONESB[:], 1.0), writes=["ones1b"])
    P.op("gpsimd", lambda e: e.memset(c.eps_ln[:], EPS_LN), writes=["consts"])
    P.op("gpsimd", lambda e: e.memset(c.eps_rms[:], EPS_RMS), writes=["consts"])
    st = set(stages)
    if "l1_attn_b" in st:
        P.dma("gpsimd", c.causal[:, :], c.d_c_causal, writes=[(c.causal, 0, 512)])
        P.dma("gpsimd", c.kb_row[:, :], c.d_lb_kb_row, writes=["kb_row"])
        P.dma("gpsimd", c.vb_row[:, :], c.d_lb_vb_row, writes=["vb_row"])
    if "l0_attn_a" in st:
        P.dma("gpsimd", c.bv_row[:, :], c.d_aa_bv_row, writes=["bv_row"])
    if "l0_ffn" in st:
        P.dma("gpsimd", c.identb[:, :], c.d_c_identb, writes=[(c.identb, 0, 128)])
    if "l0_ffn" in st:
        P.dma("gpsimd", c.eb_row[:, :], c.d_f_ebias, writes=["eb_row"])


def emit_m_build(P, c):
    wr = Rv(c.Wp1, 0, 8192)
    wi = Rv(c.Wp2, 0, 8192)
    Fr = Rv(c.Wp3, 0, 8192)
    Fin = Rv(c.B1, 0, 8192)
    TrB = Rv(c.B1, 8192, 8192)
    TiB = Rv(c.B2, 8192, 8192)
    emit_wload(P, c, wr, c.d_lf_wrT, 1024, 1024, "wr", spill=False)
    emit_wload(P, c, wi, c.d_lf_wiT, 1024, 1024, "wi", spill=False)
    emit_wload(P, c, Fr, c.d_c_fr, 1024, 1024, "fr", spill=False)
    emit_wload(P, c, Fin, c.d_c_fineg, 1024, 1024, "fin", spill=False)
    gq = [Rv(c.xn, 0, 1024), Rv(c.xn, 1024, 1024), Rv(c.xn, 2048, 1024)]
    for oq in range(4):
        o0 = oq * 256
        for kh in range(2):
            for k2 in range(8):
                for gi, dsrc in enumerate((c.d_c_gr, c.d_c_gi, c.d_c_gineg)):
                    P.dma("gpsimd", gq[gi][:, :256],
                          dsrc[k2 * 128:(k2 + 1) * 128, o0:o0 + 256],
                          writes=[(gq[gi], 0, 256)])
                for kb in range(4):
                    kc = kh * 4 + kb
                    wcol = k2 * 1024 + kc * 128
                    P.op("tensor", lambda e, o=c.ps[kb][:, 0:256],
                         l=wr[:, wcol:wcol + 128], r=gq[0][:, :256],
                         st=(k2 == 0): e.matmul(out=o, lhsT=l, rhs=r, start=st, stop=False),
                         reads=[(wr, wcol, wcol + 128), (gq[0], 0, 256)],
                         writes=[f"ps{kb}"], group=f"tr{oq}_{kc}")
                    P.op("tensor", lambda e, o=c.ps[kb][:, 0:256],
                         l=wi[:, wcol:wcol + 128], r=gq[2][:, :256]: e.matmul(
                             out=o, lhsT=l, rhs=r, start=False, stop=False),
                         reads=[(wi, wcol, wcol + 128), (gq[2], 0, 256)],
                         writes=[f"ps{kb}"], group=f"tr{oq}_{kc}")
                    P.op("tensor", lambda e, o=c.ps[kb][:, 256:512],
                         l=wr[:, wcol:wcol + 128], r=gq[1][:, :256]: e.matmul(
                             out=o, lhsT=l, rhs=r, start=False, stop=False),
                         reads=[(wr, wcol, wcol + 128), (gq[1], 0, 256)],
                         writes=[f"ps{kb}"], group=f"tr{oq}_{kc}")
                    P.op("tensor", lambda e, o=c.ps[kb][:, 256:512],
                         l=wi[:, wcol:wcol + 128], r=gq[0][:, :256],
                         sp=(k2 == 7): e.matmul(out=o, lhsT=l, rhs=r, start=False, stop=sp),
                         reads=[(wi, wcol, wcol + 128), (gq[0], 0, 256)],
                         writes=[f"ps{kb}"], group=f"tr{oq}_{kc}")
            for kb in range(4):
                kc = kh * 4 + kb
                P.op("vector", lambda e, o=TrB[:, kc * 1024 + o0: kc * 1024 + o0 + 256],
                     i=c.ps[kb][:, 0:256]: e.tensor_copy(out=o, in_=i),
                     reads=[f"ps{kb}"], writes=[(TrB, kc * 1024 + o0, kc * 1024 + o0 + 256)])
                P.op("vector", lambda e, o=TiB[:, kc * 1024 + o0: kc * 1024 + o0 + 256],
                     i=c.ps[kb][:, 256:512]: e.tensor_copy(out=o, in_=i),
                     reads=[f"ps{kb}"], writes=[(TiB, kc * 1024 + o0, kc * 1024 + o0 + 256)])
    for oh in range(2):
        o0 = oh * 512
        for dch in range(8):
            bank = dch % 4
            for kc in range(8):
                P.op("tensor", lambda e, o=c.ps[bank][:, :512],
                     l=Fr[:, kc * 1024 + dch * 128: kc * 1024 + dch * 128 + 128],
                     r=TrB[:, kc * 1024 + o0: kc * 1024 + o0 + 512],
                     st=(kc == 0): e.matmul(out=o, lhsT=l, rhs=r, start=st, stop=False),
                     reads=[(Fr, kc * 1024 + dch * 128, kc * 1024 + dch * 128 + 128),
                            (TrB, kc * 1024 + o0, kc * 1024 + o0 + 512)],
                     writes=[f"ps{bank}"], group=f"mb{oh}_{dch}")
                P.op("tensor", lambda e, o=c.ps[bank][:, :512],
                     l=Fin[:, kc * 1024 + dch * 128: kc * 1024 + dch * 128 + 128],
                     r=TiB[:, kc * 1024 + o0: kc * 1024 + o0 + 512],
                     sp=(kc == 7): e.matmul(out=o, lhsT=l, rhs=r, start=False, stop=sp),
                     reads=[(Fin, kc * 1024 + dch * 128, kc * 1024 + dch * 128 + 128),
                            (TiB, kc * 1024 + o0, kc * 1024 + o0 + 512)],
                     writes=[f"ps{bank}"], group=f"mb{oh}_{dch}")
            P.op("vector", lambda e, o=c.Mv[:, dch * 1024 + o0: dch * 1024 + o0 + 512],
                 i=c.ps[bank][:, :512]: e.tensor_copy(out=o, in_=i),
                 reads=[f"ps{bank}"],
                 writes=[(c.Mv, dch * 1024 + o0, dch * 1024 + o0 + 512)])
    P.dma("sync", c.d_mscr, c.Mv[:, :], reads=[(c.Mv, 0, 8192)], writes=["mscr"])
    if DEBUG_DUMP_M and hasattr(c, "d_mdump"):
        P.dma("sync", c.d_mdump, c.Mv[:, :], reads=[(c.Mv, 0, 8192)], writes=["mdump"])
    c.m_built = True
    c.m_resident = True


def emit_l1_ffn(P, c, X):
    emit_ln_rows(P, c, X, 0, c.row_m, c.row_r, c.row_mr, "lf_ln")
    if not c.m_built:
        emit_m_build(P, c)
    elif not c.m_resident:
        P.dma("sync", c.Mv[:, :], c.d_mscr, reads=["mscr"], writes=[(c.Mv, 0, 8192)])
        c.m_resident = True
    P.dma("sync", c.colA[:, 0:8], c.d_lf_bias[:, 0:8], writes=["colA"])
    for tg in range(NTG):
        t0 = tg * TG
        emit_ln_bcast(P, c, c.row_r, c.row_mr, tg, "lf_ln", 5, 6)
        for ch in range(NCH):
            emit_xn_tile(P, c, X, ch, t0, TG, c.xn, ch * TG, 5, 6)

        def ep(fch, bank, t0=t0):
            P.op("vector", lambda e, o=X[:, fch * S + t0: fch * S + t0 + TG],
                 i=c.ps[bank][:, :TG], s=c.colA[:, fch:fch + 1],
                 x=X[:, fch * S + t0: fch * S + t0 + TG]: e.scalar_tensor_tensor(
                     out=o, in0=i, scalar=s, in1=x, op0=ALU.add, op1=ALU.add),
                 reads=[f"ps{bank}", "colA", XK(X, fch * S + t0, fch * S + t0 + TG)],
                 writes=[XK(X, fch * S + t0, fch * S + t0 + TG)])
        emit_proj(P, c, c.Mv, f"lfm{tg}", NCH, 1024, c.xn, "xn",
                  lambda ch: ch * TG, 128, ep, banks=(0, 1, 2, 3))


def emit_l0_ffn(P, c, X, inv):
    K = f"f{inv}"
    SCALE = 1.0 / 16.0
    c.m_resident = False
    emit_ln_rows(P, c, X, 0, c.row_m, c.row_r, c.row_mr, K + "ln")
    KCB = Rv(c.B1, 0, 4096)
    VCB = Rv(c.B1, 4096, 4096)
    AB = Rv(c.B1, 10240, 1024)
    W1B = Rv(c.B1, 0, 16384)
    NX2 = Rv(c.B2, 0, 16384)
    OB = Rv(c.B2, 0, 16384)
    G1S = Rv(c.B1, 0, 4096)
    G3S = Rv(c.B1, 4096, 8192 - 4096)
    CBT = Rv(c.B1, 15360, 512)
    COMB = Rv(c.B1, 12288, 2048, p0=0, np_=8)
    CE8 = Rv(c.B1, 14336, 1024, p0=0, np_=8)
    emit_wload(P, c, Rv(c.Wp1, 0, 8192), c.d_f_wqT, 1024, 1024, K + "wq",
               wid="fwq")
    emit_wload(P, c, Rv(c.Wp2, 0, 2048), c.d_f_wkcT, 1024, 256, K + "wkc",
               wid="fwkc")
    emit_wload(P, c, Rv(c.Wp2, 2048, 2048), c.d_f_wvcT, 1024, 256, K + "wvc",
               wid="fwvc")
    emit_wload(P, c, Rv(c.Wp3, 0, 8192), c.d_f_woT, 1024, 1024, K + "wo",
               wid="fwo")
    P.dma("sync", c.colA[:, 0:8], c.d_f_g1[:, 0:8], writes=["colA"])
    P.dma("sync", c.colA[:, 8:16], c.d_f_g2[:, 0:8], writes=["colA"])

    for tg in range(NTG):
        t0 = tg * TG
        emit_ln_bcast(P, c, c.row_r, c.row_mr, tg, K + "ln", 5, 6)
        for ch in range(NCH):
            emit_xn_tile(P, c, X, ch, t0, TG, c.xn, ch * TG, 5, 6)
        for ch in range(NCH):
            P.op("scalar", lambda e, o=c.sc_u[:, :TG], i=c.xn[:, ch * TG: ch * TG + TG]:
                 e.activation(out=o, in_=i, func=AF.Square),
                 reads=[XK(c.xn, ch * TG, ch * TG + TG)], writes=[XK(c.sc_u, 0, TG)])
            P.op("tensor", lambda e, o=c.ps[4][0:1, :TG], l=c.ONESF[:, 0:1], r=c.sc_u[:, :TG],
                 st=(ch == 0), sp=(ch == NCH - 1): e.matmul(out=o, lhsT=l, rhs=r,
                                                            start=st, stop=sp),
                 reads=[XK(c.sc_u, 0, TG), "onesc"], writes=["ps4"],
                 group=K + f"rms{tg}")
        va = c.row_a[:, :TG]
        P.op("scalar", lambda e, o=va, i=c.ps[4][0:1, :TG], b=c.eps_rms[0:1, 0:1]:
             e.activation(out=o, in_=i, func=AF.Ln, bias=b, scale=1.0 / D),
             reads=["ps4", "consts"], writes=["row_a"])
        P.op("scalar", lambda e, o=c.row_r2[:, t0:t0 + TG], i=va: e.activation(
            out=o, in_=i, func=AF.Exp, scale=-0.5),
            reads=["row_a"], writes=["row_r2"])
        P.op("tensor", lambda e, o=c.ps[4][:, :TG], l=c.ONESF[0:1, :],
             r=c.row_r2[:, t0:t0 + TG]: e.matmul(out=o, lhsT=l, rhs=r,
                                                 start=True, stop=True),
             reads=["row_r2", "ones1f"], writes=["ps4"])
        for ch in range(NCH):
            P.op("vector", lambda e, o=c.xn[:, ch * TG: ch * TG + TG],
                 i=c.xn[:, ch * TG: ch * TG + TG], s=c.colA[:, ch:ch + 1],
                 b=c.ps[4][:, :TG]: e.scalar_tensor_tensor(
                     out=o, in0=i, scalar=s, in1=b, op0=ALU.mult, op1=ALU.mult),
                 reads=[XK(c.xn, ch * TG, ch * TG + TG), "colA", "ps4"],
                 writes=[XK(c.xn, ch * TG, ch * TG + TG)])
        for f2 in range(2):
            bank = f2
            for ch in range(NCH):
                P.op("tensor", lambda e, o=c.ps[bank][:, :TG],
                     l=c.Wp2[:, ch * 256 + f2 * 128: ch * 256 + f2 * 128 + 128],
                     r=c.xn[:, ch * TG: ch * TG + TG],
                     st=(ch == 0), sp=(ch == NCH - 1): e.matmul(
                         out=o, lhsT=l, rhs=r, start=st, stop=sp),
                     reads=[(c.Wp2, ch * 256 + f2 * 128, ch * 256 + f2 * 128 + 128),
                            XK(c.xn, ch * TG, ch * TG + TG)],
                     writes=[f"ps{bank}"], group=K + f"kc{tg}_{f2}")
            P.op("scalar", lambda e, o=KCB[:, f2 * 2048 + t0: f2 * 2048 + t0 + TG],
                 i=c.ps[bank][:, :TG]: e.activation(out=o, in_=i, func=AF.Identity),
                 reads=[f"ps{bank}"],
                 writes=[(KCB, f2 * 2048 + t0, f2 * 2048 + t0 + TG)])
        for ti in range(4):
            tt = tg * 4 + ti
            bank = 2 + ti % 2
            for ch in range(NCH):
                P.op("tensor", lambda e, o=c.ps[bank][:, :256],
                     l=c.xn[:, ch * TG + ti * 128: ch * TG + ti * 128 + 128],
                     r=c.Wp2[:, 2048 + ch * 256: 2048 + ch * 256 + 256],
                     st=(ch == 0), sp=(ch == NCH - 1): e.matmul(
                         out=o, lhsT=l, rhs=r, start=st, stop=sp),
                     reads=[(c.Wp2, 2048 + ch * 256, 2048 + ch * 256 + 256),
                            XK(c.xn, ch * TG + ti * 128, ch * TG + ti * 128 + 128)],
                     writes=[f"ps{bank}"], group=K + f"vc{tt}")
            P.op("scalar", lambda e, o=VCB[:, tt * 256: tt * 256 + 256],
                 i=c.ps[bank][:, :256]: e.activation(out=o, in_=i, func=AF.Identity),
                 reads=[f"ps{bank}"], writes=[(VCB, tt * 256, tt * 256 + 256)])
        for f2 in range(2):
            P.op("scalar", lambda e, o=c.sc_u[:, :TG],
                 i=KCB[:, f2 * 2048 + t0: f2 * 2048 + t0 + TG]: e.activation(
                     out=o, in_=i, func=AF.Square),
                 reads=[(KCB, f2 * 2048 + t0, f2 * 2048 + t0 + TG)],
                 writes=[XK(c.sc_u, 0, TG)])
            P.op("tensor", lambda e, o=c.ps[0][:, :TG], l=c.ONESF[:, 0:128],
                 r=c.sc_u[:, :TG], st=(f2 == 0), sp=(f2 == 1): e.matmul(
                     out=o, lhsT=l, rhs=r, start=st, stop=sp),
                 reads=[XK(c.sc_u, 0, TG), "onesc"], writes=["ps0"],
                 group=K + f"kn{tg}")
        P.op("vector", lambda e, o=c.colB[:, 0:1], i=c.ps[0][:, :TG]:
             e.reduce_max(out=o, in_=i, axis=AX.X),
             reads=["ps0"], writes=["colB1"])
        if tg == 0:
            P.op("vector", lambda e, o=c.colB[:, 1:2], i=c.colB[:, 0:1]:
                 e.tensor_copy(out=o, in_=i), reads=["colB1"], writes=["colB2"])
        else:
            P.op("vector", lambda e, o=c.colB[:, 1:2], a=c.colB[:, 0:1],
                 b=c.colB[:, 1:2]: e.tensor_tensor(out=o, in0=a, in1=b, op=ALU.max),
                 reads=["colB1", "colB2"], writes=["colB2"])

    for qg in range(NTG):
        t0 = qg * TG
        emit_ln_bcast(P, c, c.row_r, c.row_mr, qg, K + "ln", 5, 6)
        P.op("tensor", lambda e, o=c.ps[4][:, :TG], l=c.ONESF[0:1, :],
             r=c.row_r2[:, t0:t0 + TG]: e.matmul(out=o, lhsT=l, rhs=r,
                                                 start=True, stop=True),
             reads=["row_r2", "ones1f"], writes=["ps4"])
        for ch in range(NCH):
            emit_xn_tile(P, c, X, ch, t0, TG, c.xn, ch * TG, 5, 6)
            P.op("vector", lambda e, o=c.xn[:, ch * TG: ch * TG + TG],
                 i=c.xn[:, ch * TG: ch * TG + TG], s=c.colA[:, ch:ch + 1],
                 b=c.ps[4][:, :TG]: e.scalar_tensor_tensor(
                     out=o, in0=i, scalar=s, in1=b, op0=ALU.mult, op1=ALU.mult),
                 reads=[XK(c.xn, ch * TG, ch * TG + TG), "colA", "ps4"],
                 writes=[XK(c.xn, ch * TG, ch * TG + TG)])
        QB2 = [Rv(c.B1, 8192, 1024), Rv(c.B1, 9216, 1024)]

        def prep_a(h, sl):
            for f2 in range(2):
                for ch in range(NCH):
                    wc = ch * 1024 + h * 256 + f2 * 128
                    P.op("tensor", lambda e, o=c.ps[6][:, :TG],
                         l=c.Wp1[:, wc:wc + 128], r=c.xn[:, ch * TG: ch * TG + TG],
                         st=(ch == 0), sp=(ch == NCH - 1): e.matmul(
                             out=o, lhsT=l, rhs=r, start=st, stop=sp),
                         reads=[(c.Wp1, wc, wc + 128),
                                XK(c.xn, ch * TG, ch * TG + TG)],
                         writes=["ps6"], group=K + f"q{qg}_{h}_{f2}")
                P.op("scalar", lambda e, o=QB2[sl][:, f2 * TG: f2 * TG + TG],
                     i=c.ps[6][:, :TG]: e.activation(out=o, in_=i, func=AF.Identity),
                     reads=["ps6"], writes=[(QB2[sl], f2 * TG, f2 * TG + TG)])
        def prep_b(h, sl):
            for f2 in range(2):
                P.op("scalar", lambda e, o=c.sc_u[:, :TG],
                     i=QB2[sl][:, f2 * TG: f2 * TG + TG]: e.activation(
                         out=o, in_=i, func=AF.Square),
                     reads=[(QB2[sl], f2 * TG, f2 * TG + TG)], writes=[XK(c.sc_u, 0, TG)])
                P.op("tensor", lambda e, o=c.ps[6][:, :TG], l=c.ONESF[:, 0:128],
                     r=c.sc_u[:, :TG], st=(f2 == 0), sp=(f2 == 1): e.matmul(
                         out=o, lhsT=l, rhs=r, start=st, stop=sp),
                     reads=[XK(c.sc_u, 0, TG), "ones1f"], writes=["ps6"],
                     group=K + f"qn{qg}_{h}")
            P.op("vector", lambda e, o=c.colB[:, 2:3], i=c.ps[6][:, :TG]:
                 e.reduce_max(out=o, in_=i, axis=AX.X), reads=["ps6"], writes=["colB3"])
            P.op("vector", lambda e, o=c.colB[:, 3:4], a=c.colB[:, 2:3],
                 b=c.colB[:, 1:2]: e.tensor_tensor(out=o, in0=a, in1=b, op=ALU.mult),
                 reads=["colB3", "colB2"], writes=["colB4"])
            P.op("scalar", lambda e, o=c.colB[:, 3:4], i=c.colB[:, 3:4]:
                 e.activation(out=o, in_=i, func=AF.Ln),
                 reads=["colB4"], writes=["colB4"])
            P.op("scalar", lambda e, o=c.colB[:, 3:4], i=c.colB[:, 3:4]:
                 e.activation(out=o, in_=i, func=AF.Exp, scale=0.5),
                 reads=["colB4"], writes=["colB4"])
            P.op("vector", lambda e, o=c.colC[:, 8 + sl:9 + sl], i=c.colB[:, 3:4]:
                 e.tensor_scalar_mul(out=o, in0=i, scalar1=-SCALE),
                 reads=["colB4"], writes=[f"negc{sl}"])

        def attend(h, sl, hook0=None, hook1=None, hook2=None):
            dn = 4 + (h % 2)
            db = dn

            def st_mm(kt):
                sb_ = kt % 2
                for ch2 in range(2):
                    kc_col = ch2 * 2048 + kt * 128
                    P.op("tensor", lambda e, o=c.ps[sb_][:, :TG],
                         l=KCB[:, kc_col:kc_col + 128],
                         r=QB2[sl][:, ch2 * TG: ch2 * TG + TG],
                         st=(ch2 == 0), sp=(ch2 == 1): e.matmul(
                             out=o, lhsT=l, rhs=r, start=st, stop=sp),
                         reads=[(KCB, kc_col, kc_col + 128),
                                (QB2[sl], ch2 * TG, ch2 * TG + TG)],
                         writes=[f"ps{sb_}"], group=K + f"st{qg}_{h}_{kt}")
                P.op("scalar", lambda e, o=AB[:, sb_ * TG: sb_ * TG + TG],
                     i=c.ps[sb_][:, :TG], b=c.colC[:, 8 + sl:9 + sl]: e.activation(
                         out=o, in_=i, func=AF.Exp, bias=b, scale=SCALE),
                     reads=[f"ps{sb_}", f"negc{sl}"],
                     writes=[(AB, sb_ * TG, sb_ * TG + TG)])

            def av_mm(kt):
                sb_ = kt % 2
                for vh in range(2):
                    P.op("tensor", lambda e, o=c.ps[2 + vh][:, :TG],
                         l=VCB[:, kt * 256 + vh * 128: kt * 256 + vh * 128 + 128],
                         r=AB[:, sb_ * TG: sb_ * TG + TG],
                         st=(kt == 0), sp=(kt == 15): e.matmul(
                             out=o, lhsT=l, rhs=r, start=st, stop=sp),
                         reads=[(VCB, kt * 256 + vh * 128, kt * 256 + vh * 128 + 128),
                                (AB, sb_ * TG, sb_ * TG + TG)],
                         writes=[f"ps{2 + vh}"], group=K + f"o{qg}_{h}_{vh}")
                P.op("tensor", lambda e, o=c.ps[dn][:, :TG], l=c.ONESB[:, 0:128],
                     r=AB[:, sb_ * TG: sb_ * TG + TG],
                     st=(kt == 0), sp=(kt == 15): e.matmul(
                         out=o, lhsT=l, rhs=r, start=st, stop=sp),
                     reads=[(AB, sb_ * TG, sb_ * TG + TG), "onesc"],
                     writes=[f"ps{dn}"], group=K + f"dn{qg}_{h}")

            st_mm(0)
            st_mm(1)
            if hook0:
                hook0()
            av_mm(0)
            nxt = 2
            done = 1
            while done < 5:
                if nxt < 16:
                    st_mm(nxt)
                    nxt += 1
                av_mm(done)
                done += 1
            if hook1:
                hook1()
            while done < 9:
                st_mm(nxt)
                nxt += 1
                av_mm(done)
                done += 1
            if hook2:
                hook2()
            while done < 16:
                if nxt < 16:
                    st_mm(nxt)
                    nxt += 1
                av_mm(done)
                done += 1

            P.op("vector", lambda e, o=c.drep[:, 0:TG], i=c.ps[dn][:, :TG]:
                 _recip_bf16(e, o, i), reads=[f"ps{dn}"], writes=["row_a"])

            def epilogue():
                for vh in range(2):
                    P.op("scalar", lambda e, o=QB2[sl][:, vh * TG: vh * TG + TG],
                         i=c.ps[2 + vh][:, :TG]: e.activation(out=o, in_=i,
                                                              func=AF.Identity),
                         reads=[f"ps{2 + vh}"], writes=[(QB2[sl], vh * TG, vh * TG + TG)])
                    oc = (2 * h + vh) * 2048 + t0
                    P.op("vector", lambda e, o=OB[:, oc:oc + TG],
                         a=QB2[sl][:, vh * TG: vh * TG + TG],
                         b=c.drep[:, 0:TG]: e.tensor_tensor(out=o, in0=a, in1=b,
                                                            op=ALU.mult),
                         reads=[(QB2[sl], vh * TG, vh * TG + TG), "row_a"],
                         writes=[(OB, oc, oc + TG)])
            return epilogue

        prep_a(0, 0)
        prep_b(0, 0)
        pend = None
        for h in range(4):
            if h + 1 < 4:
                h1, s1 = h + 1, (h + 1) % 2
                pend = attend(h, h % 2, hook0=pend,
                              hook1=(lambda h1=h1, s1=s1: prep_a(h1, s1)),
                              hook2=(lambda h1=h1, s1=s1: prep_b(h1, s1)))
            else:
                pend = attend(h, h % 2, hook0=pend)
        pend()
    for tg in range(NTG):
        t0 = tg * TG
        emit_ln_bcast(P, c, c.row_r, c.row_mr, tg, K + "ln", 5, 6)
        for fch in range(NCH):
            bank = fch % 4
            for ch in range(NCH):
                wc = ch * 1024 + fch * 128
                P.op("tensor", lambda e, o=c.ps[bank][:, :TG],
                     l=c.Wp3[:, wc:wc + 128], r=OB[:, ch * 2048 + t0: ch * 2048 + t0 + TG],
                     st=(ch == 0), sp=(ch == NCH - 1): e.matmul(
                         out=o, lhsT=l, rhs=r, start=st, stop=sp),
                     reads=[(c.Wp3, wc, wc + 128), (OB, ch * 2048 + t0, ch * 2048 + t0 + TG)],
                     writes=[f"ps{bank}"], group=K + f"wo{tg}_{fch}")
            xc = fch * S + t0
            P.op("vector", lambda e, o=c.sc_u[:, :TG], a=X[:, xc:xc + TG],
                 b=c.rb[:, 0:TG]: e.tensor_tensor(out=o, in0=a, in1=b, op=ALU.mult),
                 reads=[XK(X, xc, xc + TG), "rbc_r"], writes=[XK(c.sc_u, 0, TG)])
            P.op("vector", lambda e, o=c.sc_u[:, :TG], a=c.sc_u[:, :TG],
                 b=c.mrb[:, 0:TG]: e.tensor_tensor(out=o, in0=a, in1=b, op=ALU.subtract),
                 reads=[XK(c.sc_u, 0, TG), "rbc_mr"], writes=[XK(c.sc_u, 0, TG)])
            P.op("vector", lambda e, o=c.sc_u[:, :TG], a=c.sc_u[:, :TG],
                 b=c.ps[bank][:, :TG]: e.tensor_tensor(out=o, in0=a, in1=b, op=ALU.add),
                 reads=[XK(c.sc_u, 0, TG), f"ps{bank}"], writes=[XK(c.sc_u, 0, TG)])
            P.op("vector", lambda e, o=X[:, xc:xc + TG], a=X[:, xc:xc + TG],
                 b=c.sc_u[:, :TG]: e.tensor_tensor(out=o, in0=a, in1=b, op=ALU.add),
                 reads=[XK(c.sc_u, 0, TG), XK(X, xc, xc + TG)],
                 writes=[XK(X, xc, xc + TG)])
            P.op("vector", lambda e, o=W1B[:, xc:xc + TG], i=c.sc_u[:, :TG]:
                 e.tensor_copy(out=o, in_=i),
                 reads=[XK(c.sc_u, 0, TG)], writes=[(W1B, xc, xc + TG)])

    for tg in range(NTG):
        t0 = tg * TG
        for ch in range(NCH):
            P.op("scalar", lambda e, o=c.sc_u[:, :TG],
                 i=W1B[:, ch * S + t0: ch * S + t0 + TG]: e.activation(
                     out=o, in_=i, func=AF.Square),
                 reads=[(W1B, ch * S + t0, ch * S + t0 + TG)],
                 writes=[XK(c.sc_u, 0, TG)])
            P.op("tensor", lambda e, o=c.ps[4][0:1, :TG], l=c.ONESF[:, 0:1],
                 r=c.sc_u[:, :TG], st=(ch == 0), sp=(ch == NCH - 1): e.matmul(
                     out=o, lhsT=l, rhs=r, start=st, stop=sp),
                 reads=[XK(c.sc_u, 0, TG), "onesc"], writes=["ps4"],
                 group=K + f"r2m{tg}")
        va = c.row_a[:, :TG]
        P.op("scalar", lambda e, o=va, i=c.ps[4][0:1, :TG], b=c.eps_rms[0:1, 0:1]:
             e.activation(out=o, in_=i, func=AF.Ln, bias=b, scale=1.0 / D),
             reads=["ps4", "consts"], writes=["row_a"])
        P.op("scalar", lambda e, o=va, i=va: e.activation(out=o, in_=i,
                                                          func=AF.Exp, scale=-0.5),
             reads=["row_a"], writes=["row_a"])
        P.op("tensor", lambda e, o=c.ps[5][:, :TG], l=c.ONESF[0:1, :], r=va:
             e.matmul(out=o, lhsT=l, rhs=r, start=True, stop=True),
             reads=["row_a", "ones1f"], writes=["ps5"])
        for ch in range(NCH):
            P.op("vector", lambda e, o=NX2[:, ch * S + t0: ch * S + t0 + TG],
                 i=W1B[:, ch * S + t0: ch * S + t0 + TG], s=c.colA[:, 8 + ch:9 + ch],
                 b=c.ps[5][:, :TG]: e.scalar_tensor_tensor(
                     out=o, in0=i, scalar=s, in1=b, op0=ALU.mult, op1=ALU.mult),
                 reads=[(W1B, ch * S + t0, ch * S + t0 + TG), "colA", "ps5"],
                 writes=[(NX2, ch * S + t0, ch * S + t0 + TG)])

    P.dma("gpsimd", c.xn[:, 0:64].rearrange("p (c f) -> p c f", c=8),
          c.d_f_wrT.rearrange("(c p) f -> p c f", p=128),
          writes=[XK(c.xn, 0, 64)])
    for tt in range(16):
        q0 = tt * 128
        lg = c.ps[5][:, tt * 8: tt * 8 + 8]
        for ch in range(NCH):
            P.op("tensor", lambda e, o=lg,
                 l=NX2[:, ch * S + q0: ch * S + q0 + 128],
                 r=c.xn[:, ch * 8: ch * 8 + 8], st=(ch == 0): e.matmul(
                     out=o, lhsT=l, rhs=r, start=st, stop=False),
                 reads=[(NX2, ch * S + q0, ch * S + q0 + 128), XK(c.xn, ch * 8, ch * 8 + 8)],
                 writes=["ps5"], group=K + f"rt{tt}")
        P.op("tensor", lambda e, o=lg, l=c.ONESB[0:1, 0:128],
             r=c.eb_row[:, 0:8]: e.matmul(out=o, lhsT=l, rhs=r, start=False,
                                          stop=True),
             reads=["eb_row", "ones1b"], writes=["ps5"], group=K + f"rt{tt}")
    for tt in range(16):
        q0 = tt * 128
        lg = c.ps[5][:, tt * 8: tt * 8 + 8]
        P.op("vector", lambda e, o=c.colB[:, 6:7], i=lg:
             e.reduce_max(out=o, in_=i, axis=AX.X, negate=True),
             reads=["ps5"], writes=["colB7"])
        P.op("scalar", lambda e, o=c.sc_u[:, 0:8], i=lg, b=c.colB[:, 6:7]:
             e.activation(out=o, in_=i, func=AF.Exp, bias=b),
             reads=["ps5", "colB7"], writes=[XK(c.sc_u, 0, 8)])
        P.op("vector", lambda e, o=c.colB[:, 7:8], i=c.sc_u[:, 0:8]:
             e.reduce_sum(out=o, in_=i, axis=AX.X),
             reads=[XK(c.sc_u, 0, 8)], writes=["colB8"])
        P.op("vector", lambda e, o=c.colB[:, 8:9], i=c.colB[:, 7:8]:
             e.reciprocal(out=o, in_=i), reads=["colB8"], writes=["colB9"])
        P.op("vector", lambda e, o=c.sc_u[:, 8:16], i=c.sc_u[:, 0:8],
             s=c.colB[:, 8:9]: e.tensor_scalar(out=o, in0=i, scalar1=s, scalar2=None,
                                               op0=ALU.mult),
             reads=[XK(c.sc_u, 0, 8), "colB9"], writes=[XK(c.sc_u, 8, 16)])
        P.op("vector", lambda e, o=c.colB[:, 9:10], i=c.sc_u[:, 8:16]:
             e.reduce_max(out=o, in_=i, axis=AX.X),
             reads=[XK(c.sc_u, 8, 16)], writes=["colB10"])
        P.op("vector", lambda e, o=c.sc_u[:, 16:24], i=c.sc_u[:, 8:16],
             s=c.colB[:, 9:10]: e.tensor_scalar(out=o, in0=i, scalar1=s, scalar2=None,
                                                op0=ALU.is_equal),
             reads=[XK(c.sc_u, 8, 16), "colB10"], writes=[XK(c.sc_u, 16, 24)])
        P.op("vector", lambda e, o=c.sc_u[:, 24:32], i=c.sc_u[:, 16:24],
             b=c.sc_u[:, 8:16]: e.scalar_tensor_tensor(
                 out=o, in0=i, scalar=-1e9, in1=b, op0=ALU.mult, op1=ALU.add),
             reads=[XK(c.sc_u, 16, 24), XK(c.sc_u, 8, 16)], writes=[XK(c.sc_u, 24, 32)])
        P.op("vector", lambda e, o=c.colB[:, 10:11], i=c.sc_u[:, 24:32]:
             e.reduce_max(out=o, in_=i, axis=AX.X),
             reads=[XK(c.sc_u, 24, 32)], writes=["colB11"])
        P.op("vector", lambda e, o=c.sc_u[:, 32:40], i=c.sc_u[:, 24:32],
             s=c.colB[:, 10:11]: e.tensor_scalar(out=o, in0=i, scalar1=s, scalar2=None,
                                                 op0=ALU.is_equal),
             reads=[XK(c.sc_u, 24, 32), "colB11"], writes=[XK(c.sc_u, 32, 40)])
        P.op("vector", lambda e, o=c.colB[:, 11:12], a=c.colB[:, 10:11],
             b=c.colB[:, 9:10]: e.tensor_tensor(out=o, in0=a, in1=b, op=ALU.subtract),
             reads=["colB11", "colB10"], writes=["colB12"])
        P.op("scalar", lambda e, o=c.colB[:, 12:13], i=c.colB[:, 11:12]:
             e.activation(out=o, in_=i, func=AF.Exp),
             reads=["colB12"], writes=["colB13"])
        P.op("vector", lambda e, o=c.colB[:, 13:14], i=c.colB[:, 12:13]:
             e.tensor_scalar_add(out=o, in0=i, scalar1=1.0),
             reads=["colB13"], writes=["colB14"])
        P.op("vector", lambda e, o=c.colB[:, 13:14], i=c.colB[:, 13:14]:
             e.reciprocal(out=o, in_=i), reads=["colB14"], writes=["colB14"])
        P.op("vector", lambda e, o=c.colB[:, 14:15], a=c.colB[:, 12:13],
             b=c.colB[:, 13:14]: e.tensor_tensor(out=o, in0=a, in1=b, op=ALU.mult),
             reads=["colB13", "colB14"], writes=["colB15"])
        P.op("vector", lambda e, o=c.sc_u[:, 40:48], i=c.sc_u[:, 32:40],
             s=c.colB[:, 14:15]: e.tensor_scalar(out=o, in0=i, scalar1=s, scalar2=None,
                                                 op0=ALU.mult),
             reads=[XK(c.sc_u, 32, 40), "colB15"], writes=[XK(c.sc_u, 40, 48)])
        P.op("vector", lambda e, o=c.xn[:, 64 + tt * 8: 72 + tt * 8], i=c.sc_u[:, 16:24],
             s=c.colB[:, 13:14], b=c.sc_u[:, 40:48]: e.scalar_tensor_tensor(
                 out=o, in0=i, scalar=s, in1=b, op0=ALU.mult, op1=ALU.add),
             reads=[XK(c.sc_u, 16, 24), "colB14", XK(c.sc_u, 40, 48)],
             writes=[XK(c.xn, 64 + tt * 8, 72 + tt * 8)])


    P.dma("gpsimd", CE8[:, :], c.d_c_e8, writes=[(CE8, 0, 1024)])

    HST = Rv(c.B1, 0, 8192)

    def ffn_expert(tag, wsl1, wsl2, wsl3, use_comb, eidx):
        for tp in range(2):
            t0s = (tp * 2 * TG, (tp * 2 + 1) * TG)
            if use_comb:
                for half in range(2):
                    P.op("tensor", lambda e, o=c.ps[4 + half][:, :TG],
                         l=CE8[:, eidx * 128: eidx * 128 + 128],
                         r=COMB[:, t0s[half]:t0s[half] + TG]: e.matmul(
                             out=o, lhsT=l, rhs=r, start=True, stop=True),
                         reads=[(CE8, eidx * 128, eidx * 128 + 128),
                                (COMB, t0s[half], t0s[half] + TG)],
                         writes=[f"ps{4 + half}"])
                    P.op("scalar", lambda e, o=CBT[:, half * TG: half * TG + TG],
                         i=c.ps[4 + half][:, :TG]: e.activation(out=o, in_=i,
                                                                func=AF.Identity),
                         reads=[f"ps{4 + half}"],
                         writes=[(CBT, half * TG, half * TG + TG)])
            for fch in range(NCH):
                for ch in range(NCH):
                    wc = ch * 1024 + fch * 128
                    for half in range(2):
                        P.op("tensor", lambda e, o=c.ps[half][:, :TG],
                             l=wsl1[:, wc:wc + 128],
                             r=NX2[:, ch * S + t0s[half]: ch * S + t0s[half] + TG],
                             st=(ch == 0), sp=(ch == NCH - 1): e.matmul(
                                 out=o, lhsT=l, rhs=r, start=st, stop=sp),
                             reads=[(wsl1, wc, wc + 128),
                                    (NX2, ch * S + t0s[half], ch * S + t0s[half] + TG)],
                             writes=[f"ps{half}"], group=tag + f"g1_{tp}_{fch}_{half}")
                for half in range(2):
                    P.op("scalar", lambda e, o=HST[:, fch * 1024 + half * TG: fch * 1024 + half * TG + TG],
                         i=c.ps[half][:, :TG]: e.activation(out=o, in_=i, func=AF.Silu),
                         reads=[f"ps{half}"],
                         writes=[(HST, fch * 1024 + half * TG, fch * 1024 + half * TG + TG)])
                for ch in range(NCH):
                    wc = ch * 1024 + fch * 128
                    for half in range(2):
                        P.op("tensor", lambda e, o=c.ps[2 + half][:, :TG],
                             l=wsl3[:, wc:wc + 128],
                             r=NX2[:, ch * S + t0s[half]: ch * S + t0s[half] + TG],
                             st=(ch == 0), sp=(ch == NCH - 1): e.matmul(
                                 out=o, lhsT=l, rhs=r, start=st, stop=sp),
                             reads=[(wsl3, wc, wc + 128),
                                    (NX2, ch * S + t0s[half], ch * S + t0s[half] + TG)],
                             writes=[f"ps{2 + half}"], group=tag + f"g3_{tp}_{fch}_{half}")
                for half in range(2):
                    hc_ = fch * 1024 + half * TG
                    P.op("vector", lambda e, o=HST[:, hc_:hc_ + TG], a=HST[:, hc_:hc_ + TG],
                         b=c.ps[2 + half][:, :TG]: e.tensor_tensor(out=o, in0=a, in1=b,
                                                                   op=ALU.mult),
                         reads=[(HST, hc_, hc_ + TG), f"ps{2 + half}"],
                         writes=[(HST, hc_, hc_ + TG)])
                    if use_comb:
                        P.op("vector", lambda e, o=HST[:, hc_:hc_ + TG],
                             a=HST[:, hc_:hc_ + TG], b=CBT[:, half * TG: half * TG + TG]:
                             e.tensor_tensor(out=o, in0=a, in1=b, op=ALU.mult),
                             reads=[(HST, hc_, hc_ + TG), (CBT, half * TG, half * TG + TG)],
                             writes=[(HST, hc_, hc_ + TG)])
            bankrot = ((0, 1), (2, 3), (4, 5))
            for dch in range(NCH):
                bk = bankrot[dch % 3]
                for fch in range(NCH):
                    wc = fch * 1024 + dch * 128
                    for half in range(2):
                        P.op("tensor", lambda e, o=c.ps[bk[half]][:, :TG],
                             l=wsl2[:, wc:wc + 128],
                             r=HST[:, fch * 1024 + half * TG: fch * 1024 + half * TG + TG],
                             st=(fch == 0), sp=(fch == NCH - 1): e.matmul(
                                 out=o, lhsT=l, rhs=r, start=st, stop=sp),
                             reads=[(wsl2, wc, wc + 128),
                                    (HST, fch * 1024 + half * TG, fch * 1024 + half * TG + TG)],
                             writes=[f"ps{bk[half]}"], group=tag + f"o_{tp}_{dch}_{half}")
                for half in range(2):
                    xc = dch * S + t0s[half]
                    P.op("vector", lambda e, o=X[:, xc:xc + TG], a=X[:, xc:xc + TG],
                         b=c.ps[bk[half]][:, :TG]: e.tensor_tensor(out=o, in0=a, in1=b,
                                                                   op=ALU.add),
                         reads=[f"ps{bk[half]}", XK(X, xc, xc + TG)],
                         writes=[XK(X, xc, xc + TG)])

    w1s, w3s, w2s = Rv(c.Wp1, 0, 8192), Rv(c.Wp2, 0, 8192), Rv(c.Wp3, 0, 8192)
    emit_wload(P, c, w1s, c.d_f_sw1T, 1024, 1024, K + "sw1", wid="fsw1")
    emit_wload(P, c, w3s, c.d_f_sw3T, 1024, 1024, K + "sw3", wid="fsw3")
    emit_wload(P, c, w2s, c.d_f_sw2T, 1024, 1024, K + "sw2", wid="fsw2")
    ffn_expert(K + "sh", w1s, w2s, w3s, False, 0)
    for tt in range(16):
        q0 = tt * 128
        P.op("tensor", lambda e, o=c.psb[0:8, 0:128], i=c.xn[:, 64 + tt * 8: 72 + tt * 8],
             d=c.identb[:, :]: e.transpose(out=o, in_=i, identity=d),
             reads=[XK(c.xn, 64 + tt * 8, 72 + tt * 8), (c.identb, 0, 128)],
             writes=["psb"])
        P.op("vector", lambda e, o=COMB[:, q0:q0 + 128], i=c.psb[0:8, 0:128]:
             e.tensor_copy(out=o, in_=i), reads=["psb"],
             writes=[(COMB, q0, q0 + 128)])
    for e8 in range(8):
        emit_wload(P, c, w1s, c.d_f_ew1T[e8], 1024, 1024, K + f"e{e8}w1",
                   wid=f"fe{e8}w1")
        emit_wload(P, c, w3s, c.d_f_ew3T[e8], 1024, 1024, K + f"e{e8}w3",
                   wid=f"fe{e8}w3")
        emit_wload(P, c, w2s, c.d_f_ew2T[e8], 1024, 1024, K + f"e{e8}w2",
                   wid=f"fe{e8}w2")
        ffn_expert(K + f"ex{e8}", w1s, w2s, w3s, True, e8)


def emit_l0_attn_a(P, c, X):
    K = "aa"
    SC = 0.125
    c.m_resident = False
    emit_ln_rows(P, c, X, 0, c.row_m, c.row_r, c.row_mr, K + "ln")
    KW = Rv(c.B1, 0, 8192)
    VW = Rv(c.B1, 8192, 8192)
    OTG = Rv(c.B2, 0, 4096)
    BND = Rv(c.B2, 4096, 1408)
    AB2 = Rv(c.B2, 5504, 1536)
    QQ = Rv(c.B2, 7040, 512)
    WD = Rv(c.B2, 7552, 8192)
    emit_wload(P, c, Rv(c.Wp1, 0, 8192), c.d_aa_wq, 1024, 1024, K + "wq")
    emit_wload(P, c, Rv(c.Wp2, 0, 8192), c.d_aa_wk, 1024, 1024, K + "wk")
    emit_wload(P, c, Rv(c.Wp3, 0, 8192), c.d_aa_wv, 1024, 1024, K + "wv")
    emit_wload(P, c, WD, c.d_aa_wd, 1024, 1024, K + "wd")
    P.dma("gpsimd", BND[:, :], c.d_c_band[:, 128:1536], writes=[(BND, 0, 1408)])
    P.dma("sync", c.colA[:, 0:8], c.d_aa_bq[:, 0:8], writes=["colA"])
    P.dma("sync", c.colA[:, 8:16], c.d_aa_bk[:, 0:8], writes=["colA"])
    P.dma("sync", c.colB[:, 0:8], c.d_aa_bd[:, 0:8], writes=["colBb"])

    def build_kv(j0, n, first):
        av = j0 * 128
        wdt = n * 128
        emit_ln_bcast_w(P, c, c.row_r, c.row_mr, av, wdt, K + "ln", 5, 6)
        for ch in range(NCH):
            emit_xn_tile(P, c, X, ch, av, wdt, c.xn, ch * 512, 5, 6)
        for fch in range(NCH):
            bank = fch % 2
            for ch in range(NCH):
                wc = ch * 1024 + fch * 128
                P.op("tensor", lambda e, o=c.ps[bank][:, :wdt],
                     l=c.Wp2[:, wc:wc + 128], r=c.xn[:, ch * 512: ch * 512 + wdt],
                     st=(ch == 0), sp=(ch == NCH - 1): e.matmul(
                         out=o, lhsT=l, rhs=r, start=st, stop=sp),
                     reads=[(c.Wp2, wc, wc + 128), XK(c.xn, ch * 512, ch * 512 + wdt)],
                     writes=[f"ps{bank}"], group=K + f"k{j0}_{fch}")
            for ti in range(n):
                sl = (j0 + ti) % 8
                P.op("scalar", lambda e,
                     o=KW[:, fch * 1024 + sl * 128: fch * 1024 + sl * 128 + 128],
                     i=c.ps[bank][:, ti * 128: ti * 128 + 128],
                     b=c.colA[:, 8 + fch:9 + fch]: e.activation(
                         out=o, in_=i, func=AF.Identity, bias=b),
                     reads=[f"ps{bank}", "colA"],
                     writes=[(KW, fch * 1024 + sl * 128, fch * 1024 + sl * 128 + 128)])
        for ch in range(NCH):
            for ti in range(n):
                sl = (j0 + ti) % 8
                P.op("scalar", lambda e, o=c.sc_u[:, ti * 128: ti * 128 + 128],
                     i=KW[:, ch * 1024 + sl * 128: ch * 1024 + sl * 128 + 128]:
                     e.activation(out=o, in_=i, func=AF.Square),
                     reads=[(KW, ch * 1024 + sl * 128, ch * 1024 + sl * 128 + 128)],
                     writes=[XK(c.sc_u, ti * 128, ti * 128 + 128)])
            P.op("tensor", lambda e, o=c.ps[4][:, :wdt], l=c.ONESF[:, 0:128],
                 r=c.sc_u[:, :wdt], st=(ch == 0), sp=(ch == NCH - 1): e.matmul(
                     out=o, lhsT=l, rhs=r, start=st, stop=sp),
                 reads=[XK(c.sc_u, 0, wdt), "onesc"], writes=["ps4"],
                 group=K + f"kn{j0}")
        P.op("vector", lambda e, o=c.colC[:, 0:1], i=c.ps[4][:, :wdt]:
             e.reduce_max(out=o, in_=i, axis=AX.X), reads=["ps4"], writes=["cC0"])
        if first:
            P.op("vector", lambda e, o=c.colC[:, 1:2], i=c.colC[:, 0:1]:
                 e.tensor_copy(out=o, in_=i), reads=["cC0"], writes=["cC1"])
        else:
            P.op("vector", lambda e, o=c.colC[:, 1:2], a=c.colC[:, 0:1],
                 b=c.colC[:, 1:2]: e.tensor_tensor(out=o, in0=a, in1=b, op=ALU.max),
                 reads=["cC0", "cC1"], writes=["cC1"])
        for ti in range(n):
            sl = (j0 + ti) % 8
            for half in range(2):
                bank = 2 + half
                for ch in range(NCH):
                    P.op("tensor", lambda e, o=c.ps[bank][:, :512],
                         l=c.xn[:, ch * 512 + ti * 128: ch * 512 + ti * 128 + 128],
                         r=c.Wp3[:, ch * 1024 + half * 512: ch * 1024 + half * 512 + 512],
                         st=(ch == 0): e.matmul(out=o, lhsT=l, rhs=r,
                                                start=st, stop=False),
                         reads=[(c.Wp3, ch * 1024 + half * 512, ch * 1024 + half * 512 + 512),
                                XK(c.xn, ch * 512 + ti * 128, ch * 512 + ti * 128 + 128)],
                         writes=[f"ps{bank}"], group=K + f"v{j0}_{ti}_{half}")
                P.op("tensor", lambda e, o=c.ps[bank][:, :512], l=c.ONESB[64:65, 0:128],
                     r=c.bv_row[:, half * 512: half * 512 + 512]: e.matmul(
                         out=o, lhsT=l, rhs=r, start=False, stop=True),
                     reads=["bv_row", "ones1b"], writes=[f"ps{bank}"],
                     group=K + f"v{j0}_{ti}_{half}")
                P.op("scalar", lambda e,
                     o=VW[:, sl * 1024 + half * 512: sl * 1024 + half * 512 + 512],
                     i=c.ps[bank][:, :512]: e.activation(out=o, in_=i, func=AF.Identity),
                     reads=[f"ps{bank}"],
                     writes=[(VW, sl * 1024 + half * 512, sl * 1024 + half * 512 + 512)])

    build_kv(0, 4, True)
    build_kv(4, 2, False)
    for qg in range(NTG):
        t0 = qg * TG
        emit_ln_bcast_w(P, c, c.row_r, c.row_mr, t0, TG, K + "ln", 5, 6)
        for ch in range(NCH):
            emit_xn_tile(P, c, X, ch, t0, TG, c.xn, ch * TG, 5, 6)
        QQ2 = [QQ, Rv(c.B2, 15744, 512)]

        def prep_a(h, sl):
            base = 64 * (h % 2)
            fc = h // 2
            for ch in range(NCH):
                wc = ch * 1024 + h * 64
                P.op("tensor", lambda e, o=c.ps[6][base:base + 64, :TG],
                     l=c.Wp1[:, wc:wc + 64], r=c.xn[:, ch * TG: ch * TG + TG],
                     st=(ch == 0), sp=(ch == NCH - 1): e.matmul(
                         out=o, lhsT=l, rhs=r, start=st, stop=sp),
                     reads=[(c.Wp1, wc, wc + 64), XK(c.xn, ch * TG, ch * TG + TG)],
                     writes=["ps6"], group=K + f"q{qg}_{h}")
            P.op("scalar", lambda e, o=QQ2[sl][base:base + 64, 0:TG],
                 i=c.ps[6][base:base + 64, :TG],
                 b=c.colA[base:base + 64, fc:fc + 1]: e.activation(
                     out=o, in_=i, func=AF.Identity, bias=b),
                 reads=["ps6", "colA"], writes=[(QQ2[sl], 0, TG)])
            P.op("scalar", lambda e, o=c.sc_u[base:base + 64, :TG],
                 i=QQ2[sl][base:base + 64, 0:TG]: e.activation(out=o, in_=i,
                                                              func=AF.Square),
                 reads=[(QQ2[sl], 0, TG)], writes=[XK(c.sc_u, 0, TG)])

        def prep_m(h, sl):
            base = 64 * (h % 2)
            P.op("tensor", lambda e, o=c.ps[6][:, :TG],
                 l=c.ONESF[base:base + 64, 0:128],
                 r=c.sc_u[base:base + 64, :TG]: e.matmul(out=o, lhsT=l, rhs=r,
                                                         start=True, stop=True),
                 reads=[XK(c.sc_u, 0, TG), "ones1f"], writes=["ps6"])

        def prep_b(h, sl):
            P.op("vector", lambda e, o=c.colC[:, 2:3], i=c.ps[6][:, :TG]:
                 e.reduce_max(out=o, in_=i, axis=AX.X), reads=["ps6"], writes=["cC2"])
            P.op("vector", lambda e, o=c.colC[:, 3:4], a=c.colC[:, 2:3],
                 b=c.colC[:, 1:2]: e.tensor_tensor(out=o, in0=a, in1=b, op=ALU.mult),
                 reads=["cC2", "cC1"], writes=["cC3"])
            P.op("scalar", lambda e, o=c.colC[:, 3:4], i=c.colC[:, 3:4]:
                 e.activation(out=o, in_=i, func=AF.Ln), reads=["cC3"], writes=["cC3"])
            P.op("scalar", lambda e, o=c.colC[:, 3:4], i=c.colC[:, 3:4]:
                 e.activation(out=o, in_=i, func=AF.Exp, scale=0.5),
                 reads=["cC3"], writes=["cC3"])
            P.op("vector", lambda e, o=c.colC[:, 8 + sl:9 + sl], i=c.colC[:, 3:4]:
                 e.tensor_scalar_mul(out=o, in0=i, scalar1=-SC),
                 reads=["cC3"], writes=[f"negc{sl}"])

        def attend(h, sl, hook0=None, hook1=None, hook2=None, hook3=None):
            base = 64 * (h % 2)
            fc = h // 2
            ob = 2 + (h % 2)
            dn = 4 + (h % 2)
            db = dn
            valid = [j for j in range(4 * qg - 2, 4 * qg + 6) if 0 <= j < 16]
            nv = len(valid)

            def st_mm(ij):
                j = valid[ij]
                if ij % 3 == 2:
                    sb_ = dn if ij == 2 else 4 + (h + 1) % 2
                else:
                    sb_ = ij % 3
                ac = (ij % 3) * TG
                sl_ = j % 8
                P.op("tensor", lambda e, o=c.ps[sb_][:, :TG],
                     l=KW[base:base + 64, fc * 1024 + sl_ * 128: fc * 1024 + sl_ * 128 + 128],
                     r=QQ2[sl][base:base + 64, 0:TG]: e.matmul(out=o, lhsT=l, rhs=r,
                                                              start=True, stop=True),
                     reads=[(KW, fc * 1024 + sl_ * 128, fc * 1024 + sl_ * 128 + 128),
                            (QQ2[sl], 0, TG)],
                     writes=[f"ps{sb_}"], group=K + f"st{qg}_{h}_{j}")
                P.op("scalar", lambda e, o=AB2[:, ac: ac + TG],
                     i=c.ps[sb_][:, :TG], b=c.colC[:, 8 + sl:9 + sl]: e.activation(
                         out=o, in_=i, func=AF.Exp, bias=b, scale=SC),
                     reads=[f"ps{sb_}", f"negc{sl}"], writes=[(AB2, ac, ac + TG)])
                moff = 640 - 128 * (j - 4 * qg)
                P.op("vector", lambda e, o=AB2[:, ac: ac + TG],
                     a=AB2[:, ac: ac + TG], b=BND[:, moff:moff + 512]:
                     e.tensor_tensor(out=o, in0=a, in1=b, op=ALU.mult),
                     reads=[(AB2, ac, ac + TG), (BND, moff, moff + 512)],
                     writes=[(AB2, ac, ac + TG)])

            def av_mm(ij):
                j = valid[ij]
                ac = (ij % 3) * TG
                sl_ = j % 8
                P.op("tensor", lambda e, o=c.ps[ob][0:64, :TG],
                     l=VW[:, sl_ * 1024 + h * 64: sl_ * 1024 + h * 64 + 64],
                     r=AB2[:, ac: ac + TG],
                     st=(ij == 0), sp=(ij == nv - 1): e.matmul(
                         out=o, lhsT=l, rhs=r, start=st, stop=sp),
                     reads=[(VW, sl_ * 1024 + h * 64, sl_ * 1024 + h * 64 + 64),
                            (AB2, ac, ac + TG)],
                     writes=[f"ps{ob}"], group=K + f"o{qg}_{h}")
                P.op("tensor", lambda e, o=c.ps[dn][0:64, :TG], l=c.ONESB[:, 0:64],
                     r=AB2[:, ac: ac + TG],
                     st=(ij == 0), sp=(ij == nv - 1): e.matmul(
                         out=o, lhsT=l, rhs=r, start=st, stop=sp),
                     reads=[(AB2, ac, ac + TG), "onesc"],
                     writes=[f"ps{dn}"], group=K + f"dn{qg}_{h}")

            st_mm(0)
            if nv > 1:
                st_mm(1)
            if hook0:
                hook0()
            if nv > 2:
                st_mm(2)
            av_mm(0)
            nxt = 3
            done = 1
            while done < min(3, nv):
                if nxt < nv:
                    st_mm(nxt)
                    nxt += 1
                av_mm(done)
                done += 1
            if hook1:
                hook1()
            while done < min(6, nv):
                if nxt < nv:
                    st_mm(nxt)
                    nxt += 1
                av_mm(done)
                done += 1
            if hook2:
                hook2()
            while done < max(0, nv - 2):
                if nxt < nv:
                    st_mm(nxt)
                    nxt += 1
                av_mm(done)
                done += 1
            if hook3:
                hook3()
            while done < nv:
                if nxt < nv:
                    st_mm(nxt)
                    nxt += 1
                av_mm(done)
                done += 1

            def epilogue():
                P.op("vector", lambda e, o=c.drep[0:64, 0:TG], i=c.ps[dn][0:64, :TG]:
                     _recip_bf16(e, o, i), reads=[f"ps{dn}"], writes=["row_a"])
                P.op("scalar", lambda e, o=QQ2[sl][0:64, 0:TG],
                     i=c.ps[ob][0:64, :TG]: e.activation(out=o, in_=i, func=AF.Identity),
                     reads=[f"ps{ob}"], writes=[(QQ2[sl], 0, TG)])
                P.op("vector", lambda e, o=OTG[base:base + 64, fc * 512: fc * 512 + TG],
                     a=QQ2[sl][0:64, 0:TG], b=c.drep[0:64, 0:TG]:
                     e.tensor_tensor(out=o, in0=a, in1=b, op=ALU.mult),
                     reads=[(QQ2[sl], 0, TG), "row_a"],
                     writes=[(OTG, fc * 512, fc * 512 + TG)])
            return epilogue

        prep_a(0, 0)
        prep_m(0, 0)
        prep_b(0, 0)
        pend = None
        for h in range(16):
            if h + 1 < 16:
                h1 = h + 1
                s1 = h1 % 2
                pend = attend(h, h % 2, hook0=pend,
                              hook1=(lambda h1=h1, s1=s1: prep_a(h1, s1)),
                              hook2=(lambda h1=h1, s1=s1: prep_m(h1, s1)),
                              hook3=(lambda h1=h1, s1=s1: prep_b(h1, s1)))
            else:
                pend = attend(h, h % 2, hook0=pend)
        if pend is not None:
            pend()
        j0n = 4 * qg + 6
        nn = min(16, j0n + 4) - j0n
        if nn > 0:
            build_kv(j0n, nn, False)
        for fch in range(NCH):
            bank = fch % 4
            for ch in range(NCH):
                wc = ch * 1024 + fch * 128
                P.op("tensor", lambda e, o=c.ps[bank][:, :TG],
                     l=WD[:, wc:wc + 128], r=OTG[:, ch * 512: ch * 512 + TG],
                     st=(ch == 0), sp=(ch == NCH - 1): e.matmul(
                         out=o, lhsT=l, rhs=r, start=st, stop=sp),
                     reads=[(WD, wc, wc + 128), (OTG, ch * 512, ch * 512 + TG)],
                     writes=[f"ps{bank}"], group=K + f"wd{qg}_{fch}")
            xc = fch * S + t0
            P.op("vector", lambda e, o=X[:, xc:xc + TG], i=c.ps[bank][:, :TG],
                 s=c.colB[:, fch:fch + 1], x=X[:, xc:xc + TG]: e.scalar_tensor_tensor(
                     out=o, in0=i, scalar=s, in1=x, op0=ALU.add, op1=ALU.add),
                 reads=[f"ps{bank}", "colBb", XK(X, xc, xc + TG)],
                 writes=[XK(X, xc, xc + TG)])


def emit_l0_attn_b(P, c, X):
    K = "ab"
    SC = 0.125
    c.m_resident = False
    emit_ln_rows(P, c, X, 0, c.row_m, c.row_r, c.row_mr, K + "ln")
    KB = Rv(c.B1, 0, 4096)
    VB = Rv(c.B1, 4096, 4160)
    QQ = Rv(c.B1, 8256, 512)
    AB = Rv(c.B1, 9280, 1536)
    OB = Rv(c.B2, 0, 16384)
    emit_wload(P, c, Rv(c.Wp1, 0, 8192), c.d_ab_wq, 1024, 1024, K + "wq")
    emit_wload(P, c, Rv(c.Wp2, 0, 2048), c.d_ab_wk, 1024, 256, K + "wk")
    emit_wload(P, c, Rv(c.Wp2, 2048, 2048), c.d_ab_wv, 1024, 256, K + "wv")
    emit_wload(P, c, Rv(c.Wp3, 0, 8192), c.d_ab_wpost, 1024, 1024, K + "wp")

    for tg in range(NTG):
        t0 = tg * TG
        emit_ln_bcast_w(P, c, c.row_r, c.row_mr, t0, TG, K + "ln", 5, 6)
        for ch in range(NCH):
            emit_xn_tile(P, c, X, ch, t0, TG, c.xn, ch * TG, 5, 6)
        for f2 in range(2):
            bank = f2
            for ch in range(NCH):
                wc = ch * 256 + f2 * 128
                P.op("tensor", lambda e, o=c.ps[bank][:, :TG],
                     l=c.Wp2[:, wc:wc + 128], r=c.xn[:, ch * TG: ch * TG + TG],
                     st=(ch == 0), sp=(ch == NCH - 1): e.matmul(
                         out=o, lhsT=l, rhs=r, start=st, stop=sp),
                     reads=[(c.Wp2, wc, wc + 128), XK(c.xn, ch * TG, ch * TG + TG)],
                     writes=[f"ps{bank}"], group=K + f"k{tg}_{f2}")
            P.op("scalar", lambda e, o=KB[:, f2 * 2048 + t0: f2 * 2048 + t0 + TG],
                 i=c.ps[bank][:, :TG]: e.activation(out=o, in_=i, func=AF.Identity),
                 reads=[f"ps{bank}"],
                 writes=[(KB, f2 * 2048 + t0, f2 * 2048 + t0 + TG)])
        P.op("gpsimd", lambda e, o=VB[:, tg * 1040: tg * 1040 + 1040]:
             e.memset(o, 1.0), writes=[(VB, tg * 1040, tg * 1040 + 1040)])
        for ti in range(4):
            tt = tg * 4 + ti
            bank = 2 + ti % 2
            for ch in range(NCH):
                P.op("tensor", lambda e, o=c.ps[bank][:, :256],
                     l=c.xn[:, ch * TG + ti * 128: ch * TG + ti * 128 + 128],
                     r=c.Wp2[:, 2048 + ch * 256: 2048 + ch * 256 + 256],
                     st=(ch == 0), sp=(ch == NCH - 1): e.matmul(
                         out=o, lhsT=l, rhs=r, start=st, stop=sp),
                     reads=[(c.Wp2, 2048 + ch * 256, 2048 + ch * 256 + 256),
                            XK(c.xn, ch * TG + ti * 128, ch * TG + ti * 128 + 128)],
                     writes=[f"ps{bank}"], group=K + f"v{tt}")
            for g_ in range(4):
                P.op("scalar", lambda e, o=VB[:, tt * 260 + g_ * 65: tt * 260 + g_ * 65 + 64],
                     i=c.ps[bank][:, g_ * 64: g_ * 64 + 64]: e.activation(
                         out=o, in_=i, func=AF.Identity),
                     reads=[f"ps{bank}"],
                     writes=[(VB, tt * 260 + g_ * 65, tt * 260 + g_ * 65 + 64)])
        for f2 in range(2):
            P.op("scalar", lambda e, o=c.sc_u[:, :TG],
                 i=KB[:, f2 * 2048 + t0: f2 * 2048 + t0 + TG]: e.activation(
                     out=o, in_=i, func=AF.Square),
                 reads=[(KB, f2 * 2048 + t0, f2 * 2048 + t0 + TG)],
                 writes=[XK(c.sc_u, 0, TG)])
            P.op("tensor", lambda e, o=c.ps[4][:, :TG], l=c.ONESF[:, 0:128],
                 r=c.sc_u[:, :TG], st=(f2 == 0), sp=(f2 == 1): e.matmul(
                     out=o, lhsT=l, rhs=r, start=st, stop=sp),
                 reads=[XK(c.sc_u, 0, TG), "onesc"], writes=["ps4"],
                 group=K + f"kn{tg}")
        P.op("vector", lambda e, o=c.colC[:, 0:1], i=c.ps[4][:, :TG]:
             e.reduce_max(out=o, in_=i, axis=AX.X), reads=["ps4"], writes=["cC0"])
        if tg == 0:
            P.op("vector", lambda e, o=c.colC[:, 1:2], i=c.colC[:, 0:1]:
                 e.tensor_copy(out=o, in_=i), reads=["cC0"], writes=["cC1"])
        else:
            P.op("vector", lambda e, o=c.colC[:, 1:2], a=c.colC[:, 0:1],
                 b=c.colC[:, 1:2]: e.tensor_tensor(out=o, in0=a, in1=b, op=ALU.max),
                 reads=["cC0", "cC1"], writes=["cC1"])

    for qg in range(NTG):
        t0 = qg * TG
        emit_ln_bcast_w(P, c, c.row_r, c.row_mr, t0, TG, K + "ln", 5, 6)
        for ch in range(NCH):
            emit_xn_tile(P, c, X, ch, t0, TG, c.xn, ch * TG, 5, 6)
        QQ2 = [QQ, Rv(c.B1, 8768, 512)]

        def prep_a(g, wv_, sl):
            base = 64 * (g % 2)
            hw = g * 4 + wv_
            for ch in range(NCH):
                wc = ch * 1024 + g * 256 + wv_ * 64
                P.op("tensor", lambda e, o=c.ps[6][base:base + 64, :TG],
                     l=c.Wp1[:, wc:wc + 64], r=c.xn[:, ch * TG: ch * TG + TG],
                     st=(ch == 0), sp=(ch == NCH - 1): e.matmul(
                         out=o, lhsT=l, rhs=r, start=st, stop=sp),
                     reads=[(c.Wp1, wc, wc + 64), XK(c.xn, ch * TG, ch * TG + TG)],
                     writes=["ps6"], group=K + f"q{qg}_{hw}")
            P.op("vector", lambda e, o=QQ2[sl][base:base + 64, 0:TG],
                 i=c.ps[6][base:base + 64, :TG]: _copy_bf16(e, o, i),
                 reads=["ps6"], writes=[(QQ2[sl], 0, TG)])
            P.op("vector", lambda e, o=c.sc_u[base:base + 64, :TG],
                 a=QQ2[sl][base:base + 64, 0:TG]: e.tensor_tensor(
                     out=o, in0=a, in1=a, op=ALU.mult),
                 reads=[(QQ2[sl], 0, TG)], writes=[XK(c.sc_u, 0, TG)])

        def prep_m(g, wv_, sl):
            base = 64 * (g % 2)
            P.op("tensor", lambda e, o=c.ps[6][:, :TG],
                 l=c.ONESF[base:base + 64, 0:128],
                 r=c.sc_u[base:base + 64, :TG]: e.matmul(out=o, lhsT=l, rhs=r,
                                                         start=True, stop=True),
                 reads=[XK(c.sc_u, 0, TG), "ones1f"], writes=["ps6"])

        def prep_b(g, wv_, sl):
            P.op("vector", lambda e, o=c.colC[:, 2:3], i=c.ps[6][:, :TG]:
                 e.reduce_max(out=o, in_=i, axis=AX.X), reads=["ps6"], writes=["cC2"])
            P.op("vector", lambda e, o=c.colC[:, 3:4], a=c.colC[:, 2:3],
                 b=c.colC[:, 1:2]: e.tensor_tensor(out=o, in0=a, in1=b, op=ALU.mult),
                 reads=["cC2", "cC1"], writes=["cC3"])
            P.op("scalar", lambda e, o=c.colC[:, 3:4], i=c.colC[:, 3:4]:
                 e.activation(out=o, in_=i, func=AF.Ln), reads=["cC3"], writes=["cC3"])
            P.op("scalar", lambda e, o=c.colC[:, 3:4], i=c.colC[:, 3:4]:
                 e.activation(out=o, in_=i, func=AF.Exp, scale=0.5),
                 reads=["cC3"], writes=["cC3"])
            P.op("vector", lambda e, o=c.colC[:, 8 + sl:9 + sl], i=c.colC[:, 3:4]:
                 e.tensor_scalar_mul(out=o, in0=i, scalar1=-SC),
                 reads=["cC3"], writes=[f"negc{sl}"])

        def attend(g, wv_, sl, hook0=None, hook1=None, hook2=None, hook3=None):
            base = 64 * (g % 2)
            hw = g * 4 + wv_
            ob = 2 + (hw % 2)
            dn = 4 + (hw % 2)
            db = dn
            stb = (0, 1, dn)

            def st_mm(kt):
                sb_ = stb[kt % 3]
                kcol = (g // 2) * 2048 + kt * 128
                P.op("tensor", lambda e, o=c.ps[sb_][:, :TG],
                     l=KB[base:base + 64, kcol:kcol + 128],
                     r=QQ2[sl][base:base + 64, 0:TG]: e.matmul(
                         out=o, lhsT=l, rhs=r, start=True, stop=True),
                     reads=[(KB, kcol, kcol + 128), (QQ2[sl], 0, TG)],
                     writes=[f"ps{sb_}"], group=K + f"st{qg}_{hw}_{kt}")
                ac = (kt % 3) * TG
                P.op("scalar", lambda e, o=AB[:, ac: ac + TG],
                     i=c.ps[sb_][:, :TG], b=c.colC[:, 8 + sl:9 + sl]: e.activation(
                         out=o, in_=i, func=AF.Exp, bias=b, scale=SC),
                     reads=[f"ps{sb_}", f"negc{sl}"],
                     writes=[(AB, ac, ac + TG)])

            def av_mm(kt):
                ac = (kt % 3) * TG
                P.op("tensor", lambda e, o=c.ps[ob][0:65, :TG],
                     l=VB[:, kt * 260 + g * 65: kt * 260 + g * 65 + 65],
                     r=AB[:, ac: ac + TG],
                     st=(kt == 0), sp=(kt == 15): e.matmul(
                         out=o, lhsT=l, rhs=r, start=st, stop=sp),
                     reads=[(VB, kt * 260 + g * 65, kt * 260 + g * 65 + 65),
                            (AB, ac, ac + TG)],
                     writes=[f"ps{ob}"], group=K + f"o{qg}_{hw}")

            st_mm(0)
            st_mm(1)
            if hook0:
                hook0()
            st_mm(2)
            av_mm(0)
            nxt = 3
            done = 1
            while done < 5:
                if nxt < 16:
                    st_mm(nxt)
                    nxt += 1
                av_mm(done)
                done += 1
            if hook1:
                hook1()
            while done < 9:
                st_mm(nxt)
                nxt += 1
                av_mm(done)
                done += 1
            if hook2:
                hook2()
            while done < 14:
                if nxt < 16:
                    st_mm(nxt)
                    nxt += 1
                av_mm(done)
                done += 1
            if hook3:
                hook3()
            while done < 16:
                if nxt < 16:
                    st_mm(nxt)
                    nxt += 1
                av_mm(done)
                done += 1

            def epilogue():
                P.op("vector", lambda e, o=c.den_bf[:, :TG], i=c.ps[ob][64:65, :TG]:
                     _recip_bf16(e, o, i), reads=[f"ps{ob}"], writes=["row_a"])
                P.op("tensor", lambda e, o=c.ps[dn][0:64, :TG], l=c.ONESB[0:1, 0:64],
                     r=c.den_bf[:, :TG]: e.matmul(out=o, lhsT=l, rhs=r,
                                                  start=True, stop=True),
                     reads=["row_a", "ones1b"], writes=[f"ps{dn}"])
                P.op("scalar", lambda e, o=QQ2[sl][0:64, 0:TG], i=c.ps[ob][0:64, :TG]:
                     e.activation(out=o, in_=i, func=AF.Identity),
                     reads=[f"ps{ob}"], writes=[(QQ2[sl], 0, TG)])
                och = 2 * g + wv_ // 2
                ob_ = 64 * (wv_ % 2)
                P.op("vector", lambda e,
                     o=OB[ob_:ob_ + 64, och * 2048 + t0: och * 2048 + t0 + TG],
                     a=QQ2[sl][0:64, 0:TG], b=c.ps[dn][0:64, :TG]: e.tensor_tensor(
                         out=o, in0=a, in1=b, op=ALU.mult),
                     reads=[(QQ2[sl], 0, TG), f"ps{dn}"],
                     writes=[(OB, och * 2048 + t0, och * 2048 + t0 + TG)])
            return epilogue

        heads = [(g, wv_) for g in range(4) for wv_ in range(4)]
        prep_a(*heads[0], 0)
        prep_m(*heads[0], 0)
        prep_b(*heads[0], 0)
        pend = None
        for i, (g, wv_) in enumerate(heads):
            if i + 1 < 16:
                g1, w1_ = heads[i + 1]
                s1 = (i + 1) % 2
                pend = attend(g, wv_, i % 2, hook0=pend,
                              hook1=(lambda g1=g1, w1_=w1_, s1=s1: prep_a(g1, w1_, s1)),
                              hook2=(lambda g1=g1, w1_=w1_, s1=s1: prep_m(g1, w1_, s1)),
                              hook3=(lambda g1=g1, w1_=w1_, s1=s1: prep_b(g1, w1_, s1)))
            else:
                pend = attend(g, wv_, i % 2, hook0=pend)
        pend()
    for tg in range(NTG):
        t0 = tg * TG
        emit_ln_bcast_w(P, c, c.row_r, c.row_mr, t0, TG, K + "ln", 5, 6)
        for fch in range(NCH):
            bank = fch % 4
            for ch in range(NCH):
                wc = ch * 1024 + fch * 128
                P.op("tensor", lambda e, o=c.ps[bank][:, :TG],
                     l=c.Wp3[:, wc:wc + 128], r=OB[:, ch * 2048 + t0: ch * 2048 + t0 + TG],
                     st=(ch == 0), sp=(ch == NCH - 1): e.matmul(
                         out=o, lhsT=l, rhs=r, start=st, stop=sp),
                     reads=[(c.Wp3, wc, wc + 128), (OB, ch * 2048 + t0, ch * 2048 + t0 + TG)],
                     writes=[f"ps{bank}"], group=K + f"wp{tg}_{fch}")
            xc = fch * S + t0
            P.op("vector", lambda e, o=c.sc_u[:, :TG], a=X[:, xc:xc + TG],
                 b=c.rb[:, 0:TG]: e.tensor_tensor(out=o, in0=a, in1=b, op=ALU.mult),
                 reads=[XK(X, xc, xc + TG), "rbc_r"], writes=[XK(c.sc_u, 0, TG)])
            P.op("vector", lambda e, o=c.sc_u[:, :TG], a=c.sc_u[:, :TG],
                 b=c.mrb[:, 0:TG]: e.tensor_tensor(out=o, in0=a, in1=b, op=ALU.subtract),
                 reads=[XK(c.sc_u, 0, TG), "rbc_mr"], writes=[XK(c.sc_u, 0, TG)])
            P.op("vector", lambda e, o=c.sc_u[:, :TG], a=c.sc_u[:, :TG],
                 b=c.ps[bank][:, :TG]: e.tensor_tensor(out=o, in0=a, in1=b, op=ALU.add),
                 reads=[XK(c.sc_u, 0, TG), f"ps{bank}"], writes=[XK(c.sc_u, 0, TG)])
            P.op("vector", lambda e, o=X[:, xc:xc + TG], a=X[:, xc:xc + TG],
                 b=c.sc_u[:, :TG]: e.tensor_tensor(out=o, in0=a, in1=b, op=ALU.add),
                 reads=[XK(c.sc_u, 0, TG), XK(X, xc, xc + TG)],
                 writes=[XK(X, xc, xc + TG)])


def emit_l1_attn_b(P, c, X):
    K = "lb"
    CSZ = 256
    emit_ln_rows(P, c, X, 0, c.row_m, c.row_r, c.row_mr, K + "ln")
    Qc = Rv(c.B1, 0, 2048)
    Kf = Rv(c.B1, 2048, 2048)
    Kt = Rv(c.B1, 4096, 2048)
    Vt = Rv(c.B1, 6144, 2048)
    WO = Rv(c.B1, 8192, 8192)
    KVb = Rv(c.B2, 8192, 528)
    PMB = Rv(c.B2, 8832, 512)
    OC = Rv(c.xn, 2048, 2048)
    emit_wload(P, c, Rv(c.Wp1, 0, 8192), c.d_lb_qkvT, 1024, 1024, K + "wq", col0=0)
    emit_wload(P, c, Rv(c.Wp2, 0, 8192), c.d_lb_qkvT, 1024, 1024, K + "wk", col0=1024)
    emit_wload(P, c, Rv(c.Wp3, 0, 8192), c.d_lb_qkvT, 1024, 1024, K + "wv", col0=2048)
    emit_wload(P, c, WO, c.d_lb_woT, 1024, 1024, K + "wo")
    P.dma("sync", c.colA[:, 0:8], c.d_lb_qb[:, 0:8], writes=["colA"])
    P.dma("sync", c.colA[:, 8:16], c.d_lb_kb[:, 0:8], writes=["colA"])
    P.dma("sync", c.colB[:, 0:8], c.d_lb_ob[:, 0:8], writes=["colBb"])

    for cn in range(8):
        c0 = cn * CSZ
        emit_ln_bcast_w(P, c, c.row_r, c.row_mr, c0, CSZ, K + "ln", 5, 6)
        for ch in range(NCH):
            emit_xn_tile(P, c, X, ch, c0, CSZ, c.xn, ch * CSZ, 5, 6)
        for which, dstr, bcol0 in ((0, Qc, 0), (1, Kf, 8)):
            wsl = c.Wp1 if which == 0 else c.Wp2
            for fch in range(NCH):
                bank = fch % 2
                for ch in range(NCH):
                    wc = ch * 1024 + fch * 128
                    P.op("tensor", lambda e, o=c.ps[bank][:, :CSZ],
                         l=wsl[:, wc:wc + 128], r=c.xn[:, ch * CSZ: ch * CSZ + CSZ],
                         st=(ch == 0), sp=(ch == NCH - 1): e.matmul(
                             out=o, lhsT=l, rhs=r, start=st, stop=sp),
                         reads=[(wsl, wc, wc + 128), XK(c.xn, ch * CSZ, ch * CSZ + CSZ)],
                         writes=[f"ps{bank}"], group=K + f"p{cn}_{which}_{fch}")
                P.op("scalar", lambda e, o=dstr[:, fch * CSZ: fch * CSZ + CSZ],
                     i=c.ps[bank][:, :CSZ], b=c.colA[:, bcol0 + fch: bcol0 + fch + 1]:
                     e.activation(out=o, in_=i, func=AF.Relu, bias=b),
                     reads=[f"ps{bank}", "colA"],
                     writes=[(dstr, fch * CSZ, fch * CSZ + CSZ)])
                P.op("vector", lambda e, o=dstr[:, fch * CSZ: fch * CSZ + CSZ],
                     i=dstr[:, fch * CSZ: fch * CSZ + CSZ]: e.tensor_scalar_add(
                         out=o, in0=i, scalar1=1e-6),
                     reads=[(dstr, fch * CSZ, fch * CSZ + CSZ)],
                     writes=[(dstr, fch * CSZ, fch * CSZ + CSZ)])
        for which, dstr, brow, fn in ((0, Kt, c.kb_row, AF.Relu), (1, Vt, c.vb_row, AF.Identity)):
            wsl = c.Wp2 if which == 0 else c.Wp3
            for ti in range(2):
                for half in range(2):
                    bank = 2 + half
                    for ch in range(NCH):
                        P.op("tensor", lambda e, o=c.ps[bank][:, :512],
                             l=c.xn[:, ch * CSZ + ti * 128: ch * CSZ + ti * 128 + 128],
                             r=wsl[:, ch * 1024 + half * 512: ch * 1024 + half * 512 + 512],
                             st=(ch == 0): e.matmul(out=o, lhsT=l, rhs=r,
                                                    start=st, stop=False),
                             reads=[(wsl, ch * 1024 + half * 512, ch * 1024 + half * 512 + 512),
                                    XK(c.xn, ch * CSZ + ti * 128, ch * CSZ + ti * 128 + 128)],
                             writes=[f"ps{bank}"], group=K + f"t{cn}_{which}_{ti}_{half}")
                    P.op("tensor", lambda e, o=c.ps[bank][:, :512],
                         l=c.ONESB[brow.p0:brow.p0 + 1, 0:128],
                         r=brow[:, half * 512: half * 512 + 512]: e.matmul(
                             out=o, lhsT=l, rhs=r, start=False, stop=True),
                         reads=["kb_row", "vb_row", "ones1b"], writes=[f"ps{bank}"],
                         group=K + f"t{cn}_{which}_{ti}_{half}")
                    P.op("scalar", lambda e,
                         o=dstr[:, ti * 1024 + half * 512: ti * 1024 + half * 512 + 512],
                         i=c.ps[bank][:, :512], f=fn: e.activation(out=o, in_=i, func=f),
                         reads=[f"ps{bank}"],
                         writes=[(dstr, ti * 1024 + half * 512, ti * 1024 + half * 512 + 512)])
                    if which == 0:
                        P.op("vector", lambda e,
                             o=dstr[:, ti * 1024 + half * 512: ti * 1024 + half * 512 + 512],
                             i=dstr[:, ti * 1024 + half * 512: ti * 1024 + half * 512 + 512]:
                             e.tensor_scalar_add(out=o, in0=i, scalar1=1e-6),
                             reads=[(dstr, ti * 1024 + half * 512, ti * 1024 + half * 512 + 512)],
                             writes=[(dstr, ti * 1024 + half * 512, ti * 1024 + half * 512 + 512)])
        EPT = Rv(c.B2, 9344, 512)
        pend_v = [None]
        pend_p = [None]

        def head_lb(h):
            base = 64 * (h % 2)
            hc = h // 2
            nb = 2 + (h % 2)
            dn = 4 + (h % 2)
            for jt in range(2):
                P.op("tensor", lambda e, o=c.ps[jt][:, :CSZ],
                     l=Kf[base:base + 64, hc * 256 + jt * 128: hc * 256 + jt * 128 + 128],
                     r=Qc[base:base + 64, hc * 256: hc * 256 + CSZ]: e.matmul(
                         out=o, lhsT=l, rhs=r, start=True, stop=True),
                     reads=[(Kf, hc * 256 + jt * 128, hc * 256 + jt * 128 + 128),
                            (Qc, hc * 256, hc * 256 + CSZ)],
                     writes=[f"ps{jt}"], group=K + f"pr{cn}_{h}_{jt}")
                P.op("vector", lambda e, o=PMB[:, jt * 256: jt * 256 + 256],
                     a=c.ps[jt][:, :CSZ], b=c.causal[:, jt * 256: jt * 256 + 256]:
                     e.tensor_tensor(out=o, in0=a, in1=b, op=ALU.mult),
                     reads=[f"ps{jt}", (c.causal, jt * 256, jt * 256 + 256)],
                     writes=[(PMB, jt * 256, jt * 256 + 256)])
            if pend_v[0] is not None:
                pend_v[0]()
                pend_v[0] = None
            for jt in range(2):
                P.op("tensor", lambda e, o=c.ps[nb][0:64, :CSZ],
                     l=Vt[:, jt * 1024 + h * 64: jt * 1024 + h * 64 + 64],
                     r=PMB[:, jt * 256: jt * 256 + 256],
                     st=(jt == 0), sp=(jt == 1 and cn == 0): e.matmul(
                         out=o, lhsT=l, rhs=r, start=st, stop=sp),
                     reads=[(Vt, jt * 1024 + h * 64, jt * 1024 + h * 64 + 64),
                            (PMB, jt * 256, jt * 256 + 256)],
                     writes=[f"ps{nb}"], group=K + f"nm{cn}_{h}")
                P.op("tensor", lambda e, o=c.ps[dn][0:1, :CSZ], l=c.onesc[:],
                     r=PMB[:, jt * 256: jt * 256 + 256],
                     st=(jt == 0), sp=(jt == 1 and cn == 0): e.matmul(
                         out=o, lhsT=l, rhs=r, start=st, stop=sp),
                     reads=[(PMB, jt * 256, jt * 256 + 256), "onesc"],
                     writes=[f"ps{dn}"], group=K + f"dn{cn}_{h}")
            if cn > 0:
                P.op("tensor", lambda e, o=c.ps[nb][0:64, :CSZ],
                     l=KVb[base:base + 64, hc * 64: hc * 64 + 64],
                     r=Qc[base:base + 64, hc * 256: hc * 256 + CSZ]: e.matmul(
                         out=o, lhsT=l, rhs=r, start=False, stop=True),
                     reads=[(KVb, hc * 64, hc * 64 + 64), (Qc, hc * 256, hc * 256 + CSZ)],
                     writes=[f"ps{nb}"], group=K + f"nm{cn}_{h}")
                P.op("tensor", lambda e, o=c.ps[dn][0:1, :CSZ],
                     l=KVb[base:base + 64, 512 + hc: 512 + hc + 1],
                     r=Qc[base:base + 64, hc * 256: hc * 256 + CSZ]: e.matmul(
                         out=o, lhsT=l, rhs=r, start=False, stop=True),
                     reads=[(KVb, 512 + hc, 512 + hc + 1), (Qc, hc * 256, hc * 256 + CSZ)],
                     writes=[f"ps{dn}"], group=K + f"dn{cn}_{h}")
            if cn < 7:
                for jt in range(2):
                    P.op("tensor", lambda e, o=c.ps[6][base:base + 64, 0:64],
                         l=Kt[:, jt * 1024 + h * 64: jt * 1024 + h * 64 + 64],
                         r=Vt[:, jt * 1024 + h * 64: jt * 1024 + h * 64 + 64],
                         st=(jt == 0): e.matmul(
                             out=o, lhsT=l, rhs=r, start=st, stop=False),
                         reads=[(Kt, jt * 1024 + h * 64, jt * 1024 + h * 64 + 64),
                                (Vt, jt * 1024 + h * 64, jt * 1024 + h * 64 + 64)],
                         writes=["ps6"], group=K + f"kv{cn}_{h}")
                    P.op("tensor", lambda e, o=c.ps[6][base:base + 64, 64:65],
                         l=Kt[:, jt * 1024 + h * 64: jt * 1024 + h * 64 + 64],
                         r=c.onesc[:], sp=(jt == 1): e.matmul(
                             out=o, lhsT=l, rhs=r, start=False, stop=sp),
                         reads=[(Kt, jt * 1024 + h * 64, jt * 1024 + h * 64 + 64), "onesc"],
                         writes=["ps6"], group=K + f"kv{cn}_{h}")
                if cn == 0:
                    P.op("vector", lambda e, o=c.KVF[base:base + 64, hc * 64: hc * 64 + 64],
                         i=c.ps[6][base:base + 64, 0:64]: e.tensor_copy(out=o, in_=i),
                         reads=["ps6"], writes=["KVF"])
                    P.op("vector", lambda e, o=c.KVF[base:base + 64, 512 + hc: 512 + hc + 1],
                         i=c.ps[6][base:base + 64, 64:65]: e.tensor_copy(out=o, in_=i),
                         reads=["ps6"], writes=["KVF"])
                else:
                    P.op("vector", lambda e, o=c.KVF[base:base + 64, hc * 64: hc * 64 + 64],
                         a=c.KVF[base:base + 64, hc * 64: hc * 64 + 64],
                         b=c.ps[6][base:base + 64, 0:64]: e.tensor_tensor(
                             out=o, in0=a, in1=b, op=ALU.add),
                         reads=["ps6", "KVF"], writes=["KVF"])
                    P.op("vector", lambda e, o=c.KVF[base:base + 64, 512 + hc: 512 + hc + 1],
                         a=c.KVF[base:base + 64, 512 + hc: 512 + hc + 1],
                         b=c.ps[6][base:base + 64, 64:65]: e.tensor_tensor(
                             out=o, in0=a, in1=b, op=ALU.add),
                         reads=["ps6", "KVF"], writes=["KVF"])
                P.op("vector", lambda e, o=KVb[base:base + 64, hc * 64: hc * 64 + 64],
                     i=c.KVF[base:base + 64, hc * 64: hc * 64 + 64]: e.tensor_copy(
                         out=o, in_=i),
                     reads=["KVF"], writes=[(KVb, hc * 64, hc * 64 + 64)])
                P.op("vector", lambda e, o=KVb[base:base + 64, 512 + hc: 512 + hc + 1],
                     i=c.KVF[base:base + 64, 512 + hc: 512 + hc + 1]: e.tensor_copy(
                         out=o, in_=i),
                     reads=["KVF"], writes=[(KVb, 512 + hc, 512 + hc + 1)])

            if pend_p[0] is not None:
                pend_p[0]()
                pend_p[0] = None

            def epi_v(sl=h % 2, dn=dn):
                P.op("vector", lambda e, o=c.row_a[:, :CSZ], i=c.ps[dn][0:1, :CSZ]:
                     e.tensor_scalar_max(out=o, in0=i, scalar1=1e-6),
                     reads=[f"ps{dn}"], writes=["row_a"])
                P.op("vector", lambda e, o=c.den_bf[:, :CSZ], i=c.row_a[:, :CSZ]:
                     _recip_bf16(e, o, i), reads=["row_a"], writes=["row_a"])

            def epi_p(sl=h % 2, dn=dn, nb=nb, base=base, hc=hc):
                P.op("tensor", lambda e, o=c.ps[dn][0:64, :CSZ], l=c.ONESB[0:1, 0:64],
                     r=c.den_bf[:, :CSZ]: e.matmul(out=o, lhsT=l, rhs=r,
                                                   start=True, stop=True),
                     reads=["row_a", "ones1b"], writes=[f"ps{dn}"])
                P.op("scalar", lambda e, o=EPT[0:64, sl * 256: sl * 256 + 256],
                     i=c.ps[nb][0:64, :CSZ]: e.activation(out=o, in_=i, func=AF.Identity),
                     reads=[f"ps{nb}"], writes=[(EPT, sl * 256, sl * 256 + 256)])
                P.op("vector", lambda e,
                     o=OC[base:base + 64, hc * 256: hc * 256 + CSZ],
                     a=EPT[0:64, sl * 256: sl * 256 + 256], b=c.ps[dn][0:64, :CSZ]:
                     e.tensor_tensor(out=o, in0=a, in1=b, op=ALU.mult),
                     reads=[(EPT, sl * 256, sl * 256 + 256), f"ps{dn}"],
                     writes=[(OC, hc * 256, hc * 256 + CSZ)])
            pend_v[0] = epi_v
            pend_p[0] = epi_p

        for h in range(16):
            head_lb(h)
        pend_v[0]()
        pend_p[0]()
        pend_v[0] = None
        pend_p[0] = None

        for fch in range(NCH):
            bank = fch % 2
            for ch in range(NCH):
                wc = ch * 1024 + fch * 128
                P.op("tensor", lambda e, o=c.ps[bank][:, :CSZ],
                     l=WO[:, wc:wc + 128], r=OC[:, ch * 256: ch * 256 + CSZ],
                     st=(ch == 0), sp=(ch == NCH - 1): e.matmul(
                         out=o, lhsT=l, rhs=r, start=st, stop=sp),
                     reads=[(WO, wc, wc + 128), (OC, ch * 256, ch * 256 + CSZ)],
                     writes=[f"ps{bank}"], group=K + f"op{cn}_{fch}")
            xc = fch * S + c0
            P.op("vector", lambda e, o=X[:, xc:xc + CSZ], i=c.ps[bank][:, :CSZ],
                 s=c.colB[:, fch:fch + 1], x=X[:, xc:xc + CSZ]: e.scalar_tensor_tensor(
                     out=o, in0=i, scalar=s, in1=x, op0=ALU.add, op1=ALU.add),
                 reads=[f"ps{bank}", "colBb", XK(X, xc, xc + CSZ)],
                 writes=[XK(X, xc, xc + CSZ)])


SUBLAYERS = {
    "l0_attn_a": emit_l0_attn_a,
    "l0_attn_b": emit_l0_attn_b,
    "l1_attn_a": emit_l1_attn_a,
    "l1_attn_b": emit_l1_attn_b,
    "l1_ffn": emit_l1_ffn,
    "l0_ffn": None,
}

DRAM_SPECS = {
    "aa_wq": [1024, 1024], "aa_wk": [1024, 1024], "aa_wv": [1024, 1024],
    "aa_wd": [1024, 1024], "aa_bq": [128, 8], "aa_bk": [128, 8],
    "aa_bv": [128, 8], "aa_bd": [128, 8], "aa_bv_row": [1, 1024],
    "f_wqT": [1024, 1024], "f_wkcT": [1024, 256], "f_wvcT": [1024, 256],
    "f_woT": [1024, 1024], "f_wrT": [1024, 8], "f_ebias": [1, 8],
    "f_sw1T": [1024, 1024], "f_sw2T": [1024, 1024], "f_sw3T": [1024, 1024],
    "f_ew1T": [8, 1024, 1024], "f_ew2T": [8, 1024, 1024],
    "f_ew3T": [8, 1024, 1024], "f_g1": [128, 8], "f_g2": [128, 8],
    "ab_wq": [1024, 1024], "ab_wk": [1024, 256], "ab_wv": [1024, 256],
    "ab_wpost": [1024, 1024],
    "sa_waT": [1024, 256], "sa_bT": [1024, 256], "sa_cT": [256, 1024],
    "sa_dT": [1024, 1024], "sa_wab": [128, 2],
    "lb_qkvT": [1024, 3072], "lb_qb": [128, 8], "lb_kb": [128, 8],
    "lb_kb_row": [1, 1024], "lb_vb_row": [1, 1024], "lb_woT": [1024, 1024],
    "lb_ob": [128, 8],
    "lf_wrT": [1024, 1024], "lf_wiT": [1024, 1024], "lf_bias": [128, 8],
    "c_fr": [1024, 1024], "c_fineg": [1024, 1024], "c_gr": [1024, 1024],
    "c_gi": [1024, 1024], "c_gineg": [1024, 1024],
    "c_identb": [128, 128], "c_identf": [128, 128],
    "c_band": [128, 1792], "c_causal": [128, 2 * 256], "c_e8": [8, 1024],
}
DRAM_BF16 = {"c_identb", "c_band", "c_causal"}

STAGE_INPUTS = {
    "l0_attn_a": ["aa_wq", "aa_wk", "aa_wv", "aa_wd", "aa_bq", "aa_bk",
                  "aa_bv_row", "aa_bd", "c_band"],
    "l0_ffn": ["f_wqT", "f_wkcT", "f_wvcT", "f_woT", "f_wrT", "f_ebias",
               "f_sw1T", "f_sw2T", "f_sw3T", "f_ew1T", "f_ew2T", "f_ew3T",
               "f_g1", "f_g2", "c_identb", "c_e8"],
    "l0_attn_b": ["ab_wq", "ab_wk", "ab_wv", "ab_wpost"],
    "l1_attn_a": ["sa_waT", "sa_bT", "sa_cT", "sa_dT", "sa_wab"],
    "l1_ffn": ["lf_wrT", "lf_wiT", "lf_bias", "c_fr", "c_fineg", "c_gr",
               "c_gi", "c_gineg"],
    "l1_attn_b": ["lb_qkvT", "lb_qb", "lb_kb", "lb_kb_row", "lb_vb_row",
                  "lb_woT", "lb_ob", "c_causal"],
}


def stage_input_names(stages):
    names = []
    for st in stages:
        for nm in STAGE_INPUTS[st]:
            if nm not in names:
                names.append(nm)
    return names


def build_program(stages, nbatch=2):
    nc = bass.Bass("TRN2", target_bir_lowering=False)
    from contextlib import ExitStack
    stk = ExitStack()
    nc._stk = stk
    xin = nc.dram_tensor("xT", [nbatch, D, S], F32, kind="ExternalInput").ap()
    xout = nc.dram_tensor("yT", [nbatch, D, S], F32, kind="ExternalOutput").ap()
    dr = {}
    for nm in stage_input_names(stages):
        dt = BF16 if nm in DRAM_BF16 else F32
        dr[nm] = nc.dram_tensor(nm, DRAM_SPECS[nm], dt, kind="ExternalInput").ap()
    if "l1_ffn" in stages:
        dr["mscr"] = nc.dram_tensor("mscr", [128, 8192], BF16, kind="Internal").ap()
        if DEBUG_DUMP_M:
            dr["mdump"] = nc.dram_tensor("mdump", [128, 8192], BF16,
                                         kind="ExternalOutput").ap()

    with stk:
        c = alloc_ctx(nc, stk, set(stages))
        for nm, ap in dr.items():
            setattr(c, "d_" + nm, ap)
        P = Prog(nc)
        emit_const_init(P, c, stages)
        c.ffn_count = 0
        for b in range(nbatch):
            c.ffn_count = 0
            src = xin[b].rearrange("(c p) t -> p c t", p=128)
            P.dma("sync", c.X[:, :].rearrange("p (c t) -> p c t", c=NCH), src,
                  writes=[XK(c.X, 0, NCH * S)])
            for st in stages:
                if st == "l0_ffn":
                    emit_l0_ffn(P, c, c.X, c.ffn_count)
                    c.ffn_count += 1
                else:
                    SUBLAYERS[st](P, c, c.X)
            dst = xout[b].rearrange("(c p) t -> p c t", p=128)
            P.dma("sync", dst, c.X[:, :].rearrange("p (c t) -> p c t", c=NCH),
                  reads=[XK(c.X, 0, NCH * S)], writes=[f"yT{b}"])
        P.finish_dmas("sync")
        P.replay()
    return P, nc


def kernel(**inputs):
    w = prep_weights(inputs)
    x = np.asarray(inputs["L0_attn_a_x"])
    B = x.shape[0]
    bpc = B // NCORES
    stages = ["l0_attn_a", "l0_ffn", "l0_attn_b", "l0_ffn",
              "l1_attn_a", "l1_ffn", "l1_attn_b", "l1_ffn"]
    P, nc = build_program(stages, nbatch=bpc)
    in_maps = []
    for core in range(NCORES):
        m = {"xT": np.ascontiguousarray(
            x[core * bpc:(core + 1) * bpc].transpose(0, 2, 1))}
        for nm in stage_input_names(stages):
            m[nm] = w[nm]
        in_maps.append(m)
    res = run_bass_kernel_spmd(nc, in_maps, core_ids=list(range(NCORES)))
    out = np.empty_like(x)
    for core in range(NCORES):
        yT = np.asarray(res.results[core]["yT"])
        out[core * bpc:(core + 1) * bpc] = yT.transpose(0, 2, 1)
    return out


# version 16
# speedup vs baseline: 1.0059x; 1.0059x over previous
import numpy as np
import ml_dtypes
import concourse.bass as bass
import concourse.mybir as mybir
from concourse.bass_utils import run_bass_kernel_spmd
from concourse.alu_op_type import AluOpType as ALU

F32 = mybir.dt.float32
BF16 = mybir.dt.bfloat16
AF = mybir.ActivationFunctionType
AX = mybir.AxisListType
BF = ml_dtypes.bfloat16

ENGS = ("sync", "tensor", "vector", "scalar", "gpsimd")
DEBUG_DUMP_M = False
NCORES = 8
S = 2048
D = 1024
NCH = 8
TG = 512
NTG = S // TG
EPS_LN = 1e-5
EPS_RMS = 1e-6


class _Sem:
    __slots__ = ("h", "n")

    def __init__(self, h):
        self.h = h
        self.n = 0


class Rv:
    __slots__ = ("buf", "p0", "np_", "base", "ncols")

    def __init__(self, buf, base, ncols, p0=0, np_=128):
        self.buf, self.base, self.ncols, self.p0, self.np_ = buf, base, ncols, p0, np_

    def __getitem__(self, idx):
        if not isinstance(idx, tuple):
            idx = (idx, slice(None))
        p, q = idx
        ps = slice(self.p0 + (p.start or 0),
                   self.p0 + (p.stop if p.stop is not None else self.np_))
        qs = slice(self.base + (q.start or 0),
                   self.base + (q.stop if q.stop is not None else self.ncols))
        return self.buf[ps, qs]

    @property
    def name(self):
        return f"{self.buf.name}@{self.p0}+{self.base}"


def _keys(region):
    if isinstance(region, str):
        return [region]
    buf, lo, hi = region
    if isinstance(buf, Rv):
        name = f"{buf.buf.name}@{buf.p0}"
        lo, hi = buf.base + lo, buf.base + hi
    else:
        name = (buf.name if hasattr(buf, "name") else str(buf)) + "@0"
    return [f"{name}:{b}" for b in range(lo // 128, (hi + 127) // 128)]


DMA_POOL = 8


class Prog:
    def __init__(self, nc):
        self.nc = nc
        self.q = {e: [] for e in ENGS}
        self.esem = {}
        self.dpool = {}
        self.duse = {}
        self.didx = {}
        for e in ("tensor", "vector", "scalar", "gpsimd"):
            self.esem[e] = _Sem(nc._stk.enter_context(nc.semaphore(f"se_{e}")))
        for e in ("sync", "gpsimd", "scalar"):
            self.dpool[e] = [_Sem(nc._stk.enter_context(nc.semaphore(f"sd_{e}{i}")))
                             for i in range(DMA_POOL)]
            self.didx[e] = 0
        self.last_w = {}
        self.readers = {}
        self.waited = {e: {} for e in ENGS}
        self.nops = {e: 0 for e in ENGS}

    def _wait(self, engine, sem, val):
        w = self.waited[engine]
        k = id(sem)
        if w.get(k, 0) >= val:
            return
        w[k] = val
        self.q[engine].append(lambda eng, s=sem.h, v=val: eng.wait_ge(s, v))

    def _edges(self, engine, reads, writes, group):
        tsem = self.esem["tensor"]
        skip = (engine == "tensor")
        for k in reads:
            lw = self.last_w.get(k)
            if lw is not None and not (skip and lw[0] is tsem):
                self._wait(engine, lw[0], lw[1])
        for k in writes:
            lw = self.last_w.get(k)
            if lw is not None and not (group is not None and lw[2] == group) \
                    and not (skip and lw[0] is tsem):
                self._wait(engine, lw[0], lw[1])
            for (s, v) in self.readers.get(k, ()):
                if not (skip and s is tsem):
                    self._wait(engine, s, v)

    def _update(self, reads, writes, sem, val, group):
        for k in writes:
            self.last_w[k] = (sem, val, group)
            self.readers[k] = []
        for k in reads:
            self.readers.setdefault(k, []).append((sem, val))

    def op(self, engine, fn, reads=(), writes=(), group=None, dma=False):
        rk = [k for r in reads for k in _keys(r)]
        wk = [k for w in writes for k in _keys(w)]
        self._edges(engine, rk, wk, group)
        if dma:
            sem = self.dpool[engine][self.didx[engine] % DMA_POOL]
            self.didx[engine] += 1
            if sem.n > 0:
                self._wait(engine, sem, sem.n)
            step = 16
        else:
            sem, step = self.esem[engine], 1
        sem.n += step
        self.q[engine].append(
            lambda eng, fn=fn, s=sem.h, st=step: fn(eng).then_inc(s, st))
        self._update(rk, wk, sem, sem.n, group)
        self.nops[engine] += 1
        return (sem, sem.n)

    def dma(self, engine, out, in_, reads=(), writes=(), **kw):
        return self.op(engine,
                       lambda e, o=out, i=in_, kw=kw: e.dma_start(out=o, in_=i, **kw),
                       reads=reads, writes=writes, dma=True)

    def global_barrier(self):
        sems = list(self.esem.values()) + [s for p in self.dpool.values() for s in p]
        for e in ENGS:
            for s in sems:
                if s.n > 0:
                    self._wait(e, s, s.n)

    def finish_dmas(self, engine):
        for q, pool in self.dpool.items():
            for s in pool:
                if s.n > 0:
                    self._wait(engine, s, s.n)

    def barrier_wait(self, engine, sem_val):
        self._wait(engine, sem_val[0], sem_val[1])

    def replay(self):
        with self.nc.Block() as block:
            @block.sync
            def _(eng):
                for f in self.q["sync"]:
                    f(eng)

            @block.tensor
            def _(eng):
                for f in self.q["tensor"]:
                    f(eng)

            @block.vector
            def _(eng):
                for f in self.q["vector"]:
                    f(eng)

            @block.scalar
            def _(eng):
                for f in self.q["scalar"]:
                    f(eng)

            @block.gpsimd
            def _(eng):
                for f in self.q["gpsimd"]:
                    f(eng)


def _t(a):
    return np.ascontiguousarray(np.asarray(a).T)


def _col(b):
    b = np.asarray(b)
    return np.ascontiguousarray(b.reshape(-1, 128).T)


def prep_weights(I):
    w = {}
    for nm in ("Wq", "Wk", "Wv", "Wd"):
        w["aa_" + nm.lower()] = np.ascontiguousarray(I[f"L0_attn_a_{nm}"])
    for nm in ("bq", "bk", "bv", "bd"):
        w["aa_" + nm] = _col(I[f"L0_attn_a_{nm}"])
    w["aa_bv_row"] = np.ascontiguousarray(I["L0_attn_a_bv"].reshape(1, D))
    w["f_wqT"] = _t(I["L0_ffn_Wq"])
    w["f_wkcT"] = _t(I["L0_ffn_Wkc"])
    w["f_wvcT"] = _t(I["L0_ffn_Wvc"])
    w["f_woT"] = _t(I["L0_ffn_Wo"])
    w["f_wrT"] = _t(I["L0_ffn_Wr"])
    w["f_ebias"] = np.ascontiguousarray(I["L0_ffn_expert_bias"].reshape(1, 8))
    w["f_sw1T"] = _t(I["L0_ffn_sW1"])
    w["f_sw2T"] = _t(I["L0_ffn_sW2"])
    w["f_sw3T"] = _t(I["L0_ffn_sW3"])
    w["f_ew1T"] = np.ascontiguousarray(I["L0_ffn_eW1"].transpose(0, 2, 1))
    w["f_ew2T"] = np.ascontiguousarray(I["L0_ffn_eW2"].transpose(0, 2, 1))
    w["f_ew3T"] = np.ascontiguousarray(I["L0_ffn_eW3"].transpose(0, 2, 1))
    w["f_g1"] = _col(I["L0_ffn_g1"])
    w["f_g2"] = _col(I["L0_ffn_g2"])
    w["ab_wq"] = np.ascontiguousarray(I["L0_attn_b_Wq"])
    w["ab_wk"] = np.ascontiguousarray(I["L0_attn_b_Wk"])
    w["ab_wv"] = np.ascontiguousarray(I["L0_attn_b_Wv"])
    w["ab_wpost"] = np.ascontiguousarray(I["L0_attn_b_Wpost"])
    w["sa_waT"] = _t(I["L1_attn_a_Wa_w"])
    w["sa_bT"] = _t(I["L1_attn_a_B_w"])
    w["sa_cT"] = _t(I["L1_attn_a_C_w"])
    w["sa_dT"] = _t(I["L1_attn_a_D_w"])
    w["sa_wab"] = _col(I["L1_attn_a_Wa_b"])
    w["lb_qkvT"] = _t(I["L1_attn_b_qkv_w"])
    qb = np.asarray(I["L1_attn_b_qkv_b"])
    w["lb_qb"] = _col(qb[:D])
    w["lb_kb"] = _col(qb[D:2 * D])
    w["lb_kb_row"] = np.ascontiguousarray(qb[D:2 * D].reshape(1, D))
    w["lb_vb_row"] = np.ascontiguousarray(qb[2 * D:].reshape(1, D))
    w["lb_woT"] = _t(I["L1_attn_b_out_w"])
    w["lb_ob"] = _col(I["L1_attn_b_out_b"])
    w["lf_wrT"] = _t(I["L1_ffn_w_freq_real"])
    w["lf_wiT"] = _t(I["L1_ffn_w_freq_imag"])
    w["lf_bias"] = _col(I["L1_ffn_bias"])
    k = np.arange(D)
    ang = 2.0 * np.pi * np.outer(k, k) / D
    w["c_fr"] = np.cos(ang).astype(np.float32)
    w["c_fineg"] = np.sin(ang).astype(np.float32)
    w["c_gr"] = (np.cos(ang) / D).astype(np.float32)
    w["c_gi"] = (np.sin(ang) / D).astype(np.float32)
    w["c_gineg"] = (-np.sin(ang) / D).astype(np.float32)
    w["c_identb"] = np.eye(128, dtype=BF)
    w["c_identf"] = np.eye(128, dtype=np.float32)
    j = np.arange(128)[:, None]
    t = np.arange(1792)[None, :]
    w["c_band"] = (np.abs(j - (t - 768)) <= 256).astype(BF)
    cm = np.zeros((128, 2 * 256), dtype=BF)
    for jt in range(2):
        j = jt * 128 + np.arange(128)[:, None]
        t = np.arange(256)[None, :]
        cm[:, jt * 256:(jt + 1) * 256] = (j <= t).astype(BF)
    w["c_causal"] = cm
    e8 = np.zeros((8, 8 * 128), dtype=np.float32)
    for e in range(8):
        e8[e, e * 128:(e + 1) * 128] = 1.0
    w["c_e8"] = e8
    return w


class Ctx:
    pass


def XK(buf, lo, hi):
    return (buf, lo, hi)


def _recip_bf16(e, o, i):
    with e.bass.allow_low_precision(reason="attention denominator reciprocal "
                                    "rounded to bf16 for the cheap broadcast"):
        return e.reciprocal(out=o, in_=i)


def _copy_bf16(e, o, i):
    with e.bass.allow_low_precision(reason="staging a bf16 matmul operand; "
                                    "same rounding as the ACT Identity evac"):
        return e.tensor_copy(out=o, in_=i)


def emit_ln_rows(P, c, X, x0, mrow, rrow, mrrow, key):
    nc = P.nc
    for tg in range(NTG):
        t0 = tg * TG
        ps_sx = c.ps[4]
        ps_sx2 = c.ps[5]
        for ch in range(NCH):
            xr = XK(X, ch * S + t0, ch * S + t0 + TG)
            xcol = X[:, ch * S + t0: ch * S + t0 + TG]
            if ch % 2 == 0:
                sq, sqk = c.sc_u[:, :TG], XK(c.sc_u, 0, TG)
            else:
                sq, sqk = c.rb[:, 0:TG], "rbc_r"
            P.op("scalar", lambda e, o=sq, i=xcol: e.activation(
                out=o, in_=i, func=AF.Square),
                reads=[xr], writes=[sqk])
            P.op("tensor", lambda e, o=ps_sx[0:1, :TG], l=c.ONESF[:, 0:1], r=xcol,
                 st=(ch == 0), sp=(ch == NCH - 1): e.matmul(
                     out=o, lhsT=l, rhs=r, start=st, stop=sp),
                 reads=[xr, "ones1f"], writes=["ps4"], group=key + f"sx{tg}")
            P.op("tensor", lambda e, o=ps_sx2[0:1, :TG], l=c.ONESF[:, 0:1], r=sq,
                 st=(ch == 0), sp=(ch == NCH - 1): e.matmul(
                     out=o, lhsT=l, rhs=r, start=st, stop=sp),
                 reads=[sqk, "onesc"], writes=["ps5"], group=key + f"sx2{tg}")
        mr = c.row_b[:, :TG]
        rr = rrow[:, t0:t0 + TG]
        mrr = mrrow[:, t0:t0 + TG]
        vr = c.row_a[:, :TG]
        P.op("vector", lambda e, o=mr, i=ps_sx[0:1, :TG]: e.tensor_scalar_mul(
            out=o, in0=i, scalar1=1.0 / D),
            reads=["ps4"], writes=[key + "m"])
        P.op("vector", lambda e, o=vr, i=mr: e.tensor_tensor(
            out=o, in0=i, in1=i, op=ALU.mult),
            reads=[key + "m"], writes=["row_a"])
        P.op("vector", lambda e, o=vr, i=ps_sx2[0:1, :TG], t=vr: e.scalar_tensor_tensor(
            out=o, in0=i, scalar=1.0 / D, in1=t, op0=ALU.mult, op1=ALU.subtract),
            reads=["ps5", "row_a"], writes=["row_a"])
        P.op("scalar", lambda e, o=vr, i=vr, b=c.eps_ln[0:1, 0:1]: e.activation(
            out=o, in_=i, func=AF.Ln, bias=b),
            reads=["row_a", "consts"], writes=["row_a"])
        P.op("scalar", lambda e, o=rr, i=vr: e.activation(
            out=o, in_=i, func=AF.Exp, scale=-0.5),
            reads=["row_a"], writes=[key + "r"])
        P.op("vector", lambda e, o=mrr, a=mr, b=rr: e.tensor_tensor(
            out=o, in0=a, in1=b, op=ALU.mult),
            reads=[key + "m", key + "r"], writes=[key + "mr"])


def emit_ln_bcast_w(P, c, rrow, mrrow, t0, w, key, bank_r, bank_mr):
    p1, p2 = rrow.p0, mrrow.p0
    P.op("tensor", lambda e, o=c.ps[bank_r][:, :w], l=c.ONESF[p1:p1 + 1, :],
         r=rrow[:, t0:t0 + w]: e.matmul(out=o, lhsT=l, rhs=r, start=True, stop=True),
         reads=[key + "r", "ones1f"], writes=[f"ps{bank_r}"])
    P.op("tensor", lambda e, o=c.ps[bank_mr][:, :w], l=c.ONESF[p2:p2 + 1, :],
         r=mrrow[:, t0:t0 + w]: e.matmul(out=o, lhsT=l, rhs=r, start=True, stop=True),
         reads=[key + "mr", "ones1f"], writes=[f"ps{bank_mr}"])
    P.op("scalar", lambda e, o=c.rb[:, 0:w], i=c.ps[bank_r][:, :w]:
         e.activation(out=o, in_=i, func=AF.Identity),
         reads=[f"ps{bank_r}"], writes=["rbc_r"])
    P.op("scalar", lambda e, o=c.mrb[:, 0:w], i=c.ps[bank_mr][:, :w]:
         e.activation(out=o, in_=i, func=AF.Identity),
         reads=[f"ps{bank_mr}"], writes=["rbc_mr"])


def emit_ln_bcast(P, c, rrow, mrrow, tg, key, bank_r, bank_mr):
    emit_ln_bcast_w(P, c, rrow, mrrow, tg * TG, TG, key, bank_r, bank_mr)


def emit_xn_tile(P, c, X, ch, t0, w, dst, dcol, bank_r, bank_mr, extra_write=None):
    xr = XK(X, ch * S + t0, ch * S + t0 + w)
    if ch % 2 == 0:
        u = c.sc_u[:, :w]
        P.op("vector", lambda e, o=u, a=X[:, ch * S + t0:ch * S + t0 + w],
             b=c.rb[:, 0:w]: e.tensor_tensor(out=o, in0=a, in1=b, op=ALU.mult),
             reads=[xr, "rbc_r"], writes=[XK(c.sc_u, 0, w)])
        P.op("vector", lambda e, o=dst[:, dcol:dcol + w], a=u, b=c.mrb[:, 0:w]:
             e.tensor_tensor(out=o, in0=a, in1=b, op=ALU.subtract),
             reads=[XK(c.sc_u, 0, w), "rbc_mr"], writes=[XK(dst, dcol, dcol + w)])
    else:
        P.op("gpsimd", lambda e, o=dst[:, dcol:dcol + w],
             a=X[:, ch * S + t0:ch * S + t0 + w], b=c.rb[:, 0:w]:
             e.tensor_tensor(out=o, in0=a, in1=b, op=ALU.mult),
             reads=[xr, "rbc_r"], writes=[XK(dst, dcol, dcol + w)])
        P.op("gpsimd", lambda e, o=dst[:, dcol:dcol + w], a=dst[:, dcol:dcol + w],
             b=c.mrb[:, 0:w]: e.tensor_tensor(out=o, in0=a, in1=b, op=ALU.subtract),
             reads=[XK(dst, dcol, dcol + w), "rbc_mr"],
             writes=[XK(dst, dcol, dcol + w)])


def emit_wload(P, c, sbuf, dram, rows, cols, key, row0=0, col0=0, wid=None,
               spill=True):
    nch = rows // 128
    n = nch * cols
    wid = wid or key
    scr = c.spills.get(wid)
    if scr is None:
        src = dram[row0:row0 + rows, col0:col0 + cols].rearrange(
            "(c p) f -> p c f", p=128)
        P.dma("gpsimd", sbuf[:, :n].rearrange("p (c f) -> p c f", c=nch),
              src, writes=[XK(sbuf, 0, n)])
        if spill:
            scr = P.nc.dram_tensor(f"sp_{wid}", [128, n], BF16,
                                   kind="Internal").ap()
            P.dma("sync", scr, sbuf[:, :n], reads=[XK(sbuf, 0, n)],
                  writes=[f"sp_{wid}"])
            c.spills[wid] = scr
    else:
        h = max(n // 2 // 128 * 128, 128) if n >= 256 else n
        P.dma("sync", sbuf[:, 0:h], scr[:, 0:h], reads=[f"sp_{wid}"],
              writes=[XK(sbuf, 0, h)])
        if h < n:
            P.dma("scalar", sbuf[:, h:n], scr[:, h:n], reads=[f"sp_{wid}"],
                  writes=[XK(sbuf, h, n)])
    return nch


def emit_proj(P, c, wbuf, wkey, nch_in, fout, rhs_buf, rhs_key, rhs_col, wtile,
              epilogue, banks=(0, 1), wr=128):
    nf = (fout + wr - 1) // wr
    for fch in range(nf):
        w0 = fch * wr
        wlen = min(wr, fout - w0)
        bank = banks[fch % len(banks)]
        for ch in range(nch_in):
            rc = rhs_col(ch)
            P.op("tensor", lambda e, o=c.ps[bank][:wlen, :TG],
                 l=wbuf[:, ch * fout + w0: ch * fout + w0 + wlen],
                 r=rhs_buf[:, rc:rc + TG],
                 st=(ch == 0), sp=(ch == nch_in - 1): e.matmul(
                     out=o, lhsT=l, rhs=r, start=st, stop=sp),
                 reads=[XK(wbuf, ch * fout + w0, ch * fout + w0 + wlen),
                        XK(rhs_buf, rc, rc + TG)],
                 writes=[f"ps{bank}"], group=f"{wkey}g{fch}")
        epilogue(fch, bank)


def emit_l1_attn_a(P, c, X):
    nc = P.nc
    emit_ln_rows(P, c, X, 0, c.row_m, c.row_r, c.row_mr, "sa_ln")
    emit_wload(P, c, c.wA, c.d_sa_waT, 1024, 256, "wA")
    emit_wload(P, c, c.wB, c.d_sa_bT, 1024, 256, "wB")
    emit_wload(P, c, c.wC, c.d_sa_cT, 256, 1024, "wC")
    emit_wload(P, c, c.wD, c.d_sa_dT, 1024, 1024, "wD")
    P.dma("sync", c.colA[:, 0:2], c.d_sa_wab[:, 0:2], writes=["colA"])

    for tg in range(NTG):
        t0 = tg * TG
        emit_ln_bcast(P, c, c.row_r, c.row_mr, tg, "sa_ln", 5, 6)
        for ch in range(NCH):
            emit_xn_tile(P, c, X, ch, t0, TG, c.xn, ch * TG, 5, 6)
        def ep_a(fch, bank, t0=t0):
            P.op("scalar", lambda e, o=c.bufA[:, fch * S + t0: fch * S + t0 + TG],
                 i=c.ps[bank][:, :TG], b=c.colA[:, fch:fch + 1]: e.activation(
                     out=o, in_=i, func=AF.Sigmoid, bias=b),
                 reads=[f"ps{bank}", "colA"],
                 writes=[XK(c.bufA, fch * S + t0, fch * S + t0 + TG)])
        emit_proj(P, c, c.wA, f"wA{tg}", NCH, 256, c.xn, "xn",
                  lambda ch: ch * TG, 128, ep_a, banks=(0, 1))

        def ep_b(fch, bank, t0=t0):
            P.op("scalar", lambda e, o=c.bufB[:, fch * S + t0: fch * S + t0 + TG],
                 i=c.ps[bank][:, :TG]: e.activation(out=o, in_=i, func=AF.Identity),
                 reads=[f"ps{bank}"],
                 writes=[XK(c.bufB, fch * S + t0, fch * S + t0 + TG)])
        emit_proj(P, c, c.wB, f"wB{tg}", NCH, 256, c.xn, "xn",
                  lambda ch: ch * TG, 128, ep_b, banks=(2, 3))
        def ep_d(fch, bank, t0=t0):
            P.op("vector", lambda e, o=X[:, fch * S + t0: fch * S + t0 + TG],
                 a=X[:, fch * S + t0: fch * S + t0 + TG], b=c.ps[bank][:, :TG]:
                 e.tensor_tensor(out=o, in0=a, in1=b, op=ALU.add),
                 reads=[f"ps{bank}", XK(X, fch * S + t0, fch * S + t0 + TG)],
                 writes=[XK(X, fch * S + t0, fch * S + t0 + TG)])
        emit_proj(P, c, c.wD, f"wD{tg}", NCH, 1024, c.xn, "xn",
                  lambda ch: ch * TG, 128, ep_d, banks=(0, 1, 2, 3))

    for ch in range(2):
        P.op("vector", lambda e, o=c.bufH[:, ch * S:(ch + 1) * S],
             a=c.bufA[:, ch * S:(ch + 1) * S], b=c.bufB[:, ch * S:(ch + 1) * S]:
             e.tensor_tensor_scan(out=o, data0=a, data1=b, initial=0.0,
                                  op0=ALU.mult, op1=ALU.add),
             reads=[XK(c.bufA, ch * S, (ch + 1) * S), XK(c.bufB, ch * S, (ch + 1) * S)],
             writes=[XK(c.bufH, ch * S, (ch + 1) * S)])

    for tg in range(NTG):
        t0 = tg * TG

        def ep_c(fch, bank, t0=t0):
            P.op("vector", lambda e, o=X[:, fch * S + t0: fch * S + t0 + TG],
                 a=X[:, fch * S + t0: fch * S + t0 + TG], b=c.ps[bank][:, :TG]:
                 e.tensor_tensor(out=o, in0=a, in1=b, op=ALU.add),
                 reads=[f"ps{bank}", XK(X, fch * S + t0, fch * S + t0 + TG)],
                 writes=[XK(X, fch * S + t0, fch * S + t0 + TG)])
        emit_proj(P, c, c.wC, f"wC{tg}", 2, 1024, c.bufH, "bufH",
                  lambda ch, t0=t0: ch * S + t0, 128, ep_c, banks=(0, 1, 2, 3))


def alloc_ctx(nc, stk, need):
    c = Ctx()
    c.spills = {}
    sb = lambda name, shape, dt: stk.enter_context(nc.sbuf_tensor(name, shape, dt))
    psa = lambda name, shape, dt: stk.enter_context(nc.psum_tensor(name, shape, dt))
    c.X = sb("Xres", [128, NCH * S], F32)
    c.Wp1 = sb("Wp1", [128, 8192], BF16)
    c.Wp2 = sb("Wp2", [128, 8192], BF16)
    c.Wp3 = sb("Wp3", [128, 8192], BF16)
    c.B1 = sb("B1", [128, 16384], BF16)
    c.B2 = sb("B2", [128, 16384], BF16)
    c.ps = [psa(f"pb{i}", [128, 512], F32) for i in range(7)]
    c.psb = psa("pb7b", [128, 1024], BF16)
    c.xn = sb("xn", [128, NCH * TG], BF16)
    c.sc_u = sb("sc_u", [128, TG], F32)
    c.RBC = sb("RBC", [128, 1024], F32)
    c.rb = Rv(c.RBC, 0, 512)
    c.mrb = Rv(c.RBC, 512, 512)
    c.ROWS = sb("ROWS", [128, S], F32)
    c.row_m = None
    c.row_r = Rv(c.ROWS, 0, S, p0=32, np_=1)
    c.row_mr = Rv(c.ROWS, 0, S, p0=64, np_=1)
    c.ROWB = sb("ROWB", [128, TG], F32)
    c.row_a = Rv(c.ROWB, 0, TG, p0=0, np_=1)
    c.row_b = Rv(c.ROWB, 0, TG, p0=32, np_=1)
    c.row_c = Rv(c.ROWB, 0, TG, p0=64, np_=1)
    c.onesc = sb("onesc", [128, 1], BF16)
    c.eps_ln = sb("eps_ln", [1, 1], F32)
    c.eps_rms = sb("eps_rms", [1, 1], F32)
    c.ONESF = sb("ONESF", [128, 128], F32)
    c.ONESB = sb("ONESB", [128, 128], BF16)
    c.colA = sb("colA", [128, 32], F32)
    c.colB = sb("colB", [128, 32], F32)
    c.CB = sb("CB", [128, 640], BF16)
    c.causal = Rv(c.CB, 0, 512)
    c.identb = Rv(c.CB, 512, 128)
    c.row_r2 = Rv(c.ROWS, 0, S, p0=0, np_=1)
    c.CB2 = sb("CB2", [128, 1600], BF16)
    c.kb_row = Rv(c.CB2, 0, 1024, p0=0, np_=1)
    c.vb_row = Rv(c.CB2, 0, 1024, p0=32, np_=1)
    c.bv_row = Rv(c.CB2, 0, 1024, p0=64, np_=1)
    c.eb_row = Rv(c.CB2, 1024, 8, p0=0, np_=1)
    c.den_bf = Rv(c.CB2, 1056, 512, p0=0, np_=1)
    c.drep = Rv(c.CB2, 1056, 512)
    c.colC = sb("colC", [128, 16], F32)
    c.KVF = sb("KVF", [128, 520], F32)
    c.CE8 = Rv(c.B2, 12288, 1024, p0=0, np_=8)
    c.COMB = Rv(c.B2, 14336, S, p0=0, np_=8)
    c.wA = Rv(c.Wp1, 0, 2048)
    c.wB = Rv(c.Wp1, 2048, 2048)
    c.wC = Rv(c.Wp1, 4096, 2048)
    c.wD = Rv(c.Wp2, 0, 8192)
    c.bufA = Rv(c.B1, 0, 4096)
    c.bufB = Rv(c.B1, 4096, 4096)
    c.bufH = Rv(c.B1, 8192, 4096)
    c.Mv = Rv(c.B2, 0, 8192)
    c.m_built = False
    c.m_resident = False
    return c


def emit_const_init(P, c, stages):
    P.op("gpsimd", lambda e: e.memset(c.onesc[:], 1.0), writes=["onesc"])
    P.op("gpsimd", lambda e: e.memset(c.ONESF[:], 1.0), writes=["ones1f"])
    P.op("gpsimd", lambda e: e.memset(c.ONESB[:], 1.0), writes=["ones1b"])
    P.op("gpsimd", lambda e: e.memset(c.eps_ln[:], EPS_LN), writes=["consts"])
    P.op("gpsimd", lambda e: e.memset(c.eps_rms[:], EPS_RMS), writes=["consts"])
    st = set(stages)
    if "l1_attn_b" in st:
        P.dma("gpsimd", c.causal[:, :], c.d_c_causal, writes=[(c.causal, 0, 512)])
        P.dma("gpsimd", c.kb_row[:, :], c.d_lb_kb_row, writes=["kb_row"])
        P.dma("gpsimd", c.vb_row[:, :], c.d_lb_vb_row, writes=["vb_row"])
    if "l0_attn_a" in st:
        P.dma("gpsimd", c.bv_row[:, :], c.d_aa_bv_row, writes=["bv_row"])
    if "l0_ffn" in st:
        P.dma("gpsimd", c.identb[:, :], c.d_c_identb, writes=[(c.identb, 0, 128)])
    if "l0_ffn" in st:
        P.dma("gpsimd", c.eb_row[:, :], c.d_f_ebias, writes=["eb_row"])


def emit_m_build(P, c):
    wr = Rv(c.Wp1, 0, 8192)
    wi = Rv(c.Wp2, 0, 8192)
    Fr = Rv(c.Wp3, 0, 8192)
    Fin = Rv(c.B1, 0, 8192)
    TrB = Rv(c.B1, 8192, 8192)
    TiB = Rv(c.B2, 8192, 8192)
    emit_wload(P, c, wr, c.d_lf_wrT, 1024, 1024, "wr", spill=False)
    emit_wload(P, c, wi, c.d_lf_wiT, 1024, 1024, "wi", spill=False)
    emit_wload(P, c, Fr, c.d_c_fr, 1024, 1024, "fr", spill=False)
    emit_wload(P, c, Fin, c.d_c_fineg, 1024, 1024, "fin", spill=False)
    gq = [Rv(c.xn, 0, 1024), Rv(c.xn, 1024, 1024), Rv(c.xn, 2048, 1024)]
    for oq in range(4):
        o0 = oq * 256
        for kh in range(2):
            for k2 in range(8):
                for gi, dsrc in enumerate((c.d_c_gr, c.d_c_gi, c.d_c_gineg)):
                    P.dma("gpsimd", gq[gi][:, :256],
                          dsrc[k2 * 128:(k2 + 1) * 128, o0:o0 + 256],
                          writes=[(gq[gi], 0, 256)])
                for kb in range(4):
                    kc = kh * 4 + kb
                    wcol = k2 * 1024 + kc * 128
                    P.op("tensor", lambda e, o=c.ps[kb][:, 0:256],
                         l=wr[:, wcol:wcol + 128], r=gq[0][:, :256],
                         st=(k2 == 0): e.matmul(out=o, lhsT=l, rhs=r, start=st, stop=False),
                         reads=[(wr, wcol, wcol + 128), (gq[0], 0, 256)],
                         writes=[f"ps{kb}"], group=f"tr{oq}_{kc}")
                    P.op("tensor", lambda e, o=c.ps[kb][:, 0:256],
                         l=wi[:, wcol:wcol + 128], r=gq[2][:, :256]: e.matmul(
                             out=o, lhsT=l, rhs=r, start=False, stop=False),
                         reads=[(wi, wcol, wcol + 128), (gq[2], 0, 256)],
                         writes=[f"ps{kb}"], group=f"tr{oq}_{kc}")
                    P.op("tensor", lambda e, o=c.ps[kb][:, 256:512],
                         l=wr[:, wcol:wcol + 128], r=gq[1][:, :256]: e.matmul(
                             out=o, lhsT=l, rhs=r, start=False, stop=False),
                         reads=[(wr, wcol, wcol + 128), (gq[1], 0, 256)],
                         writes=[f"ps{kb}"], group=f"tr{oq}_{kc}")
                    P.op("tensor", lambda e, o=c.ps[kb][:, 256:512],
                         l=wi[:, wcol:wcol + 128], r=gq[0][:, :256],
                         sp=(k2 == 7): e.matmul(out=o, lhsT=l, rhs=r, start=False, stop=sp),
                         reads=[(wi, wcol, wcol + 128), (gq[0], 0, 256)],
                         writes=[f"ps{kb}"], group=f"tr{oq}_{kc}")
            for kb in range(4):
                kc = kh * 4 + kb
                P.op("vector", lambda e, o=TrB[:, kc * 1024 + o0: kc * 1024 + o0 + 256],
                     i=c.ps[kb][:, 0:256]: e.tensor_copy(out=o, in_=i),
                     reads=[f"ps{kb}"], writes=[(TrB, kc * 1024 + o0, kc * 1024 + o0 + 256)])
                P.op("vector", lambda e, o=TiB[:, kc * 1024 + o0: kc * 1024 + o0 + 256],
                     i=c.ps[kb][:, 256:512]: e.tensor_copy(out=o, in_=i),
                     reads=[f"ps{kb}"], writes=[(TiB, kc * 1024 + o0, kc * 1024 + o0 + 256)])
    for oh in range(2):
        o0 = oh * 512
        for dch in range(8):
            bank = dch % 4
            for kc in range(8):
                P.op("tensor", lambda e, o=c.ps[bank][:, :512],
                     l=Fr[:, kc * 1024 + dch * 128: kc * 1024 + dch * 128 + 128],
                     r=TrB[:, kc * 1024 + o0: kc * 1024 + o0 + 512],
                     st=(kc == 0): e.matmul(out=o, lhsT=l, rhs=r, start=st, stop=False),
                     reads=[(Fr, kc * 1024 + dch * 128, kc * 1024 + dch * 128 + 128),
                            (TrB, kc * 1024 + o0, kc * 1024 + o0 + 512)],
                     writes=[f"ps{bank}"], group=f"mb{oh}_{dch}")
                P.op("tensor", lambda e, o=c.ps[bank][:, :512],
                     l=Fin[:, kc * 1024 + dch * 128: kc * 1024 + dch * 128 + 128],
                     r=TiB[:, kc * 1024 + o0: kc * 1024 + o0 + 512],
                     sp=(kc == 7): e.matmul(out=o, lhsT=l, rhs=r, start=False, stop=sp),
                     reads=[(Fin, kc * 1024 + dch * 128, kc * 1024 + dch * 128 + 128),
                            (TiB, kc * 1024 + o0, kc * 1024 + o0 + 512)],
                     writes=[f"ps{bank}"], group=f"mb{oh}_{dch}")
            P.op("vector", lambda e, o=c.Mv[:, dch * 1024 + o0: dch * 1024 + o0 + 512],
                 i=c.ps[bank][:, :512]: e.tensor_copy(out=o, in_=i),
                 reads=[f"ps{bank}"],
                 writes=[(c.Mv, dch * 1024 + o0, dch * 1024 + o0 + 512)])
    P.dma("sync", c.d_mscr, c.Mv[:, :], reads=[(c.Mv, 0, 8192)], writes=["mscr"])
    if DEBUG_DUMP_M and hasattr(c, "d_mdump"):
        P.dma("sync", c.d_mdump, c.Mv[:, :], reads=[(c.Mv, 0, 8192)], writes=["mdump"])
    c.m_built = True
    c.m_resident = True


def emit_l1_ffn(P, c, X):
    emit_ln_rows(P, c, X, 0, c.row_m, c.row_r, c.row_mr, "lf_ln")
    if not c.m_built:
        emit_m_build(P, c)
    elif not c.m_resident:
        P.dma("sync", c.Mv[:, :], c.d_mscr, reads=["mscr"], writes=[(c.Mv, 0, 8192)])
        c.m_resident = True
    P.dma("sync", c.colA[:, 0:8], c.d_lf_bias[:, 0:8], writes=["colA"])
    for tg in range(NTG):
        t0 = tg * TG
        emit_ln_bcast(P, c, c.row_r, c.row_mr, tg, "lf_ln", 5, 6)
        for ch in range(NCH):
            emit_xn_tile(P, c, X, ch, t0, TG, c.xn, ch * TG, 5, 6)

        def ep(fch, bank, t0=t0):
            P.op("vector", lambda e, o=X[:, fch * S + t0: fch * S + t0 + TG],
                 i=c.ps[bank][:, :TG], s=c.colA[:, fch:fch + 1],
                 x=X[:, fch * S + t0: fch * S + t0 + TG]: e.scalar_tensor_tensor(
                     out=o, in0=i, scalar=s, in1=x, op0=ALU.add, op1=ALU.add),
                 reads=[f"ps{bank}", "colA", XK(X, fch * S + t0, fch * S + t0 + TG)],
                 writes=[XK(X, fch * S + t0, fch * S + t0 + TG)])
        emit_proj(P, c, c.Mv, f"lfm{tg}", NCH, 1024, c.xn, "xn",
                  lambda ch: ch * TG, 128, ep, banks=(0, 1, 2, 3))


def emit_l0_ffn(P, c, X, inv):
    K = f"f{inv}"
    SCALE = 1.0 / 16.0
    c.m_resident = False
    emit_ln_rows(P, c, X, 0, c.row_m, c.row_r, c.row_mr, K + "ln")
    KCB = Rv(c.B1, 0, 4096)
    VCB = Rv(c.B1, 4096, 4096)
    AB = Rv(c.B1, 10240, 1024)
    W1B = Rv(c.B1, 0, 16384)
    NX2 = Rv(c.B2, 0, 16384)
    OB = Rv(c.B2, 0, 16384)
    G1S = Rv(c.B1, 0, 4096)
    G3S = Rv(c.B1, 4096, 8192 - 4096)
    CBT = Rv(c.B1, 15360, 512)
    COMB = Rv(c.B1, 12288, 2048, p0=0, np_=8)
    CE8 = Rv(c.B1, 14336, 1024, p0=0, np_=8)
    emit_wload(P, c, Rv(c.Wp1, 0, 8192), c.d_f_wqT, 1024, 1024, K + "wq",
               wid="fwq")
    emit_wload(P, c, Rv(c.Wp2, 0, 2048), c.d_f_wkcT, 1024, 256, K + "wkc",
               wid="fwkc")
    emit_wload(P, c, Rv(c.Wp2, 2048, 2048), c.d_f_wvcT, 1024, 256, K + "wvc",
               wid="fwvc")
    emit_wload(P, c, Rv(c.Wp3, 0, 8192), c.d_f_woT, 1024, 1024, K + "wo",
               wid="fwo")
    P.dma("sync", c.colA[:, 0:8], c.d_f_g1[:, 0:8], writes=["colA"])
    P.dma("sync", c.colA[:, 8:16], c.d_f_g2[:, 0:8], writes=["colA"])

    for tg in range(NTG):
        t0 = tg * TG
        emit_ln_bcast(P, c, c.row_r, c.row_mr, tg, K + "ln", 5, 6)
        for ch in range(NCH):
            emit_xn_tile(P, c, X, ch, t0, TG, c.xn, ch * TG, 5, 6)
        for ch in range(NCH):
            P.op("scalar", lambda e, o=c.sc_u[:, :TG], i=c.xn[:, ch * TG: ch * TG + TG]:
                 e.activation(out=o, in_=i, func=AF.Square),
                 reads=[XK(c.xn, ch * TG, ch * TG + TG)], writes=[XK(c.sc_u, 0, TG)])
            P.op("tensor", lambda e, o=c.ps[4][0:1, :TG], l=c.ONESF[:, 0:1], r=c.sc_u[:, :TG],
                 st=(ch == 0), sp=(ch == NCH - 1): e.matmul(out=o, lhsT=l, rhs=r,
                                                            start=st, stop=sp),
                 reads=[XK(c.sc_u, 0, TG), "onesc"], writes=["ps4"],
                 group=K + f"rms{tg}")
        va = c.row_a[:, :TG]
        P.op("scalar", lambda e, o=va, i=c.ps[4][0:1, :TG], b=c.eps_rms[0:1, 0:1]:
             e.activation(out=o, in_=i, func=AF.Ln, bias=b, scale=1.0 / D),
             reads=["ps4", "consts"], writes=["row_a"])
        P.op("scalar", lambda e, o=c.row_r2[:, t0:t0 + TG], i=va: e.activation(
            out=o, in_=i, func=AF.Exp, scale=-0.5),
            reads=["row_a"], writes=["row_r2"])
        P.op("tensor", lambda e, o=c.ps[4][:, :TG], l=c.ONESF[0:1, :],
             r=c.row_r2[:, t0:t0 + TG]: e.matmul(out=o, lhsT=l, rhs=r,
                                                 start=True, stop=True),
             reads=["row_r2", "ones1f"], writes=["ps4"])
        for ch in range(NCH):
            P.op("vector", lambda e, o=c.xn[:, ch * TG: ch * TG + TG],
                 i=c.xn[:, ch * TG: ch * TG + TG], s=c.colA[:, ch:ch + 1],
                 b=c.ps[4][:, :TG]: e.scalar_tensor_tensor(
                     out=o, in0=i, scalar=s, in1=b, op0=ALU.mult, op1=ALU.mult),
                 reads=[XK(c.xn, ch * TG, ch * TG + TG), "colA", "ps4"],
                 writes=[XK(c.xn, ch * TG, ch * TG + TG)])
        for f2 in range(2):
            bank = f2
            for ch in range(NCH):
                P.op("tensor", lambda e, o=c.ps[bank][:, :TG],
                     l=c.Wp2[:, ch * 256 + f2 * 128: ch * 256 + f2 * 128 + 128],
                     r=c.xn[:, ch * TG: ch * TG + TG],
                     st=(ch == 0), sp=(ch == NCH - 1): e.matmul(
                         out=o, lhsT=l, rhs=r, start=st, stop=sp),
                     reads=[(c.Wp2, ch * 256 + f2 * 128, ch * 256 + f2 * 128 + 128),
                            XK(c.xn, ch * TG, ch * TG + TG)],
                     writes=[f"ps{bank}"], group=K + f"kc{tg}_{f2}")
            P.op("scalar", lambda e, o=KCB[:, f2 * 2048 + t0: f2 * 2048 + t0 + TG],
                 i=c.ps[bank][:, :TG]: e.activation(out=o, in_=i, func=AF.Identity),
                 reads=[f"ps{bank}"],
                 writes=[(KCB, f2 * 2048 + t0, f2 * 2048 + t0 + TG)])
        for ti in range(4):
            tt = tg * 4 + ti
            bank = 2 + ti % 2
            for ch in range(NCH):
                P.op("tensor", lambda e, o=c.ps[bank][:, :256],
                     l=c.xn[:, ch * TG + ti * 128: ch * TG + ti * 128 + 128],
                     r=c.Wp2[:, 2048 + ch * 256: 2048 + ch * 256 + 256],
                     st=(ch == 0), sp=(ch == NCH - 1): e.matmul(
                         out=o, lhsT=l, rhs=r, start=st, stop=sp),
                     reads=[(c.Wp2, 2048 + ch * 256, 2048 + ch * 256 + 256),
                            XK(c.xn, ch * TG + ti * 128, ch * TG + ti * 128 + 128)],
                     writes=[f"ps{bank}"], group=K + f"vc{tt}")
            P.op("scalar", lambda e, o=VCB[:, tt * 256: tt * 256 + 256],
                 i=c.ps[bank][:, :256]: e.activation(out=o, in_=i, func=AF.Identity),
                 reads=[f"ps{bank}"], writes=[(VCB, tt * 256, tt * 256 + 256)])
        for f2 in range(2):
            P.op("scalar", lambda e, o=c.sc_u[:, :TG],
                 i=KCB[:, f2 * 2048 + t0: f2 * 2048 + t0 + TG]: e.activation(
                     out=o, in_=i, func=AF.Square),
                 reads=[(KCB, f2 * 2048 + t0, f2 * 2048 + t0 + TG)],
                 writes=[XK(c.sc_u, 0, TG)])
            P.op("tensor", lambda e, o=c.ps[0][:, :TG], l=c.ONESF[:, 0:128],
                 r=c.sc_u[:, :TG], st=(f2 == 0), sp=(f2 == 1): e.matmul(
                     out=o, lhsT=l, rhs=r, start=st, stop=sp),
                 reads=[XK(c.sc_u, 0, TG), "onesc"], writes=["ps0"],
                 group=K + f"kn{tg}")
        P.op("vector", lambda e, o=c.colB[:, 0:1], i=c.ps[0][:, :TG]:
             e.reduce_max(out=o, in_=i, axis=AX.X),
             reads=["ps0"], writes=["colB1"])
        if tg == 0:
            P.op("vector", lambda e, o=c.colB[:, 1:2], i=c.colB[:, 0:1]:
                 e.tensor_copy(out=o, in_=i), reads=["colB1"], writes=["colB2"])
        else:
            P.op("vector", lambda e, o=c.colB[:, 1:2], a=c.colB[:, 0:1],
                 b=c.colB[:, 1:2]: e.tensor_tensor(out=o, in0=a, in1=b, op=ALU.max),
                 reads=["colB1", "colB2"], writes=["colB2"])

    for qg in range(NTG):
        t0 = qg * TG
        emit_ln_bcast(P, c, c.row_r, c.row_mr, qg, K + "ln", 5, 6)
        P.op("tensor", lambda e, o=c.ps[4][:, :TG], l=c.ONESF[0:1, :],
             r=c.row_r2[:, t0:t0 + TG]: e.matmul(out=o, lhsT=l, rhs=r,
                                                 start=True, stop=True),
             reads=["row_r2", "ones1f"], writes=["ps4"])
        for ch in range(NCH):
            emit_xn_tile(P, c, X, ch, t0, TG, c.xn, ch * TG, 5, 6)
            P.op("vector", lambda e, o=c.xn[:, ch * TG: ch * TG + TG],
                 i=c.xn[:, ch * TG: ch * TG + TG], s=c.colA[:, ch:ch + 1],
                 b=c.ps[4][:, :TG]: e.scalar_tensor_tensor(
                     out=o, in0=i, scalar=s, in1=b, op0=ALU.mult, op1=ALU.mult),
                 reads=[XK(c.xn, ch * TG, ch * TG + TG), "colA", "ps4"],
                 writes=[XK(c.xn, ch * TG, ch * TG + TG)])
        QB2 = [Rv(c.B1, 8192, 1024), Rv(c.B1, 9216, 1024)]

        def prep_a(h, sl):
            for f2 in range(2):
                for ch in range(NCH):
                    wc = ch * 1024 + h * 256 + f2 * 128
                    P.op("tensor", lambda e, o=c.ps[6][:, :TG],
                         l=c.Wp1[:, wc:wc + 128], r=c.xn[:, ch * TG: ch * TG + TG],
                         st=(ch == 0), sp=(ch == NCH - 1): e.matmul(
                             out=o, lhsT=l, rhs=r, start=st, stop=sp),
                         reads=[(c.Wp1, wc, wc + 128),
                                XK(c.xn, ch * TG, ch * TG + TG)],
                         writes=["ps6"], group=K + f"q{qg}_{h}_{f2}")
                P.op("scalar", lambda e, o=QB2[sl][:, f2 * TG: f2 * TG + TG],
                     i=c.ps[6][:, :TG]: e.activation(out=o, in_=i, func=AF.Identity),
                     reads=["ps6"], writes=[(QB2[sl], f2 * TG, f2 * TG + TG)])
        def prep_b(h, sl):
            for f2 in range(2):
                P.op("scalar", lambda e, o=c.sc_u[:, :TG],
                     i=QB2[sl][:, f2 * TG: f2 * TG + TG]: e.activation(
                         out=o, in_=i, func=AF.Square),
                     reads=[(QB2[sl], f2 * TG, f2 * TG + TG)], writes=[XK(c.sc_u, 0, TG)])
                P.op("tensor", lambda e, o=c.ps[6][:, :TG], l=c.ONESF[:, 0:128],
                     r=c.sc_u[:, :TG], st=(f2 == 0), sp=(f2 == 1): e.matmul(
                         out=o, lhsT=l, rhs=r, start=st, stop=sp),
                     reads=[XK(c.sc_u, 0, TG), "ones1f"], writes=["ps6"],
                     group=K + f"qn{qg}_{h}")
            P.op("vector", lambda e, o=c.colB[:, 2:3], i=c.ps[6][:, :TG]:
                 e.reduce_max(out=o, in_=i, axis=AX.X), reads=["ps6"], writes=["colB3"])
            P.op("vector", lambda e, o=c.colB[:, 3:4], a=c.colB[:, 2:3],
                 b=c.colB[:, 1:2]: e.tensor_tensor(out=o, in0=a, in1=b, op=ALU.mult),
                 reads=["colB3", "colB2"], writes=["colB4"])
            P.op("scalar", lambda e, o=c.colB[:, 3:4], i=c.colB[:, 3:4]:
                 e.activation(out=o, in_=i, func=AF.Ln),
                 reads=["colB4"], writes=["colB4"])
            P.op("scalar", lambda e, o=c.colB[:, 3:4], i=c.colB[:, 3:4]:
                 e.activation(out=o, in_=i, func=AF.Exp, scale=0.5),
                 reads=["colB4"], writes=["colB4"])
            P.op("vector", lambda e, o=c.colC[:, 8 + sl:9 + sl], i=c.colB[:, 3:4]:
                 e.tensor_scalar_mul(out=o, in0=i, scalar1=-SCALE),
                 reads=["colB4"], writes=[f"negc{sl}"])

        def attend(h, sl, hook0=None, hook1=None, hook2=None):
            dn = 4 + (h % 2)
            db = dn

            def st_mm(kt):
                sb_ = kt % 2
                for ch2 in range(2):
                    kc_col = ch2 * 2048 + kt * 128
                    P.op("tensor", lambda e, o=c.ps[sb_][:, :TG],
                         l=KCB[:, kc_col:kc_col + 128],
                         r=QB2[sl][:, ch2 * TG: ch2 * TG + TG],
                         st=(ch2 == 0), sp=(ch2 == 1): e.matmul(
                             out=o, lhsT=l, rhs=r, start=st, stop=sp),
                         reads=[(KCB, kc_col, kc_col + 128),
                                (QB2[sl], ch2 * TG, ch2 * TG + TG)],
                         writes=[f"ps{sb_}"], group=K + f"st{qg}_{h}_{kt}")
                P.op("scalar", lambda e, o=AB[:, sb_ * TG: sb_ * TG + TG],
                     i=c.ps[sb_][:, :TG], b=c.colC[:, 8 + sl:9 + sl]: e.activation(
                         out=o, in_=i, func=AF.Exp, bias=b, scale=SCALE),
                     reads=[f"ps{sb_}", f"negc{sl}"],
                     writes=[(AB, sb_ * TG, sb_ * TG + TG)])

            def av_mm(kt):
                sb_ = kt % 2
                for vh in range(2):
                    P.op("tensor", lambda e, o=c.ps[2 + vh][:, :TG],
                         l=VCB[:, kt * 256 + vh * 128: kt * 256 + vh * 128 + 128],
                         r=AB[:, sb_ * TG: sb_ * TG + TG],
                         st=(kt == 0), sp=(kt == 15): e.matmul(
                             out=o, lhsT=l, rhs=r, start=st, stop=sp),
                         reads=[(VCB, kt * 256 + vh * 128, kt * 256 + vh * 128 + 128),
                                (AB, sb_ * TG, sb_ * TG + TG)],
                         writes=[f"ps{2 + vh}"], group=K + f"o{qg}_{h}_{vh}")
                P.op("tensor", lambda e, o=c.ps[dn][:, :TG], l=c.ONESB[:, 0:128],
                     r=AB[:, sb_ * TG: sb_ * TG + TG],
                     st=(kt == 0), sp=(kt == 15): e.matmul(
                         out=o, lhsT=l, rhs=r, start=st, stop=sp),
                     reads=[(AB, sb_ * TG, sb_ * TG + TG), "onesc"],
                     writes=[f"ps{dn}"], group=K + f"dn{qg}_{h}")

            st_mm(0)
            st_mm(1)
            if hook0:
                hook0()
            av_mm(0)
            nxt = 2
            done = 1
            while done < 5:
                if nxt < 16:
                    st_mm(nxt)
                    nxt += 1
                av_mm(done)
                done += 1
            if hook1:
                hook1()
            while done < 9:
                st_mm(nxt)
                nxt += 1
                av_mm(done)
                done += 1
            if hook2:
                hook2()
            while done < 16:
                if nxt < 16:
                    st_mm(nxt)
                    nxt += 1
                av_mm(done)
                done += 1

            P.op("vector", lambda e, o=c.drep[:, 0:TG], i=c.ps[dn][:, :TG]:
                 _recip_bf16(e, o, i), reads=[f"ps{dn}"], writes=["row_a"])

            def epilogue():
                for vh in range(2):
                    P.op("scalar", lambda e, o=QB2[sl][:, vh * TG: vh * TG + TG],
                         i=c.ps[2 + vh][:, :TG]: e.activation(out=o, in_=i,
                                                              func=AF.Identity),
                         reads=[f"ps{2 + vh}"], writes=[(QB2[sl], vh * TG, vh * TG + TG)])
                    oc = (2 * h + vh) * 2048 + t0
                    P.op("vector", lambda e, o=OB[:, oc:oc + TG],
                         a=QB2[sl][:, vh * TG: vh * TG + TG],
                         b=c.drep[:, 0:TG]: e.tensor_tensor(out=o, in0=a, in1=b,
                                                            op=ALU.mult),
                         reads=[(QB2[sl], vh * TG, vh * TG + TG), "row_a"],
                         writes=[(OB, oc, oc + TG)])
            return epilogue

        prep_a(0, 0)
        prep_b(0, 0)
        pend = None
        for h in range(4):
            if h + 1 < 4:
                h1, s1 = h + 1, (h + 1) % 2
                pend = attend(h, h % 2, hook0=pend,
                              hook1=(lambda h1=h1, s1=s1: prep_a(h1, s1)),
                              hook2=(lambda h1=h1, s1=s1: prep_b(h1, s1)))
            else:
                pend = attend(h, h % 2, hook0=pend)
        pend()
    for tg in range(NTG):
        t0 = tg * TG
        emit_ln_bcast(P, c, c.row_r, c.row_mr, tg, K + "ln", 5, 6)
        for fch in range(NCH):
            bank = fch % 4
            for ch in range(NCH):
                wc = ch * 1024 + fch * 128
                P.op("tensor", lambda e, o=c.ps[bank][:, :TG],
                     l=c.Wp3[:, wc:wc + 128], r=OB[:, ch * 2048 + t0: ch * 2048 + t0 + TG],
                     st=(ch == 0), sp=(ch == NCH - 1): e.matmul(
                         out=o, lhsT=l, rhs=r, start=st, stop=sp),
                     reads=[(c.Wp3, wc, wc + 128), (OB, ch * 2048 + t0, ch * 2048 + t0 + TG)],
                     writes=[f"ps{bank}"], group=K + f"wo{tg}_{fch}")
            xc = fch * S + t0
            P.op("vector", lambda e, o=c.sc_u[:, :TG], a=X[:, xc:xc + TG],
                 b=c.rb[:, 0:TG]: e.tensor_tensor(out=o, in0=a, in1=b, op=ALU.mult),
                 reads=[XK(X, xc, xc + TG), "rbc_r"], writes=[XK(c.sc_u, 0, TG)])
            P.op("vector", lambda e, o=c.sc_u[:, :TG], a=c.sc_u[:, :TG],
                 b=c.mrb[:, 0:TG]: e.tensor_tensor(out=o, in0=a, in1=b, op=ALU.subtract),
                 reads=[XK(c.sc_u, 0, TG), "rbc_mr"], writes=[XK(c.sc_u, 0, TG)])
            P.op("vector", lambda e, o=c.sc_u[:, :TG], a=c.sc_u[:, :TG],
                 b=c.ps[bank][:, :TG]: e.tensor_tensor(out=o, in0=a, in1=b, op=ALU.add),
                 reads=[XK(c.sc_u, 0, TG), f"ps{bank}"], writes=[XK(c.sc_u, 0, TG)])
            P.op("vector", lambda e, o=X[:, xc:xc + TG], a=X[:, xc:xc + TG],
                 b=c.sc_u[:, :TG]: e.tensor_tensor(out=o, in0=a, in1=b, op=ALU.add),
                 reads=[XK(c.sc_u, 0, TG), XK(X, xc, xc + TG)],
                 writes=[XK(X, xc, xc + TG)])
            P.op("vector", lambda e, o=W1B[:, xc:xc + TG], i=c.sc_u[:, :TG]:
                 e.tensor_copy(out=o, in_=i),
                 reads=[XK(c.sc_u, 0, TG)], writes=[(W1B, xc, xc + TG)])

    for tg in range(NTG):
        t0 = tg * TG
        for ch in range(NCH):
            P.op("scalar", lambda e, o=c.sc_u[:, :TG],
                 i=W1B[:, ch * S + t0: ch * S + t0 + TG]: e.activation(
                     out=o, in_=i, func=AF.Square),
                 reads=[(W1B, ch * S + t0, ch * S + t0 + TG)],
                 writes=[XK(c.sc_u, 0, TG)])
            P.op("tensor", lambda e, o=c.ps[4][0:1, :TG], l=c.ONESF[:, 0:1],
                 r=c.sc_u[:, :TG], st=(ch == 0), sp=(ch == NCH - 1): e.matmul(
                     out=o, lhsT=l, rhs=r, start=st, stop=sp),
                 reads=[XK(c.sc_u, 0, TG), "onesc"], writes=["ps4"],
                 group=K + f"r2m{tg}")
        va = c.row_a[:, :TG]
        P.op("scalar", lambda e, o=va, i=c.ps[4][0:1, :TG], b=c.eps_rms[0:1, 0:1]:
             e.activation(out=o, in_=i, func=AF.Ln, bias=b, scale=1.0 / D),
             reads=["ps4", "consts"], writes=["row_a"])
        P.op("scalar", lambda e, o=va, i=va: e.activation(out=o, in_=i,
                                                          func=AF.Exp, scale=-0.5),
             reads=["row_a"], writes=["row_a"])
        P.op("tensor", lambda e, o=c.ps[5][:, :TG], l=c.ONESF[0:1, :], r=va:
             e.matmul(out=o, lhsT=l, rhs=r, start=True, stop=True),
             reads=["row_a", "ones1f"], writes=["ps5"])
        for ch in range(NCH):
            P.op("vector", lambda e, o=NX2[:, ch * S + t0: ch * S + t0 + TG],
                 i=W1B[:, ch * S + t0: ch * S + t0 + TG], s=c.colA[:, 8 + ch:9 + ch],
                 b=c.ps[5][:, :TG]: e.scalar_tensor_tensor(
                     out=o, in0=i, scalar=s, in1=b, op0=ALU.mult, op1=ALU.mult),
                 reads=[(W1B, ch * S + t0, ch * S + t0 + TG), "colA", "ps5"],
                 writes=[(NX2, ch * S + t0, ch * S + t0 + TG)])

    P.dma("gpsimd", c.xn[:, 0:64].rearrange("p (c f) -> p c f", c=8),
          c.d_f_wrT.rearrange("(c p) f -> p c f", p=128),
          writes=[XK(c.xn, 0, 64)])
    for tt in range(16):
        q0 = tt * 128
        lg = c.ps[5][:, tt * 8: tt * 8 + 8]
        for ch in range(NCH):
            P.op("tensor", lambda e, o=lg,
                 l=NX2[:, ch * S + q0: ch * S + q0 + 128],
                 r=c.xn[:, ch * 8: ch * 8 + 8], st=(ch == 0): e.matmul(
                     out=o, lhsT=l, rhs=r, start=st, stop=False),
                 reads=[(NX2, ch * S + q0, ch * S + q0 + 128), XK(c.xn, ch * 8, ch * 8 + 8)],
                 writes=["ps5"], group=K + f"rt{tt}")
        P.op("tensor", lambda e, o=lg, l=c.ONESB[0:1, 0:128],
             r=c.eb_row[:, 0:8]: e.matmul(out=o, lhsT=l, rhs=r, start=False,
                                          stop=True),
             reads=["eb_row", "ones1b"], writes=["ps5"], group=K + f"rt{tt}")
    for tt in range(16):
        q0 = tt * 128
        lg = c.ps[5][:, tt * 8: tt * 8 + 8]
        P.op("vector", lambda e, o=c.colB[:, 6:7], i=lg:
             e.reduce_max(out=o, in_=i, axis=AX.X, negate=True),
             reads=["ps5"], writes=["colB7"])
        P.op("scalar", lambda e, o=c.sc_u[:, 0:8], i=lg, b=c.colB[:, 6:7]:
             e.activation(out=o, in_=i, func=AF.Exp, bias=b),
             reads=["ps5", "colB7"], writes=[XK(c.sc_u, 0, 8)])
        P.op("vector", lambda e, o=c.colB[:, 7:8], i=c.sc_u[:, 0:8]:
             e.reduce_sum(out=o, in_=i, axis=AX.X),
             reads=[XK(c.sc_u, 0, 8)], writes=["colB8"])
        P.op("vector", lambda e, o=c.colB[:, 8:9], i=c.colB[:, 7:8]:
             e.reciprocal(out=o, in_=i), reads=["colB8"], writes=["colB9"])
        P.op("vector", lambda e, o=c.sc_u[:, 8:16], i=c.sc_u[:, 0:8],
             s=c.colB[:, 8:9]: e.tensor_scalar(out=o, in0=i, scalar1=s, scalar2=None,
                                               op0=ALU.mult),
             reads=[XK(c.sc_u, 0, 8), "colB9"], writes=[XK(c.sc_u, 8, 16)])
        P.op("vector", lambda e, o=c.colB[:, 9:10], i=c.sc_u[:, 8:16]:
             e.reduce_max(out=o, in_=i, axis=AX.X),
             reads=[XK(c.sc_u, 8, 16)], writes=["colB10"])
        P.op("vector", lambda e, o=c.sc_u[:, 16:24], i=c.sc_u[:, 8:16],
             s=c.colB[:, 9:10]: e.tensor_scalar(out=o, in0=i, scalar1=s, scalar2=None,
                                                op0=ALU.is_equal),
             reads=[XK(c.sc_u, 8, 16), "colB10"], writes=[XK(c.sc_u, 16, 24)])
        P.op("vector", lambda e, o=c.sc_u[:, 24:32], i=c.sc_u[:, 16:24],
             b=c.sc_u[:, 8:16]: e.scalar_tensor_tensor(
                 out=o, in0=i, scalar=-1e9, in1=b, op0=ALU.mult, op1=ALU.add),
             reads=[XK(c.sc_u, 16, 24), XK(c.sc_u, 8, 16)], writes=[XK(c.sc_u, 24, 32)])
        P.op("vector", lambda e, o=c.colB[:, 10:11], i=c.sc_u[:, 24:32]:
             e.reduce_max(out=o, in_=i, axis=AX.X),
             reads=[XK(c.sc_u, 24, 32)], writes=["colB11"])
        P.op("vector", lambda e, o=c.sc_u[:, 32:40], i=c.sc_u[:, 24:32],
             s=c.colB[:, 10:11]: e.tensor_scalar(out=o, in0=i, scalar1=s, scalar2=None,
                                                 op0=ALU.is_equal),
             reads=[XK(c.sc_u, 24, 32), "colB11"], writes=[XK(c.sc_u, 32, 40)])
        P.op("vector", lambda e, o=c.colB[:, 11:12], a=c.colB[:, 10:11],
             b=c.colB[:, 9:10]: e.tensor_tensor(out=o, in0=a, in1=b, op=ALU.subtract),
             reads=["colB11", "colB10"], writes=["colB12"])
        P.op("scalar", lambda e, o=c.colB[:, 12:13], i=c.colB[:, 11:12]:
             e.activation(out=o, in_=i, func=AF.Exp),
             reads=["colB12"], writes=["colB13"])
        P.op("vector", lambda e, o=c.colB[:, 13:14], i=c.colB[:, 12:13]:
             e.tensor_scalar_add(out=o, in0=i, scalar1=1.0),
             reads=["colB13"], writes=["colB14"])
        P.op("vector", lambda e, o=c.colB[:, 13:14], i=c.colB[:, 13:14]:
             e.reciprocal(out=o, in_=i), reads=["colB14"], writes=["colB14"])
        P.op("vector", lambda e, o=c.colB[:, 14:15], a=c.colB[:, 12:13],
             b=c.colB[:, 13:14]: e.tensor_tensor(out=o, in0=a, in1=b, op=ALU.mult),
             reads=["colB13", "colB14"], writes=["colB15"])
        P.op("vector", lambda e, o=c.sc_u[:, 40:48], i=c.sc_u[:, 32:40],
             s=c.colB[:, 14:15]: e.tensor_scalar(out=o, in0=i, scalar1=s, scalar2=None,
                                                 op0=ALU.mult),
             reads=[XK(c.sc_u, 32, 40), "colB15"], writes=[XK(c.sc_u, 40, 48)])
        P.op("vector", lambda e, o=c.xn[:, 64 + tt * 8: 72 + tt * 8], i=c.sc_u[:, 16:24],
             s=c.colB[:, 13:14], b=c.sc_u[:, 40:48]: e.scalar_tensor_tensor(
                 out=o, in0=i, scalar=s, in1=b, op0=ALU.mult, op1=ALU.add),
             reads=[XK(c.sc_u, 16, 24), "colB14", XK(c.sc_u, 40, 48)],
             writes=[XK(c.xn, 64 + tt * 8, 72 + tt * 8)])


    P.dma("gpsimd", CE8[:, :], c.d_c_e8, writes=[(CE8, 0, 1024)])

    HST = Rv(c.B1, 0, 8192)

    def ffn_expert(tag, wsl1, wsl2, wsl3, use_comb, eidx):
        for tp in range(2):
            t0s = (tp * 2 * TG, (tp * 2 + 1) * TG)
            if use_comb:
                for half in range(2):
                    P.op("tensor", lambda e, o=c.ps[4 + half][:, :TG],
                         l=CE8[:, eidx * 128: eidx * 128 + 128],
                         r=COMB[:, t0s[half]:t0s[half] + TG]: e.matmul(
                             out=o, lhsT=l, rhs=r, start=True, stop=True),
                         reads=[(CE8, eidx * 128, eidx * 128 + 128),
                                (COMB, t0s[half], t0s[half] + TG)],
                         writes=[f"ps{4 + half}"])
                    P.op("scalar", lambda e, o=CBT[:, half * TG: half * TG + TG],
                         i=c.ps[4 + half][:, :TG]: e.activation(out=o, in_=i,
                                                                func=AF.Identity),
                         reads=[f"ps{4 + half}"],
                         writes=[(CBT, half * TG, half * TG + TG)])
            for fch in range(NCH):
                for ch in range(NCH):
                    wc = ch * 1024 + fch * 128
                    for half in range(2):
                        P.op("tensor", lambda e, o=c.ps[half][:, :TG],
                             l=wsl1[:, wc:wc + 128],
                             r=NX2[:, ch * S + t0s[half]: ch * S + t0s[half] + TG],
                             st=(ch == 0), sp=(ch == NCH - 1): e.matmul(
                                 out=o, lhsT=l, rhs=r, start=st, stop=sp),
                             reads=[(wsl1, wc, wc + 128),
                                    (NX2, ch * S + t0s[half], ch * S + t0s[half] + TG)],
                             writes=[f"ps{half}"], group=tag + f"g1_{tp}_{fch}_{half}")
                for half in range(2):
                    P.op("scalar", lambda e, o=HST[:, fch * 1024 + half * TG: fch * 1024 + half * TG + TG],
                         i=c.ps[half][:, :TG]: e.activation(out=o, in_=i, func=AF.Silu),
                         reads=[f"ps{half}"],
                         writes=[(HST, fch * 1024 + half * TG, fch * 1024 + half * TG + TG)])
                for ch in range(NCH):
                    wc = ch * 1024 + fch * 128
                    for half in range(2):
                        P.op("tensor", lambda e, o=c.ps[2 + half][:, :TG],
                             l=wsl3[:, wc:wc + 128],
                             r=NX2[:, ch * S + t0s[half]: ch * S + t0s[half] + TG],
                             st=(ch == 0), sp=(ch == NCH - 1): e.matmul(
                                 out=o, lhsT=l, rhs=r, start=st, stop=sp),
                             reads=[(wsl3, wc, wc + 128),
                                    (NX2, ch * S + t0s[half], ch * S + t0s[half] + TG)],
                             writes=[f"ps{2 + half}"], group=tag + f"g3_{tp}_{fch}_{half}")
                for half in range(2):
                    hc_ = fch * 1024 + half * TG
                    P.op("vector", lambda e, o=HST[:, hc_:hc_ + TG], a=HST[:, hc_:hc_ + TG],
                         b=c.ps[2 + half][:, :TG]: e.tensor_tensor(out=o, in0=a, in1=b,
                                                                   op=ALU.mult),
                         reads=[(HST, hc_, hc_ + TG), f"ps{2 + half}"],
                         writes=[(HST, hc_, hc_ + TG)])
                    if use_comb:
                        P.op("vector", lambda e, o=HST[:, hc_:hc_ + TG],
                             a=HST[:, hc_:hc_ + TG], b=CBT[:, half * TG: half * TG + TG]:
                             e.tensor_tensor(out=o, in0=a, in1=b, op=ALU.mult),
                             reads=[(HST, hc_, hc_ + TG), (CBT, half * TG, half * TG + TG)],
                             writes=[(HST, hc_, hc_ + TG)])
            bankrot = ((0, 1), (2, 3), (4, 5))
            for dch in range(NCH):
                bk = bankrot[dch % 3]
                for fch in range(NCH):
                    wc = fch * 1024 + dch * 128
                    for half in range(2):
                        P.op("tensor", lambda e, o=c.ps[bk[half]][:, :TG],
                             l=wsl2[:, wc:wc + 128],
                             r=HST[:, fch * 1024 + half * TG: fch * 1024 + half * TG + TG],
                             st=(fch == 0), sp=(fch == NCH - 1): e.matmul(
                                 out=o, lhsT=l, rhs=r, start=st, stop=sp),
                             reads=[(wsl2, wc, wc + 128),
                                    (HST, fch * 1024 + half * TG, fch * 1024 + half * TG + TG)],
                             writes=[f"ps{bk[half]}"], group=tag + f"o_{tp}_{dch}_{half}")
                for half in range(2):
                    xc = dch * S + t0s[half]
                    P.op("vector", lambda e, o=X[:, xc:xc + TG], a=X[:, xc:xc + TG],
                         b=c.ps[bk[half]][:, :TG]: e.tensor_tensor(out=o, in0=a, in1=b,
                                                                   op=ALU.add),
                         reads=[f"ps{bk[half]}", XK(X, xc, xc + TG)],
                         writes=[XK(X, xc, xc + TG)])

    w1s, w3s, w2s = Rv(c.Wp1, 0, 8192), Rv(c.Wp2, 0, 8192), Rv(c.Wp3, 0, 8192)
    emit_wload(P, c, w1s, c.d_f_sw1T, 1024, 1024, K + "sw1", wid="fsw1")
    emit_wload(P, c, w3s, c.d_f_sw3T, 1024, 1024, K + "sw3", wid="fsw3")
    emit_wload(P, c, w2s, c.d_f_sw2T, 1024, 1024, K + "sw2", wid="fsw2")
    ffn_expert(K + "sh", w1s, w2s, w3s, False, 0)
    for tt in range(16):
        q0 = tt * 128
        P.op("tensor", lambda e, o=c.psb[0:8, 0:128], i=c.xn[:, 64 + tt * 8: 72 + tt * 8],
             d=c.identb[:, :]: e.transpose(out=o, in_=i, identity=d),
             reads=[XK(c.xn, 64 + tt * 8, 72 + tt * 8), (c.identb, 0, 128)],
             writes=["psb"])
        P.op("vector", lambda e, o=COMB[:, q0:q0 + 128], i=c.psb[0:8, 0:128]:
             e.tensor_copy(out=o, in_=i), reads=["psb"],
             writes=[(COMB, q0, q0 + 128)])
    for e8 in range(8):
        emit_wload(P, c, w1s, c.d_f_ew1T[e8], 1024, 1024, K + f"e{e8}w1",
                   wid=f"fe{e8}w1")
        emit_wload(P, c, w3s, c.d_f_ew3T[e8], 1024, 1024, K + f"e{e8}w3",
                   wid=f"fe{e8}w3")
        emit_wload(P, c, w2s, c.d_f_ew2T[e8], 1024, 1024, K + f"e{e8}w2",
                   wid=f"fe{e8}w2")
        ffn_expert(K + f"ex{e8}", w1s, w2s, w3s, True, e8)


def emit_l0_attn_a(P, c, X):
    K = "aa"
    SC = 0.125
    c.m_resident = False
    emit_ln_rows(P, c, X, 0, c.row_m, c.row_r, c.row_mr, K + "ln")
    KW = Rv(c.B1, 0, 8192)
    VW = Rv(c.B1, 8192, 8192)
    OTG = Rv(c.B2, 0, 4096)
    BND = Rv(c.B2, 4096, 1408)
    AB2 = Rv(c.B2, 5504, 1536)
    QQ = Rv(c.B2, 7040, 512)
    WD = Rv(c.B2, 7552, 8192)
    emit_wload(P, c, Rv(c.Wp1, 0, 8192), c.d_aa_wq, 1024, 1024, K + "wq")
    emit_wload(P, c, Rv(c.Wp2, 0, 8192), c.d_aa_wk, 1024, 1024, K + "wk")
    emit_wload(P, c, Rv(c.Wp3, 0, 8192), c.d_aa_wv, 1024, 1024, K + "wv")
    emit_wload(P, c, WD, c.d_aa_wd, 1024, 1024, K + "wd")
    P.dma("gpsimd", BND[:, :], c.d_c_band[:, 128:1536], writes=[(BND, 0, 1408)])
    P.dma("sync", c.colA[:, 0:8], c.d_aa_bq[:, 0:8], writes=["colA"])
    P.dma("sync", c.colA[:, 8:16], c.d_aa_bk[:, 0:8], writes=["colA"])
    P.dma("sync", c.colB[:, 0:8], c.d_aa_bd[:, 0:8], writes=["colBb"])

    def build_kv(j0, n, first):
        av = j0 * 128
        wdt = n * 128
        emit_ln_bcast_w(P, c, c.row_r, c.row_mr, av, wdt, K + "ln", 5, 6)
        for ch in range(NCH):
            emit_xn_tile(P, c, X, ch, av, wdt, c.xn, ch * 512, 5, 6)
        for fch in range(NCH):
            bank = fch % 2
            for ch in range(NCH):
                wc = ch * 1024 + fch * 128
                P.op("tensor", lambda e, o=c.ps[bank][:, :wdt],
                     l=c.Wp2[:, wc:wc + 128], r=c.xn[:, ch * 512: ch * 512 + wdt],
                     st=(ch == 0), sp=(ch == NCH - 1): e.matmul(
                         out=o, lhsT=l, rhs=r, start=st, stop=sp),
                     reads=[(c.Wp2, wc, wc + 128), XK(c.xn, ch * 512, ch * 512 + wdt)],
                     writes=[f"ps{bank}"], group=K + f"k{j0}_{fch}")
            for ti in range(n):
                sl = (j0 + ti) % 8
                P.op("scalar", lambda e,
                     o=KW[:, fch * 1024 + sl * 128: fch * 1024 + sl * 128 + 128],
                     i=c.ps[bank][:, ti * 128: ti * 128 + 128],
                     b=c.colA[:, 8 + fch:9 + fch]: e.activation(
                         out=o, in_=i, func=AF.Identity, bias=b),
                     reads=[f"ps{bank}", "colA"],
                     writes=[(KW, fch * 1024 + sl * 128, fch * 1024 + sl * 128 + 128)])
        for ch in range(NCH):
            for ti in range(n):
                sl = (j0 + ti) % 8
                P.op("scalar", lambda e, o=c.sc_u[:, ti * 128: ti * 128 + 128],
                     i=KW[:, ch * 1024 + sl * 128: ch * 1024 + sl * 128 + 128]:
                     e.activation(out=o, in_=i, func=AF.Square),
                     reads=[(KW, ch * 1024 + sl * 128, ch * 1024 + sl * 128 + 128)],
                     writes=[XK(c.sc_u, ti * 128, ti * 128 + 128)])
            P.op("tensor", lambda e, o=c.ps[4][:, :wdt], l=c.ONESF[:, 0:128],
                 r=c.sc_u[:, :wdt], st=(ch == 0), sp=(ch == NCH - 1): e.matmul(
                     out=o, lhsT=l, rhs=r, start=st, stop=sp),
                 reads=[XK(c.sc_u, 0, wdt), "onesc"], writes=["ps4"],
                 group=K + f"kn{j0}")
        P.op("vector", lambda e, o=c.colC[:, 0:1], i=c.ps[4][:, :wdt]:
             e.reduce_max(out=o, in_=i, axis=AX.X), reads=["ps4"], writes=["cC0"])
        if first:
            P.op("vector", lambda e, o=c.colC[:, 1:2], i=c.colC[:, 0:1]:
                 e.tensor_copy(out=o, in_=i), reads=["cC0"], writes=["cC1"])
        else:
            P.op("vector", lambda e, o=c.colC[:, 1:2], a=c.colC[:, 0:1],
                 b=c.colC[:, 1:2]: e.tensor_tensor(out=o, in0=a, in1=b, op=ALU.max),
                 reads=["cC0", "cC1"], writes=["cC1"])
        for ti in range(n):
            sl = (j0 + ti) % 8
            for half in range(2):
                bank = 2 + half
                for ch in range(NCH):
                    P.op("tensor", lambda e, o=c.ps[bank][:, :512],
                         l=c.xn[:, ch * 512 + ti * 128: ch * 512 + ti * 128 + 128],
                         r=c.Wp3[:, ch * 1024 + half * 512: ch * 1024 + half * 512 + 512],
                         st=(ch == 0): e.matmul(out=o, lhsT=l, rhs=r,
                                                start=st, stop=False),
                         reads=[(c.Wp3, ch * 1024 + half * 512, ch * 1024 + half * 512 + 512),
                                XK(c.xn, ch * 512 + ti * 128, ch * 512 + ti * 128 + 128)],
                         writes=[f"ps{bank}"], group=K + f"v{j0}_{ti}_{half}")
                P.op("tensor", lambda e, o=c.ps[bank][:, :512], l=c.ONESB[64:65, 0:128],
                     r=c.bv_row[:, half * 512: half * 512 + 512]: e.matmul(
                         out=o, lhsT=l, rhs=r, start=False, stop=True),
                     reads=["bv_row", "ones1b"], writes=[f"ps{bank}"],
                     group=K + f"v{j0}_{ti}_{half}")
                P.op("scalar", lambda e,
                     o=VW[:, sl * 1024 + half * 512: sl * 1024 + half * 512 + 512],
                     i=c.ps[bank][:, :512]: e.activation(out=o, in_=i, func=AF.Identity),
                     reads=[f"ps{bank}"],
                     writes=[(VW, sl * 1024 + half * 512, sl * 1024 + half * 512 + 512)])

    build_kv(0, 4, True)
    build_kv(4, 2, False)
    for qg in range(NTG):
        t0 = qg * TG
        emit_ln_bcast_w(P, c, c.row_r, c.row_mr, t0, TG, K + "ln", 5, 6)
        for ch in range(NCH):
            emit_xn_tile(P, c, X, ch, t0, TG, c.xn, ch * TG, 5, 6)
        QQ2 = [QQ, Rv(c.B2, 15744, 512)]

        def prep_a(h, sl):
            base = 64 * (h % 2)
            fc = h // 2
            for ch in range(NCH):
                wc = ch * 1024 + h * 64
                P.op("tensor", lambda e, o=c.ps[6][base:base + 64, :TG],
                     l=c.Wp1[:, wc:wc + 64], r=c.xn[:, ch * TG: ch * TG + TG],
                     st=(ch == 0), sp=(ch == NCH - 1): e.matmul(
                         out=o, lhsT=l, rhs=r, start=st, stop=sp),
                     reads=[(c.Wp1, wc, wc + 64), XK(c.xn, ch * TG, ch * TG + TG)],
                     writes=["ps6"], group=K + f"q{qg}_{h}")
            P.op("scalar", lambda e, o=QQ2[sl][base:base + 64, 0:TG],
                 i=c.ps[6][base:base + 64, :TG],
                 b=c.colA[base:base + 64, fc:fc + 1]: e.activation(
                     out=o, in_=i, func=AF.Identity, bias=b),
                 reads=["ps6", "colA"], writes=[(QQ2[sl], 0, TG)])
            P.op("scalar", lambda e, o=c.sc_u[base:base + 64, :TG],
                 i=QQ2[sl][base:base + 64, 0:TG]: e.activation(out=o, in_=i,
                                                              func=AF.Square),
                 reads=[(QQ2[sl], 0, TG)], writes=[XK(c.sc_u, 0, TG)])

        def prep_m(h, sl):
            base = 64 * (h % 2)
            P.op("tensor", lambda e, o=c.ps[6][:, :TG],
                 l=c.ONESF[base:base + 64, 0:128],
                 r=c.sc_u[base:base + 64, :TG]: e.matmul(out=o, lhsT=l, rhs=r,
                                                         start=True, stop=True),
                 reads=[XK(c.sc_u, 0, TG), "ones1f"], writes=["ps6"])

        def prep_b(h, sl):
            P.op("vector", lambda e, o=c.colC[:, 2:3], i=c.ps[6][:, :TG]:
                 e.reduce_max(out=o, in_=i, axis=AX.X), reads=["ps6"], writes=["cC2"])
            P.op("vector", lambda e, o=c.colC[:, 3:4], a=c.colC[:, 2:3],
                 b=c.colC[:, 1:2]: e.tensor_tensor(out=o, in0=a, in1=b, op=ALU.mult),
                 reads=["cC2", "cC1"], writes=["cC3"])
            P.op("scalar", lambda e, o=c.colC[:, 3:4], i=c.colC[:, 3:4]:
                 e.activation(out=o, in_=i, func=AF.Ln), reads=["cC3"], writes=["cC3"])
            P.op("scalar", lambda e, o=c.colC[:, 3:4], i=c.colC[:, 3:4]:
                 e.activation(out=o, in_=i, func=AF.Exp, scale=0.5),
                 reads=["cC3"], writes=["cC3"])
            P.op("vector", lambda e, o=c.colC[:, 8 + sl:9 + sl], i=c.colC[:, 3:4]:
                 e.tensor_scalar_mul(out=o, in0=i, scalar1=-SC),
                 reads=["cC3"], writes=[f"negc{sl}"])

        def attend(h, sl, hook0=None, hook1=None, hook2=None, hook3=None):
            base = 64 * (h % 2)
            fc = h // 2
            ob = 2 + (h % 2)
            dn = 4 + (h % 2)
            db = dn
            valid = [j for j in range(4 * qg - 2, 4 * qg + 6) if 0 <= j < 16]
            nv = len(valid)

            def st_mm(ij):
                j = valid[ij]
                if ij % 3 == 2:
                    sb_ = dn if ij == 2 else 4 + (h + 1) % 2
                else:
                    sb_ = ij % 3
                ac = (ij % 3) * TG
                sl_ = j % 8
                P.op("tensor", lambda e, o=c.ps[sb_][:, :TG],
                     l=KW[base:base + 64, fc * 1024 + sl_ * 128: fc * 1024 + sl_ * 128 + 128],
                     r=QQ2[sl][base:base + 64, 0:TG]: e.matmul(out=o, lhsT=l, rhs=r,
                                                              start=True, stop=True),
                     reads=[(KW, fc * 1024 + sl_ * 128, fc * 1024 + sl_ * 128 + 128),
                            (QQ2[sl], 0, TG)],
                     writes=[f"ps{sb_}"], group=K + f"st{qg}_{h}_{j}")
                P.op("scalar", lambda e, o=AB2[:, ac: ac + TG],
                     i=c.ps[sb_][:, :TG], b=c.colC[:, 8 + sl:9 + sl]: e.activation(
                         out=o, in_=i, func=AF.Exp, bias=b, scale=SC),
                     reads=[f"ps{sb_}", f"negc{sl}"], writes=[(AB2, ac, ac + TG)])
                moff = 640 - 128 * (j - 4 * qg)
                P.op("vector", lambda e, o=AB2[:, ac: ac + TG],
                     a=AB2[:, ac: ac + TG], b=BND[:, moff:moff + 512]:
                     e.tensor_tensor(out=o, in0=a, in1=b, op=ALU.mult),
                     reads=[(AB2, ac, ac + TG), (BND, moff, moff + 512)],
                     writes=[(AB2, ac, ac + TG)])

            def av_mm(ij):
                j = valid[ij]
                ac = (ij % 3) * TG
                sl_ = j % 8
                P.op("tensor", lambda e, o=c.ps[ob][0:64, :TG],
                     l=VW[:, sl_ * 1024 + h * 64: sl_ * 1024 + h * 64 + 64],
                     r=AB2[:, ac: ac + TG],
                     st=(ij == 0), sp=(ij == nv - 1): e.matmul(
                         out=o, lhsT=l, rhs=r, start=st, stop=sp),
                     reads=[(VW, sl_ * 1024 + h * 64, sl_ * 1024 + h * 64 + 64),
                            (AB2, ac, ac + TG)],
                     writes=[f"ps{ob}"], group=K + f"o{qg}_{h}")
                P.op("tensor", lambda e, o=c.ps[dn][0:64, :TG], l=c.ONESB[:, 0:64],
                     r=AB2[:, ac: ac + TG],
                     st=(ij == 0), sp=(ij == nv - 1): e.matmul(
                         out=o, lhsT=l, rhs=r, start=st, stop=sp),
                     reads=[(AB2, ac, ac + TG), "onesc"],
                     writes=[f"ps{dn}"], group=K + f"dn{qg}_{h}")

            st_mm(0)
            if nv > 1:
                st_mm(1)
            if hook0:
                hook0()
            if nv > 2:
                st_mm(2)
            av_mm(0)
            nxt = 3
            done = 1
            while done < min(3, nv):
                if nxt < nv:
                    st_mm(nxt)
                    nxt += 1
                av_mm(done)
                done += 1
            if hook1:
                hook1()
            while done < min(6, nv):
                if nxt < nv:
                    st_mm(nxt)
                    nxt += 1
                av_mm(done)
                done += 1
            if hook2:
                hook2()
            while done < max(0, nv - 2):
                if nxt < nv:
                    st_mm(nxt)
                    nxt += 1
                av_mm(done)
                done += 1
            if hook3:
                hook3()
            while done < nv:
                if nxt < nv:
                    st_mm(nxt)
                    nxt += 1
                av_mm(done)
                done += 1

            def epilogue():
                P.op("vector", lambda e, o=c.drep[0:64, 0:TG], i=c.ps[dn][0:64, :TG]:
                     _recip_bf16(e, o, i), reads=[f"ps{dn}"], writes=["row_a"])
                P.op("scalar", lambda e, o=QQ2[sl][0:64, 0:TG],
                     i=c.ps[ob][0:64, :TG]: e.activation(out=o, in_=i, func=AF.Identity),
                     reads=[f"ps{ob}"], writes=[(QQ2[sl], 0, TG)])
                P.op("vector", lambda e, o=OTG[base:base + 64, fc * 512: fc * 512 + TG],
                     a=QQ2[sl][0:64, 0:TG], b=c.drep[0:64, 0:TG]:
                     e.tensor_tensor(out=o, in0=a, in1=b, op=ALU.mult),
                     reads=[(QQ2[sl], 0, TG), "row_a"],
                     writes=[(OTG, fc * 512, fc * 512 + TG)])
            return epilogue

        prep_a(0, 0)
        prep_m(0, 0)
        prep_b(0, 0)
        pend = None
        for h in range(16):
            if h + 1 < 16:
                h1 = h + 1
                s1 = h1 % 2
                pend = attend(h, h % 2, hook0=pend,
                              hook1=(lambda h1=h1, s1=s1: prep_a(h1, s1)),
                              hook2=(lambda h1=h1, s1=s1: prep_m(h1, s1)),
                              hook3=(lambda h1=h1, s1=s1: prep_b(h1, s1)))
            else:
                pend = attend(h, h % 2, hook0=pend)
        if pend is not None:
            pend()
        j0n = 4 * qg + 6
        nn = min(16, j0n + 4) - j0n
        if nn > 0:
            build_kv(j0n, nn, False)
        for fch in range(NCH):
            bank = fch % 4
            for ch in range(NCH):
                wc = ch * 1024 + fch * 128
                P.op("tensor", lambda e, o=c.ps[bank][:, :TG],
                     l=WD[:, wc:wc + 128], r=OTG[:, ch * 512: ch * 512 + TG],
                     st=(ch == 0), sp=(ch == NCH - 1): e.matmul(
                         out=o, lhsT=l, rhs=r, start=st, stop=sp),
                     reads=[(WD, wc, wc + 128), (OTG, ch * 512, ch * 512 + TG)],
                     writes=[f"ps{bank}"], group=K + f"wd{qg}_{fch}")
            xc = fch * S + t0
            P.op("vector", lambda e, o=X[:, xc:xc + TG], i=c.ps[bank][:, :TG],
                 s=c.colB[:, fch:fch + 1], x=X[:, xc:xc + TG]: e.scalar_tensor_tensor(
                     out=o, in0=i, scalar=s, in1=x, op0=ALU.add, op1=ALU.add),
                 reads=[f"ps{bank}", "colBb", XK(X, xc, xc + TG)],
                 writes=[XK(X, xc, xc + TG)])


def emit_l0_attn_b(P, c, X):
    K = "ab"
    SC = 0.125
    c.m_resident = False
    emit_ln_rows(P, c, X, 0, c.row_m, c.row_r, c.row_mr, K + "ln")
    KB = Rv(c.B1, 0, 4096)
    VB = Rv(c.B1, 4096, 4160)
    QQ = Rv(c.B1, 8256, 512)
    AB = Rv(c.B1, 9280, 1536)
    OB = Rv(c.B2, 0, 16384)
    emit_wload(P, c, Rv(c.Wp1, 0, 8192), c.d_ab_wq, 1024, 1024, K + "wq")
    emit_wload(P, c, Rv(c.Wp2, 0, 2048), c.d_ab_wk, 1024, 256, K + "wk")
    emit_wload(P, c, Rv(c.Wp2, 2048, 2048), c.d_ab_wv, 1024, 256, K + "wv")
    emit_wload(P, c, Rv(c.Wp3, 0, 8192), c.d_ab_wpost, 1024, 1024, K + "wp")

    for tg in range(NTG):
        t0 = tg * TG
        emit_ln_bcast_w(P, c, c.row_r, c.row_mr, t0, TG, K + "ln", 5, 6)
        for ch in range(NCH):
            emit_xn_tile(P, c, X, ch, t0, TG, c.xn, ch * TG, 5, 6)
        for f2 in range(2):
            bank = f2
            for ch in range(NCH):
                wc = ch * 256 + f2 * 128
                P.op("tensor", lambda e, o=c.ps[bank][:, :TG],
                     l=c.Wp2[:, wc:wc + 128], r=c.xn[:, ch * TG: ch * TG + TG],
                     st=(ch == 0), sp=(ch == NCH - 1): e.matmul(
                         out=o, lhsT=l, rhs=r, start=st, stop=sp),
                     reads=[(c.Wp2, wc, wc + 128), XK(c.xn, ch * TG, ch * TG + TG)],
                     writes=[f"ps{bank}"], group=K + f"k{tg}_{f2}")
            P.op("scalar", lambda e, o=KB[:, f2 * 2048 + t0: f2 * 2048 + t0 + TG],
                 i=c.ps[bank][:, :TG]: e.activation(out=o, in_=i, func=AF.Identity),
                 reads=[f"ps{bank}"],
                 writes=[(KB, f2 * 2048 + t0, f2 * 2048 + t0 + TG)])
        P.op("gpsimd", lambda e, o=VB[:, tg * 1040: tg * 1040 + 1040]:
             e.memset(o, 1.0), writes=[(VB, tg * 1040, tg * 1040 + 1040)])
        for ti in range(4):
            tt = tg * 4 + ti
            bank = 2 + ti % 2
            for ch in range(NCH):
                P.op("tensor", lambda e, o=c.ps[bank][:, :256],
                     l=c.xn[:, ch * TG + ti * 128: ch * TG + ti * 128 + 128],
                     r=c.Wp2[:, 2048 + ch * 256: 2048 + ch * 256 + 256],
                     st=(ch == 0), sp=(ch == NCH - 1): e.matmul(
                         out=o, lhsT=l, rhs=r, start=st, stop=sp),
                     reads=[(c.Wp2, 2048 + ch * 256, 2048 + ch * 256 + 256),
                            XK(c.xn, ch * TG + ti * 128, ch * TG + ti * 128 + 128)],
                     writes=[f"ps{bank}"], group=K + f"v{tt}")
            for g_ in range(4):
                P.op("scalar", lambda e, o=VB[:, tt * 260 + g_ * 65: tt * 260 + g_ * 65 + 64],
                     i=c.ps[bank][:, g_ * 64: g_ * 64 + 64]: e.activation(
                         out=o, in_=i, func=AF.Identity),
                     reads=[f"ps{bank}"],
                     writes=[(VB, tt * 260 + g_ * 65, tt * 260 + g_ * 65 + 64)])
        for f2 in range(2):
            P.op("scalar", lambda e, o=c.sc_u[:, :TG],
                 i=KB[:, f2 * 2048 + t0: f2 * 2048 + t0 + TG]: e.activation(
                     out=o, in_=i, func=AF.Square),
                 reads=[(KB, f2 * 2048 + t0, f2 * 2048 + t0 + TG)],
                 writes=[XK(c.sc_u, 0, TG)])
            P.op("tensor", lambda e, o=c.ps[4][:, :TG], l=c.ONESF[:, 0:128],
                 r=c.sc_u[:, :TG], st=(f2 == 0), sp=(f2 == 1): e.matmul(
                     out=o, lhsT=l, rhs=r, start=st, stop=sp),
                 reads=[XK(c.sc_u, 0, TG), "onesc"], writes=["ps4"],
                 group=K + f"kn{tg}")
        P.op("vector", lambda e, o=c.colC[:, 0:1], i=c.ps[4][:, :TG]:
             e.reduce_max(out=o, in_=i, axis=AX.X), reads=["ps4"], writes=["cC0"])
        if tg == 0:
            P.op("vector", lambda e, o=c.colC[:, 1:2], i=c.colC[:, 0:1]:
                 e.tensor_copy(out=o, in_=i), reads=["cC0"], writes=["cC1"])
        else:
            P.op("vector", lambda e, o=c.colC[:, 1:2], a=c.colC[:, 0:1],
                 b=c.colC[:, 1:2]: e.tensor_tensor(out=o, in0=a, in1=b, op=ALU.max),
                 reads=["cC0", "cC1"], writes=["cC1"])

    for qg in range(NTG):
        t0 = qg * TG
        emit_ln_bcast_w(P, c, c.row_r, c.row_mr, t0, TG, K + "ln", 5, 6)
        for ch in range(NCH):
            emit_xn_tile(P, c, X, ch, t0, TG, c.xn, ch * TG, 5, 6)
        QQ2 = [QQ, Rv(c.B1, 8768, 512)]

        def prep_a(g, wv_, sl):
            base = 64 * (g % 2)
            hw = g * 4 + wv_
            for ch in range(NCH):
                wc = ch * 1024 + g * 256 + wv_ * 64
                P.op("tensor", lambda e, o=c.ps[6][base:base + 64, :TG],
                     l=c.Wp1[:, wc:wc + 64], r=c.xn[:, ch * TG: ch * TG + TG],
                     st=(ch == 0), sp=(ch == NCH - 1): e.matmul(
                         out=o, lhsT=l, rhs=r, start=st, stop=sp),
                     reads=[(c.Wp1, wc, wc + 64), XK(c.xn, ch * TG, ch * TG + TG)],
                     writes=["ps6"], group=K + f"q{qg}_{hw}")
            P.op("vector", lambda e, o=QQ2[sl][base:base + 64, 0:TG],
                 i=c.ps[6][base:base + 64, :TG]: _copy_bf16(e, o, i),
                 reads=["ps6"], writes=[(QQ2[sl], 0, TG)])
            P.op("vector", lambda e, o=c.sc_u[base:base + 64, :TG],
                 a=QQ2[sl][base:base + 64, 0:TG]: e.tensor_tensor(
                     out=o, in0=a, in1=a, op=ALU.mult),
                 reads=[(QQ2[sl], 0, TG)], writes=[XK(c.sc_u, 0, TG)])

        def prep_m(g, wv_, sl):
            base = 64 * (g % 2)
            P.op("tensor", lambda e, o=c.ps[6][:, :TG],
                 l=c.ONESF[base:base + 64, 0:128],
                 r=c.sc_u[base:base + 64, :TG]: e.matmul(out=o, lhsT=l, rhs=r,
                                                         start=True, stop=True),
                 reads=[XK(c.sc_u, 0, TG), "ones1f"], writes=["ps6"])

        def prep_b(g, wv_, sl):
            P.op("vector", lambda e, o=c.colC[:, 2:3], i=c.ps[6][:, :TG]:
                 e.reduce_max(out=o, in_=i, axis=AX.X), reads=["ps6"], writes=["cC2"])
            P.op("vector", lambda e, o=c.colC[:, 3:4], a=c.colC[:, 2:3],
                 b=c.colC[:, 1:2]: e.tensor_tensor(out=o, in0=a, in1=b, op=ALU.mult),
                 reads=["cC2", "cC1"], writes=["cC3"])
            P.op("scalar", lambda e, o=c.colC[:, 3:4], i=c.colC[:, 3:4]:
                 e.activation(out=o, in_=i, func=AF.Ln), reads=["cC3"], writes=["cC3"])
            P.op("scalar", lambda e, o=c.colC[:, 3:4], i=c.colC[:, 3:4]:
                 e.activation(out=o, in_=i, func=AF.Exp, scale=0.5),
                 reads=["cC3"], writes=["cC3"])
            P.op("vector", lambda e, o=c.colC[:, 8 + sl:9 + sl], i=c.colC[:, 3:4]:
                 e.tensor_scalar_mul(out=o, in0=i, scalar1=-SC),
                 reads=["cC3"], writes=[f"negc{sl}"])

        def attend(g, wv_, sl, hook0=None, hook1=None, hook2=None, hook3=None):
            base = 64 * (g % 2)
            hw = g * 4 + wv_
            ob = 2 + (hw % 2)
            dn = 4 + (hw % 2)
            db = dn
            stb = (0, 1, dn)

            def st_mm(kt):
                sb_ = stb[kt % 3]
                kcol = (g // 2) * 2048 + kt * 128
                P.op("tensor", lambda e, o=c.ps[sb_][:, :TG],
                     l=KB[base:base + 64, kcol:kcol + 128],
                     r=QQ2[sl][base:base + 64, 0:TG]: e.matmul(
                         out=o, lhsT=l, rhs=r, start=True, stop=True),
                     reads=[(KB, kcol, kcol + 128), (QQ2[sl], 0, TG)],
                     writes=[f"ps{sb_}"], group=K + f"st{qg}_{hw}_{kt}")
                ac = (kt % 3) * TG
                P.op("scalar", lambda e, o=AB[:, ac: ac + TG],
                     i=c.ps[sb_][:, :TG], b=c.colC[:, 8 + sl:9 + sl]: e.activation(
                         out=o, in_=i, func=AF.Exp, bias=b, scale=SC),
                     reads=[f"ps{sb_}", f"negc{sl}"],
                     writes=[(AB, ac, ac + TG)])

            def av_mm(kt):
                ac = (kt % 3) * TG
                P.op("tensor", lambda e, o=c.ps[ob][0:65, :TG],
                     l=VB[:, kt * 260 + g * 65: kt * 260 + g * 65 + 65],
                     r=AB[:, ac: ac + TG],
                     st=(kt == 0), sp=(kt == 15): e.matmul(
                         out=o, lhsT=l, rhs=r, start=st, stop=sp),
                     reads=[(VB, kt * 260 + g * 65, kt * 260 + g * 65 + 65),
                            (AB, ac, ac + TG)],
                     writes=[f"ps{ob}"], group=K + f"o{qg}_{hw}")

            st_mm(0)
            st_mm(1)
            if hook0:
                hook0()
            st_mm(2)
            av_mm(0)
            nxt = 3
            done = 1
            while done < 5:
                if nxt < 16:
                    st_mm(nxt)
                    nxt += 1
                av_mm(done)
                done += 1
            if hook1:
                hook1()
            while done < 9:
                st_mm(nxt)
                nxt += 1
                av_mm(done)
                done += 1
            if hook2:
                hook2()
            while done < 14:
                if nxt < 16:
                    st_mm(nxt)
                    nxt += 1
                av_mm(done)
                done += 1
            if hook3:
                hook3()
            while done < 16:
                if nxt < 16:
                    st_mm(nxt)
                    nxt += 1
                av_mm(done)
                done += 1

            def epilogue():
                P.op("vector", lambda e, o=c.den_bf[:, :TG], i=c.ps[ob][64:65, :TG]:
                     _recip_bf16(e, o, i), reads=[f"ps{ob}"], writes=["row_a"])
                P.op("tensor", lambda e, o=c.ps[dn][0:64, :TG], l=c.ONESB[0:1, 0:64],
                     r=c.den_bf[:, :TG]: e.matmul(out=o, lhsT=l, rhs=r,
                                                  start=True, stop=True),
                     reads=["row_a", "ones1b"], writes=[f"ps{dn}"])
                P.op("scalar", lambda e, o=QQ2[sl][0:64, 0:TG], i=c.ps[ob][0:64, :TG]:
                     e.activation(out=o, in_=i, func=AF.Identity),
                     reads=[f"ps{ob}"], writes=[(QQ2[sl], 0, TG)])
                och = 2 * g + wv_ // 2
                ob_ = 64 * (wv_ % 2)
                P.op("vector", lambda e,
                     o=OB[ob_:ob_ + 64, och * 2048 + t0: och * 2048 + t0 + TG],
                     a=QQ2[sl][0:64, 0:TG], b=c.ps[dn][0:64, :TG]: e.tensor_tensor(
                         out=o, in0=a, in1=b, op=ALU.mult),
                     reads=[(QQ2[sl], 0, TG), f"ps{dn}"],
                     writes=[(OB, och * 2048 + t0, och * 2048 + t0 + TG)])
            return epilogue

        heads = [(g, wv_) for g in range(4) for wv_ in range(4)]
        prep_a(*heads[0], 0)
        prep_m(*heads[0], 0)
        prep_b(*heads[0], 0)
        pend = None
        for i, (g, wv_) in enumerate(heads):
            if i + 1 < 16:
                g1, w1_ = heads[i + 1]
                s1 = (i + 1) % 2
                pend = attend(g, wv_, i % 2, hook0=pend,
                              hook1=(lambda g1=g1, w1_=w1_, s1=s1: prep_a(g1, w1_, s1)),
                              hook2=(lambda g1=g1, w1_=w1_, s1=s1: prep_m(g1, w1_, s1)),
                              hook3=(lambda g1=g1, w1_=w1_, s1=s1: prep_b(g1, w1_, s1)))
            else:
                pend = attend(g, wv_, i % 2, hook0=pend)
        pend()
    for tg in range(NTG):
        t0 = tg * TG
        emit_ln_bcast_w(P, c, c.row_r, c.row_mr, t0, TG, K + "ln", 5, 6)
        for fch in range(NCH):
            bank = fch % 4
            for ch in range(NCH):
                wc = ch * 1024 + fch * 128
                P.op("tensor", lambda e, o=c.ps[bank][:, :TG],
                     l=c.Wp3[:, wc:wc + 128], r=OB[:, ch * 2048 + t0: ch * 2048 + t0 + TG],
                     st=(ch == 0), sp=(ch == NCH - 1): e.matmul(
                         out=o, lhsT=l, rhs=r, start=st, stop=sp),
                     reads=[(c.Wp3, wc, wc + 128), (OB, ch * 2048 + t0, ch * 2048 + t0 + TG)],
                     writes=[f"ps{bank}"], group=K + f"wp{tg}_{fch}")
            xc = fch * S + t0
            P.op("vector", lambda e, o=c.sc_u[:, :TG], a=X[:, xc:xc + TG],
                 b=c.rb[:, 0:TG]: e.tensor_tensor(out=o, in0=a, in1=b, op=ALU.mult),
                 reads=[XK(X, xc, xc + TG), "rbc_r"], writes=[XK(c.sc_u, 0, TG)])
            P.op("vector", lambda e, o=c.sc_u[:, :TG], a=c.sc_u[:, :TG],
                 b=c.mrb[:, 0:TG]: e.tensor_tensor(out=o, in0=a, in1=b, op=ALU.subtract),
                 reads=[XK(c.sc_u, 0, TG), "rbc_mr"], writes=[XK(c.sc_u, 0, TG)])
            P.op("vector", lambda e, o=c.sc_u[:, :TG], a=c.sc_u[:, :TG],
                 b=c.ps[bank][:, :TG]: e.tensor_tensor(out=o, in0=a, in1=b, op=ALU.add),
                 reads=[XK(c.sc_u, 0, TG), f"ps{bank}"], writes=[XK(c.sc_u, 0, TG)])
            P.op("vector", lambda e, o=X[:, xc:xc + TG], a=X[:, xc:xc + TG],
                 b=c.sc_u[:, :TG]: e.tensor_tensor(out=o, in0=a, in1=b, op=ALU.add),
                 reads=[XK(c.sc_u, 0, TG), XK(X, xc, xc + TG)],
                 writes=[XK(X, xc, xc + TG)])


def emit_l1_attn_b(P, c, X):
    K = "lb"
    CSZ = 256
    emit_ln_rows(P, c, X, 0, c.row_m, c.row_r, c.row_mr, K + "ln")
    Qc = Rv(c.B1, 0, 2048)
    Kf = Rv(c.B1, 2048, 2048)
    Kt = Rv(c.B1, 4096, 2048)
    Vt = Rv(c.B1, 6144, 2048)
    WO = Rv(c.B1, 8192, 8192)
    KVb = Rv(c.B2, 8192, 528)
    PMB = Rv(c.B2, 8832, 512)
    OC = Rv(c.xn, 2048, 2048)
    emit_wload(P, c, Rv(c.Wp1, 0, 8192), c.d_lb_qkvT, 1024, 1024, K + "wq", col0=0)
    emit_wload(P, c, Rv(c.Wp2, 0, 8192), c.d_lb_qkvT, 1024, 1024, K + "wk", col0=1024)
    emit_wload(P, c, Rv(c.Wp3, 0, 8192), c.d_lb_qkvT, 1024, 1024, K + "wv", col0=2048)
    emit_wload(P, c, WO, c.d_lb_woT, 1024, 1024, K + "wo")
    P.dma("sync", c.colA[:, 0:8], c.d_lb_qb[:, 0:8], writes=["colA"])
    P.dma("sync", c.colA[:, 8:16], c.d_lb_kb[:, 0:8], writes=["colA"])
    P.dma("sync", c.colB[:, 0:8], c.d_lb_ob[:, 0:8], writes=["colBb"])

    for cn in range(8):
        c0 = cn * CSZ
        emit_ln_bcast_w(P, c, c.row_r, c.row_mr, c0, CSZ, K + "ln", 5, 6)
        for ch in range(NCH):
            emit_xn_tile(P, c, X, ch, c0, CSZ, c.xn, ch * CSZ, 5, 6)
        for which, dstr, bcol0 in ((0, Qc, 0), (1, Kf, 8)):
            wsl = c.Wp1 if which == 0 else c.Wp2
            for fch in range(NCH):
                bank = fch % 2
                for ch in range(NCH):
                    wc = ch * 1024 + fch * 128
                    P.op("tensor", lambda e, o=c.ps[bank][:, :CSZ],
                         l=wsl[:, wc:wc + 128], r=c.xn[:, ch * CSZ: ch * CSZ + CSZ],
                         st=(ch == 0), sp=(ch == NCH - 1): e.matmul(
                             out=o, lhsT=l, rhs=r, start=st, stop=sp),
                         reads=[(wsl, wc, wc + 128), XK(c.xn, ch * CSZ, ch * CSZ + CSZ)],
                         writes=[f"ps{bank}"], group=K + f"p{cn}_{which}_{fch}")
                P.op("scalar", lambda e, o=dstr[:, fch * CSZ: fch * CSZ + CSZ],
                     i=c.ps[bank][:, :CSZ], b=c.colA[:, bcol0 + fch: bcol0 + fch + 1]:
                     e.activation(out=o, in_=i, func=AF.Relu, bias=b),
                     reads=[f"ps{bank}", "colA"],
                     writes=[(dstr, fch * CSZ, fch * CSZ + CSZ)])
                P.op("vector", lambda e, o=dstr[:, fch * CSZ: fch * CSZ + CSZ],
                     i=dstr[:, fch * CSZ: fch * CSZ + CSZ]: e.tensor_scalar_add(
                         out=o, in0=i, scalar1=1e-6),
                     reads=[(dstr, fch * CSZ, fch * CSZ + CSZ)],
                     writes=[(dstr, fch * CSZ, fch * CSZ + CSZ)])
        for which, dstr, brow, fn in ((0, Kt, c.kb_row, AF.Relu), (1, Vt, c.vb_row, AF.Identity)):
            wsl = c.Wp2 if which == 0 else c.Wp3
            for ti in range(2):
                for half in range(2):
                    bank = 2 + half
                    for ch in range(NCH):
                        P.op("tensor", lambda e, o=c.ps[bank][:, :512],
                             l=c.xn[:, ch * CSZ + ti * 128: ch * CSZ + ti * 128 + 128],
                             r=wsl[:, ch * 1024 + half * 512: ch * 1024 + half * 512 + 512],
                             st=(ch == 0): e.matmul(out=o, lhsT=l, rhs=r,
                                                    start=st, stop=False),
                             reads=[(wsl, ch * 1024 + half * 512, ch * 1024 + half * 512 + 512),
                                    XK(c.xn, ch * CSZ + ti * 128, ch * CSZ + ti * 128 + 128)],
                             writes=[f"ps{bank}"], group=K + f"t{cn}_{which}_{ti}_{half}")
                    P.op("tensor", lambda e, o=c.ps[bank][:, :512],
                         l=c.ONESB[brow.p0:brow.p0 + 1, 0:128],
                         r=brow[:, half * 512: half * 512 + 512]: e.matmul(
                             out=o, lhsT=l, rhs=r, start=False, stop=True),
                         reads=["kb_row", "vb_row", "ones1b"], writes=[f"ps{bank}"],
                         group=K + f"t{cn}_{which}_{ti}_{half}")
                    P.op("scalar", lambda e,
                         o=dstr[:, ti * 1024 + half * 512: ti * 1024 + half * 512 + 512],
                         i=c.ps[bank][:, :512], f=fn: e.activation(out=o, in_=i, func=f),
                         reads=[f"ps{bank}"],
                         writes=[(dstr, ti * 1024 + half * 512, ti * 1024 + half * 512 + 512)])
                    if which == 0:
                        P.op("vector", lambda e,
                             o=dstr[:, ti * 1024 + half * 512: ti * 1024 + half * 512 + 512],
                             i=dstr[:, ti * 1024 + half * 512: ti * 1024 + half * 512 + 512]:
                             e.tensor_scalar_add(out=o, in0=i, scalar1=1e-6),
                             reads=[(dstr, ti * 1024 + half * 512, ti * 1024 + half * 512 + 512)],
                             writes=[(dstr, ti * 1024 + half * 512, ti * 1024 + half * 512 + 512)])
        EPT = Rv(c.B2, 9344, 512)
        pend_v = [None]
        pend_p = [None]

        def head_lb(h):
            base = 64 * (h % 2)
            hc = h // 2
            nb = 2 + (h % 2)
            dn = 4 + (h % 2)
            for jt in range(2):
                P.op("tensor", lambda e, o=c.ps[jt][:, :CSZ],
                     l=Kf[base:base + 64, hc * 256 + jt * 128: hc * 256 + jt * 128 + 128],
                     r=Qc[base:base + 64, hc * 256: hc * 256 + CSZ]: e.matmul(
                         out=o, lhsT=l, rhs=r, start=True, stop=True),
                     reads=[(Kf, hc * 256 + jt * 128, hc * 256 + jt * 128 + 128),
                            (Qc, hc * 256, hc * 256 + CSZ)],
                     writes=[f"ps{jt}"], group=K + f"pr{cn}_{h}_{jt}")
                P.op("vector", lambda e, o=PMB[:, jt * 256: jt * 256 + 256],
                     a=c.ps[jt][:, :CSZ], b=c.causal[:, jt * 256: jt * 256 + 256]:
                     e.tensor_tensor(out=o, in0=a, in1=b, op=ALU.mult),
                     reads=[f"ps{jt}", (c.causal, jt * 256, jt * 256 + 256)],
                     writes=[(PMB, jt * 256, jt * 256 + 256)])
            if pend_v[0] is not None:
                pend_v[0]()
                pend_v[0] = None
            for jt in range(2):
                P.op("tensor", lambda e, o=c.ps[nb][0:64, :CSZ],
                     l=Vt[:, jt * 1024 + h * 64: jt * 1024 + h * 64 + 64],
                     r=PMB[:, jt * 256: jt * 256 + 256],
                     st=(jt == 0), sp=(jt == 1 and cn == 0): e.matmul(
                         out=o, lhsT=l, rhs=r, start=st, stop=sp),
                     reads=[(Vt, jt * 1024 + h * 64, jt * 1024 + h * 64 + 64),
                            (PMB, jt * 256, jt * 256 + 256)],
                     writes=[f"ps{nb}"], group=K + f"nm{cn}_{h}")
                P.op("tensor", lambda e, o=c.ps[dn][0:1, :CSZ], l=c.onesc[:],
                     r=PMB[:, jt * 256: jt * 256 + 256],
                     st=(jt == 0), sp=(jt == 1 and cn == 0): e.matmul(
                         out=o, lhsT=l, rhs=r, start=st, stop=sp),
                     reads=[(PMB, jt * 256, jt * 256 + 256), "onesc"],
                     writes=[f"ps{dn}"], group=K + f"dn{cn}_{h}")
            if cn > 0:
                P.op("tensor", lambda e, o=c.ps[nb][0:64, :CSZ],
                     l=KVb[base:base + 64, hc * 64: hc * 64 + 64],
                     r=Qc[base:base + 64, hc * 256: hc * 256 + CSZ]: e.matmul(
                         out=o, lhsT=l, rhs=r, start=False, stop=True),
                     reads=[(KVb, hc * 64, hc * 64 + 64), (Qc, hc * 256, hc * 256 + CSZ)],
                     writes=[f"ps{nb}"], group=K + f"nm{cn}_{h}")
                P.op("tensor", lambda e, o=c.ps[dn][0:1, :CSZ],
                     l=KVb[base:base + 64, 512 + hc: 512 + hc + 1],
                     r=Qc[base:base + 64, hc * 256: hc * 256 + CSZ]: e.matmul(
                         out=o, lhsT=l, rhs=r, start=False, stop=True),
                     reads=[(KVb, 512 + hc, 512 + hc + 1), (Qc, hc * 256, hc * 256 + CSZ)],
                     writes=[f"ps{dn}"], group=K + f"dn{cn}_{h}")
            if cn < 7:
                for jt in range(2):
                    P.op("tensor", lambda e, o=c.ps[6][base:base + 64, 0:64],
                         l=Kt[:, jt * 1024 + h * 64: jt * 1024 + h * 64 + 64],
                         r=Vt[:, jt * 1024 + h * 64: jt * 1024 + h * 64 + 64],
                         st=(jt == 0): e.matmul(
                             out=o, lhsT=l, rhs=r, start=st, stop=False),
                         reads=[(Kt, jt * 1024 + h * 64, jt * 1024 + h * 64 + 64),
                                (Vt, jt * 1024 + h * 64, jt * 1024 + h * 64 + 64)],
                         writes=["ps6"], group=K + f"kv{cn}_{h}")
                    P.op("tensor", lambda e, o=c.ps[6][base:base + 64, 64:65],
                         l=Kt[:, jt * 1024 + h * 64: jt * 1024 + h * 64 + 64],
                         r=c.onesc[:], sp=(jt == 1): e.matmul(
                             out=o, lhsT=l, rhs=r, start=False, stop=sp),
                         reads=[(Kt, jt * 1024 + h * 64, jt * 1024 + h * 64 + 64), "onesc"],
                         writes=["ps6"], group=K + f"kv{cn}_{h}")
                if cn == 0:
                    P.op("vector", lambda e, o=c.KVF[base:base + 64, hc * 64: hc * 64 + 64],
                         i=c.ps[6][base:base + 64, 0:64]: e.tensor_copy(out=o, in_=i),
                         reads=["ps6"], writes=["KVF"])
                    P.op("vector", lambda e, o=c.KVF[base:base + 64, 512 + hc: 512 + hc + 1],
                         i=c.ps[6][base:base + 64, 64:65]: e.tensor_copy(out=o, in_=i),
                         reads=["ps6"], writes=["KVF"])
                else:
                    P.op("vector", lambda e, o=c.KVF[base:base + 64, hc * 64: hc * 64 + 64],
                         a=c.KVF[base:base + 64, hc * 64: hc * 64 + 64],
                         b=c.ps[6][base:base + 64, 0:64]: e.tensor_tensor(
                             out=o, in0=a, in1=b, op=ALU.add),
                         reads=["ps6", "KVF"], writes=["KVF"])
                    P.op("vector", lambda e, o=c.KVF[base:base + 64, 512 + hc: 512 + hc + 1],
                         a=c.KVF[base:base + 64, 512 + hc: 512 + hc + 1],
                         b=c.ps[6][base:base + 64, 64:65]: e.tensor_tensor(
                             out=o, in0=a, in1=b, op=ALU.add),
                         reads=["ps6", "KVF"], writes=["KVF"])
                P.op("vector", lambda e, o=KVb[base:base + 64, hc * 64: hc * 64 + 64],
                     i=c.KVF[base:base + 64, hc * 64: hc * 64 + 64]: e.tensor_copy(
                         out=o, in_=i),
                     reads=["KVF"], writes=[(KVb, hc * 64, hc * 64 + 64)])
                P.op("vector", lambda e, o=KVb[base:base + 64, 512 + hc: 512 + hc + 1],
                     i=c.KVF[base:base + 64, 512 + hc: 512 + hc + 1]: e.tensor_copy(
                         out=o, in_=i),
                     reads=["KVF"], writes=[(KVb, 512 + hc, 512 + hc + 1)])

            if pend_p[0] is not None:
                pend_p[0]()
                pend_p[0] = None

            def epi_v(sl=h % 2, dn=dn):
                P.op("vector", lambda e, o=c.row_a[:, :CSZ], i=c.ps[dn][0:1, :CSZ]:
                     e.tensor_scalar_max(out=o, in0=i, scalar1=1e-6),
                     reads=[f"ps{dn}"], writes=["row_a"])
                P.op("vector", lambda e, o=c.den_bf[:, :CSZ], i=c.row_a[:, :CSZ]:
                     _recip_bf16(e, o, i), reads=["row_a"], writes=["row_a"])

            def epi_p(sl=h % 2, dn=dn, nb=nb, base=base, hc=hc):
                P.op("tensor", lambda e, o=c.ps[dn][0:64, :CSZ], l=c.ONESB[0:1, 0:64],
                     r=c.den_bf[:, :CSZ]: e.matmul(out=o, lhsT=l, rhs=r,
                                                   start=True, stop=True),
                     reads=["row_a", "ones1b"], writes=[f"ps{dn}"])
                P.op("scalar", lambda e, o=EPT[0:64, sl * 256: sl * 256 + 256],
                     i=c.ps[nb][0:64, :CSZ]: e.activation(out=o, in_=i, func=AF.Identity),
                     reads=[f"ps{nb}"], writes=[(EPT, sl * 256, sl * 256 + 256)])
                P.op("vector", lambda e,
                     o=OC[base:base + 64, hc * 256: hc * 256 + CSZ],
                     a=EPT[0:64, sl * 256: sl * 256 + 256], b=c.ps[dn][0:64, :CSZ]:
                     e.tensor_tensor(out=o, in0=a, in1=b, op=ALU.mult),
                     reads=[(EPT, sl * 256, sl * 256 + 256), f"ps{dn}"],
                     writes=[(OC, hc * 256, hc * 256 + CSZ)])
            pend_v[0] = epi_v
            pend_p[0] = epi_p

        for h in range(16):
            head_lb(h)
        pend_v[0]()
        pend_p[0]()
        pend_v[0] = None
        pend_p[0] = None

        for fch in range(NCH):
            bank = fch % 2
            for ch in range(NCH):
                wc = ch * 1024 + fch * 128
                P.op("tensor", lambda e, o=c.ps[bank][:, :CSZ],
                     l=WO[:, wc:wc + 128], r=OC[:, ch * 256: ch * 256 + CSZ],
                     st=(ch == 0), sp=(ch == NCH - 1): e.matmul(
                         out=o, lhsT=l, rhs=r, start=st, stop=sp),
                     reads=[(WO, wc, wc + 128), (OC, ch * 256, ch * 256 + CSZ)],
                     writes=[f"ps{bank}"], group=K + f"op{cn}_{fch}")
            xc = fch * S + c0
            P.op("vector", lambda e, o=X[:, xc:xc + CSZ], i=c.ps[bank][:, :CSZ],
                 s=c.colB[:, fch:fch + 1], x=X[:, xc:xc + CSZ]: e.scalar_tensor_tensor(
                     out=o, in0=i, scalar=s, in1=x, op0=ALU.add, op1=ALU.add),
                 reads=[f"ps{bank}", "colBb", XK(X, xc, xc + CSZ)],
                 writes=[XK(X, xc, xc + CSZ)])


SUBLAYERS = {
    "l0_attn_a": emit_l0_attn_a,
    "l0_attn_b": emit_l0_attn_b,
    "l1_attn_a": emit_l1_attn_a,
    "l1_attn_b": emit_l1_attn_b,
    "l1_ffn": emit_l1_ffn,
    "l0_ffn": None,
}

DRAM_SPECS = {
    "aa_wq": [1024, 1024], "aa_wk": [1024, 1024], "aa_wv": [1024, 1024],
    "aa_wd": [1024, 1024], "aa_bq": [128, 8], "aa_bk": [128, 8],
    "aa_bv": [128, 8], "aa_bd": [128, 8], "aa_bv_row": [1, 1024],
    "f_wqT": [1024, 1024], "f_wkcT": [1024, 256], "f_wvcT": [1024, 256],
    "f_woT": [1024, 1024], "f_wrT": [1024, 8], "f_ebias": [1, 8],
    "f_sw1T": [1024, 1024], "f_sw2T": [1024, 1024], "f_sw3T": [1024, 1024],
    "f_ew1T": [8, 1024, 1024], "f_ew2T": [8, 1024, 1024],
    "f_ew3T": [8, 1024, 1024], "f_g1": [128, 8], "f_g2": [128, 8],
    "ab_wq": [1024, 1024], "ab_wk": [1024, 256], "ab_wv": [1024, 256],
    "ab_wpost": [1024, 1024],
    "sa_waT": [1024, 256], "sa_bT": [1024, 256], "sa_cT": [256, 1024],
    "sa_dT": [1024, 1024], "sa_wab": [128, 2],
    "lb_qkvT": [1024, 3072], "lb_qb": [128, 8], "lb_kb": [128, 8],
    "lb_kb_row": [1, 1024], "lb_vb_row": [1, 1024], "lb_woT": [1024, 1024],
    "lb_ob": [128, 8],
    "lf_wrT": [1024, 1024], "lf_wiT": [1024, 1024], "lf_bias": [128, 8],
    "c_fr": [1024, 1024], "c_fineg": [1024, 1024], "c_gr": [1024, 1024],
    "c_gi": [1024, 1024], "c_gineg": [1024, 1024],
    "c_identb": [128, 128], "c_identf": [128, 128],
    "c_band": [128, 1792], "c_causal": [128, 2 * 256], "c_e8": [8, 1024],
}
DRAM_BF16 = {"c_identb", "c_band", "c_causal"}

STAGE_INPUTS = {
    "l0_attn_a": ["aa_wq", "aa_wk", "aa_wv", "aa_wd", "aa_bq", "aa_bk",
                  "aa_bv_row", "aa_bd", "c_band"],
    "l0_ffn": ["f_wqT", "f_wkcT", "f_wvcT", "f_woT", "f_wrT", "f_ebias",
               "f_sw1T", "f_sw2T", "f_sw3T", "f_ew1T", "f_ew2T", "f_ew3T",
               "f_g1", "f_g2", "c_identb", "c_e8"],
    "l0_attn_b": ["ab_wq", "ab_wk", "ab_wv", "ab_wpost"],
    "l1_attn_a": ["sa_waT", "sa_bT", "sa_cT", "sa_dT", "sa_wab"],
    "l1_ffn": ["lf_wrT", "lf_wiT", "lf_bias", "c_fr", "c_fineg", "c_gr",
               "c_gi", "c_gineg"],
    "l1_attn_b": ["lb_qkvT", "lb_qb", "lb_kb", "lb_kb_row", "lb_vb_row",
                  "lb_woT", "lb_ob", "c_causal"],
}


def stage_input_names(stages):
    names = []
    for st in stages:
        for nm in STAGE_INPUTS[st]:
            if nm not in names:
                names.append(nm)
    return names


def build_program(stages, nbatch=2):
    nc = bass.Bass("TRN2", target_bir_lowering=False)
    from contextlib import ExitStack
    stk = ExitStack()
    nc._stk = stk
    xin = nc.dram_tensor("xT", [nbatch, D, S], F32, kind="ExternalInput").ap()
    xout = nc.dram_tensor("yT", [nbatch, D, S], F32, kind="ExternalOutput").ap()
    dr = {}
    for nm in stage_input_names(stages):
        dt = BF16 if nm in DRAM_BF16 else F32
        dr[nm] = nc.dram_tensor(nm, DRAM_SPECS[nm], dt, kind="ExternalInput").ap()
    if "l1_ffn" in stages:
        dr["mscr"] = nc.dram_tensor("mscr", [128, 8192], BF16, kind="Internal").ap()
        if DEBUG_DUMP_M:
            dr["mdump"] = nc.dram_tensor("mdump", [128, 8192], BF16,
                                         kind="ExternalOutput").ap()

    with stk:
        c = alloc_ctx(nc, stk, set(stages))
        for nm, ap in dr.items():
            setattr(c, "d_" + nm, ap)
        P = Prog(nc)
        emit_const_init(P, c, stages)
        c.ffn_count = 0
        for b in range(nbatch):
            c.ffn_count = 0
            src = xin[b].rearrange("(c p) t -> p c t", p=128)
            P.dma("sync", c.X[:, :].rearrange("p (c t) -> p c t", c=NCH), src,
                  writes=[XK(c.X, 0, NCH * S)])
            for st in stages:
                if st == "l0_ffn":
                    emit_l0_ffn(P, c, c.X, c.ffn_count)
                    c.ffn_count += 1
                else:
                    SUBLAYERS[st](P, c, c.X)
            dst = xout[b].rearrange("(c p) t -> p c t", p=128)
            P.dma("sync", dst, c.X[:, :].rearrange("p (c t) -> p c t", c=NCH),
                  reads=[XK(c.X, 0, NCH * S)], writes=[f"yT{b}"])
        P.finish_dmas("sync")
        P.replay()
    return P, nc


def kernel(**inputs):
    w = prep_weights(inputs)
    x = np.asarray(inputs["L0_attn_a_x"])
    B = x.shape[0]
    bpc = B // NCORES
    stages = ["l0_attn_a", "l0_ffn", "l0_attn_b", "l0_ffn",
              "l1_attn_a", "l1_ffn", "l1_attn_b", "l1_ffn"]
    P, nc = build_program(stages, nbatch=bpc)
    in_maps = []
    for core in range(NCORES):
        m = {"xT": np.ascontiguousarray(
            x[core * bpc:(core + 1) * bpc].transpose(0, 2, 1))}
        for nm in stage_input_names(stages):
            m[nm] = w[nm]
        in_maps.append(m)
    res = run_bass_kernel_spmd(nc, in_maps, core_ids=list(range(NCORES)))
    out = np.empty_like(x)
    for core in range(NCORES):
        yT = np.asarray(res.results[core]["yT"])
        out[core * bpc:(core + 1) * bpc] = yT.transpose(0, 2, 1)
    return out
